# version 04 (final)
# baseline (speedup 1.0000x reference)
import numpy as np
from contextlib import ExitStack

import concourse.bass as bass
import concourse.mybir as mybir
from concourse.bass_utils import run_bass_kernel_spmd

F32 = mybir.dt.float32
BF16 = mybir.dt.bfloat16
I32 = mybir.dt.int32
AF = mybir.ActivationFunctionType
ALU = mybir.AluOpType
AX = mybir.AxisListType.X

RND = 12582912.0
NEG_HUGE = -3.4028234663852886e38
B, L, DM = 4, 4096, 1024
NC = 8
TPC = 2048


def chunks(n, c):
    return [(i, min(c, n - i)) for i in range(0, n, c)]


class Sem:
    def __init__(self, stack, nc, name):
        self.h = stack.enter_context(nc.semaphore(name))
        self.n = 0

    def inc(self, instr, k):
        instr.then_inc(self.h, k)
        self.n += k
        return instr


class AEng:

    def __init__(self, eng, sem):
        self._e = eng
        self._s = sem

    def __getattr__(self, name):
        fn = getattr(self._e, name)

        def call(*a, **kw):
            self._e.wait_ge(self._s.h, self._s.n)
            self._s.inc(fn(*a, **kw), 1)
            return self._s.n
        return call


class Prog:
    def __init__(self, name, mega=False):
        self.nc = bass.Bass("TRN2", target_bir_lowering=False, detect_race_conditions=False)
        self.stack = ExitStack()
        self.name = name
        self._sems = {}
        self.mega = mega
        self.prefix = ""
        self.feeds = {}
        self.ext_list = []
        self.phase_stack = None

    def begin_phase(self, prefix, feeds=None):
        self.prefix = prefix + "_"
        self.feeds = feeds or {}
        self.phase_stack = ExitStack()

    def end_phase(self):
        phase_barrier(self)
        self.phase_stack.close()
        self.phase_stack = None
        self.prefix = ""
        self.feeds = {}

    @property
    def V(self):
        return AEng(self.nc.vector, self.sem("cv"))

    @property
    def S(self):
        return AEng(self.nc.scalar, self.sem("cs"))

    @property
    def G(self):
        return AEng(self.nc.gpsimd, self.sem("cg"))

    def sem(self, name):
        if name not in self._sems:
            self._sems[name] = Sem(self.stack, self.nc, name)
        return self._sems[name]

    def sbuf(self, name, shape, dt):
        stk = self.phase_stack if self.phase_stack is not None else self.stack
        return stk.enter_context(self.nc.sbuf_tensor(self.prefix + name, shape, dt))

    def psum(self, name, shape, dt=F32):
        stk = self.phase_stack if self.phase_stack is not None else self.stack
        return stk.enter_context(self.nc.psum_tensor(self.prefix + name, shape, dt))

    def din(self, name, shape, dt=F32):
        if name in self.feeds:
            return self.feeds[name]
        full = self.prefix + name
        self.ext_list.append((full, name))
        return self.nc.dram_tensor(full, list(shape), dt, kind="ExternalInput").ap()

    def dout(self, name, shape, dt=F32):
        if name in self.feeds:
            return self.feeds[name]
        if self.mega:
            return self.nc.dram_tensor(self.prefix + name, list(shape), dt).ap()
        return self.nc.dram_tensor(name, list(shape), dt, kind="ExternalOutput").ap()

    def dint(self, name, shape, dt=F32):
        return self.nc.dram_tensor(self.prefix + name, list(shape), dt)


BARRIER_SEMS = ("ld", "lw", "cv", "cs", "ct", "st", "cc", "ts", "cg")


def phase_barrier(p):
    nc = p.nc
    engs = [nc.sync, nc.vector, nc.scalar, nc.tensor, nc.gpsimd]
    for s in BARRIER_SEMS:
        sem = p.sem(s)
        if sem.n == 0:
            continue
        for e in engs:
            e.wait_ge(sem.h, sem.n)


PAIRS = [[0, 1], [2, 3], [4, 5], [6, 7]]
ALL8 = [list(range(8))]


def emit_collective(p, kind, op, groups, src, dst):
    nc = p.nc
    cc = p.sem("cc")
    cc.inc(nc.gpsimd.collective_compute(kind, op, replica_groups=groups,
                                        ins=[src.ap()], outs=[dst.ap()]), 16)
    nc.gpsimd.wait_ge(cc.h, cc.n)


def emit_dram_copy(p, dst_ap, src_ap):
    st = p.sem("st")
    st.inc(p.nc.sync.dma_start(out=dst_ap, in_=src_ap, allow_slow_non_contiguous=True), 16)
    p.nc.sync.wait_ge(st.h, st.n)


def build_gather_rows(p, *, src, idx_name, dst, R, C, dst_c0=0, dt=F32):
    nc = p.nc
    idx = p.din(idx_name, (128, R // 128), I32)
    ld, cg, st = p.sem("ld"), p.sem("cg"), p.sem("st")
    idxt = p.sbuf("g_idx", [128, R // 128], I32)
    tile = p.sbuf("g_tile", [128, C], dt)
    ld.inc(nc.sync.dma_start(out=idxt[:], in_=idx), 16)
    nc.gpsimd.wait_ge(ld.h, ld.n)
    for ki in range(R // 128):
        nc.gpsimd.wait_ge(st.h, st.n)
        cg.inc(nc.gpsimd.indirect_dma_start(
            out=tile[:], out_offset=None, in_=src,
            in_offset=bass.IndirectOffsetOnAxis(ap=idxt[:, ki:ki + 1], axis=0)), 16)
        nc.sync.wait_ge(cg.h, cg.n)
        st.inc(nc.sync.dma_start(out=dst[ki * 128:(ki + 1) * 128, dst_c0:dst_c0 + C], in_=tile[:]), 16)
    nc.sync.wait_ge(st.h, st.n)


def build_zero_fill(p, dst_ap, rows, cols):
    nc = p.nc
    st = p.sem("st")
    cw = min(cols, 2048)
    z = p.sbuf("zf", [128, cw], F32)
    p.V.memset(z[:], 0.0)
    nc.sync.wait_ge(p.sem("cv").h, p.sem("cv").n)
    for k0, nk in chunks(rows, 128):
        for c0, ncn in chunks(cols, cw):
            st.inc(nc.sync.dma_start(out=dst_ap[k0:k0 + nk, c0:c0 + ncn], in_=z[0:nk, 0:ncn]), 16)
    nc.sync.wait_ge(st.h, st.n)


def build_masked_halo(p, *, agsrc, dst, rows, cols, mask_name, src_r0=0, src_c0=0):
    nc = p.nc
    maskd = p.din(mask_name, (128, 1))
    ld, st, cv = p.sem("ld"), p.sem("st"), p.sem("cv")
    mt = p.sbuf("mh_m", [128, 1], F32)
    tile = p.sbuf("mh_t", [128, cols], F32)
    ld.inc(nc.sync.dma_start(out=mt[:], in_=maskd), 16)
    for ki, (k0, nk) in enumerate(chunks(rows, 128)):
        nc.sync.wait_ge(cv.h, cv.n)
        ld.inc(nc.sync.dma_start(out=tile[0:nk, :], in_=agsrc[src_r0 + k0:src_r0 + k0 + nk, src_c0:src_c0 + cols],
                                 allow_slow_non_contiguous=True), 16)
        nc.vector.wait_ge(ld.h, ld.n)
        nc.vector.wait_ge(st.h, st.n)
        p.V.tensor_scalar(tile[0:nk, :], tile[0:nk, :], mt[0:nk, 0:1], None, ALU.mult)
        nc.sync.wait_ge(cv.h, cv.n)
        st.inc(nc.sync.dma_start(out=dst[k0:k0 + nk, 0:cols], in_=tile[0:nk, :],
                                 allow_slow_non_contiguous=True), 16)
    nc.sync.wait_ge(st.h, st.n)


_BUILT = {}


def run_launch(key, builder, in_maps, **kw):
    if key not in _BUILT:
        p = Prog(key)
        with p.stack:
            with p.nc.Block():
                builder(p, **kw)
            p.nc.finalize()
            _BUILT[key] = p.nc
    import os, time
    t0 = time.time()
    res = run_bass_kernel_spmd(_BUILT[key], in_maps, core_ids=list(range(NC)))
    if os.environ.get("KDEV"):
        print(f"[launch {key}: {time.time()-t0:.1f}s]", flush=True)
    return res.results


def epi_silu(p, ot, nm, tt, cv, cs):
    nc = p.nc
    nc.scalar.wait_ge(cv.h, cv.n)
    p.S.activation(ot[0:nm, 0:tt], ot[0:nm, 0:tt], AF.Silu)
    return cs


def epi_dt_a(p, ot, nm, tt, cv, cs, scratch):
    nc = p.nc
    s1, s2 = scratch
    nc.scalar.wait_ge(cv.h, cv.n)
    p.S.activation(s1[0:nm, 0:tt], ot[0:nm, 0:tt], AF.Abs)
    nc.scalar.wait_ge(cs.h, cs.n)
    p.S.activation(s1[0:nm, 0:tt], s1[0:nm, 0:tt], AF.Exp, scale=-1.0)
    nc.scalar.wait_ge(cs.h, cs.n)
    p.S.activation(s1[0:nm, 0:tt], s1[0:nm, 0:tt], AF.Ln, bias=1.0)
    nc.vector.wait_ge(cs.h, cs.n)
    p.V.tensor_scalar(s2[0:nm, 0:tt], ot[0:nm, 0:tt], 0.0, None, ALU.max)
    p.V.tensor_add(ot[0:nm, 0:tt], s1[0:nm, 0:tt], s2[0:nm, 0:tt])
    p.V.tensor_scalar(ot[0:nm, 0:tt], ot[0:nm, 0:tt], 1.0, 1e-4, ALU.min, ALU.max)
    p.V.tensor_scalar(ot[0:nm, 0:tt], ot[0:nm, 0:tt], -1.0, 1.0, ALU.mult, ALU.add)
    return cv


def build_mm(p, *, K, M, T, TT=512, rhs_bf16=False, lhsT_bf16=False,
             out_f32=True, out_bf16=False, nres=0, bias=False, epilogue=None):
    nc = p.nc
    kcs, mcs, tts = chunks(K, 128), chunks(M, 128), chunks(T, TT)
    KCn = len(kcs)
    lhsT = p.din("lhsT", (K, M), BF16 if lhsT_bf16 else F32)
    rhs = p.din("rhs", (K, T), BF16 if rhs_bf16 else F32)
    biasd = p.din("bias", (M, 1)) if bias else None
    resd = [p.din(f"res{j}", (M, T)) for j in range(nres)]
    outf = p.dout("out_f", (M, T)) if out_f32 else None
    outb = p.dout("out_b", (M, T), BF16) if out_bf16 else None

    ld, lw, cv, cs, ct, st = (p.sem(s) for s in ("ld", "lw", "cv", "cs", "ct", "st"))

    rhsb = p.sbuf("rhsb", [128, KCn * T], BF16)
    rhsf = None if rhs_bf16 else p.sbuf("rhsf", [128, T], F32)
    wtb = [p.sbuf(f"wtb{i}", [128, KCn * 128], BF16) for i in range(2)]
    wtf = None if lhsT_bf16 else [p.sbuf(f"wtf{i}", [128, KCn * 128], F32) for i in range(2)]
    if wtf is not None:
        p.S.memzero(wtf[0][:, :])
        p.S.memzero(wtf[1][:, :])
    ot = [p.sbuf(f"ot{i}", [128, TT], F32) for i in range(2)]
    obt = [p.sbuf(f"obt{i}", [128, TT], BF16) for i in range(2)] if out_bf16 else None
    rtl = [p.sbuf(f"rt{j}", [128, TT], F32) for j in range(nres)]
    btile = p.sbuf("btile", [128, 1], F32) if bias else None
    scr = (p.sbuf("scr1", [128, TT], F32), p.sbuf("scr2", [128, TT], F32))
    ps = [p.psum(f"ps{i}", [128, TT]) for i in range(2)]

    for ki, (k0, nk) in enumerate(kcs):
        if rhs_bf16:
            ld.inc(nc.sync.dma_start(out=rhsb[0:nk, ki * T:ki * T + T], in_=rhs[k0:k0 + nk, :]), 16)
        else:
            nc.sync.wait_ge(cv.h, cv.n)
            ld.inc(nc.sync.dma_start(out=rhsf[0:nk, :], in_=rhs[k0:k0 + nk, :]), 16)
            nc.vector.wait_ge(ld.h, ld.n)
            p.V.tensor_copy(rhsb[0:nk, ki * T:ki * T + T], rhsf[0:nk, :])
    rhs_v, rhs_d = cv.n, ld.n

    wt_last_use = [0, 0]
    ps_free = [0, 0]
    q = 0
    for mi, (m0, nm) in enumerate(mcs):
        buf = mi % 2
        nc.scalar.wait_ge(ct.h, wt_last_use[buf])
        for ki, (k0, nk) in enumerate(kcs):
            lw.inc(nc.scalar.dma_start(out=(wtb if lhsT_bf16 else wtf)[buf][0:nk, ki * 128:ki * 128 + nm],
                                       in_=lhsT[k0:k0 + nk, m0:m0 + nm]), 16)
        if not lhsT_bf16:
            nc.scalar.wait_ge(lw.h, lw.n)
            p.S.copy(wtb[buf][:, :], wtf[buf][:, :])
        w_lw, w_cs = lw.n, cs.n
        if bias:
            nc.sync.wait_ge(cv.h, cv.n)
            lw.inc(nc.sync.dma_start(out=btile[0:nm, :], in_=biasd[m0:m0 + nm, :]), 16)
            w_lw = lw.n

        for ti, (t0, tt) in enumerate(tts):
            pb = q % 2
            ob = q % 2
            for j in range(nres):
                nc.sync.wait_ge(cv.h, cv.n)
                ld.inc(nc.sync.dma_start(out=rtl[j][0:nm, 0:tt], in_=resd[j][m0:m0 + nm, t0:t0 + tt]), 16)
            res_d = ld.n
            nc.tensor.wait_ge(lw.h, w_lw)
            nc.tensor.wait_ge(cs.h, w_cs)
            nc.tensor.wait_ge(ld.h, rhs_d)
            nc.tensor.wait_ge(cv.h, max(ps_free[pb], rhs_v))
            ins = None
            for ki, (k0, nk) in enumerate(kcs):
                ins = nc.tensor.matmul(ps[pb][0:nm, 0:tt],
                                       wtb[buf][0:nk, ki * 128:ki * 128 + nm],
                                       rhsb[0:nk, ki * T + t0:ki * T + t0 + tt],
                                       start=(ki == 0), stop=(ki == KCn - 1))
            ct.inc(ins, 1)
            wt_last_use[buf] = ct.n
            mm_mark = ct.n

            nc.vector.wait_ge(ct.h, mm_mark)
            nc.vector.wait_ge(st.h, st.n)
            if bias:
                nc.vector.wait_ge(lw.h, w_lw)
                p.V.tensor_scalar(ot[ob][0:nm, 0:tt], ps[pb][0:nm, 0:tt],
                                               btile[0:nm, 0:1], None, ALU.add)
            else:
                p.V.tensor_copy(ot[ob][0:nm, 0:tt], ps[pb][0:nm, 0:tt])
            ps_free[pb] = cv.n
            for j in range(nres):
                nc.vector.wait_ge(ld.h, res_d)
                p.V.tensor_add(ot[ob][0:nm, 0:tt], ot[ob][0:nm, 0:tt], rtl[j][0:nm, 0:tt])
            prod = cv
            if epilogue is not None:
                prod = epi_dt_a(p, ot[ob], nm, tt, cv, cs, scr) if epilogue == "dt_a" else epilogue(p, ot[ob], nm, tt, cv, cs)
            prod_mark = prod.n

            nc.sync.wait_ge(prod.h, prod_mark)
            if out_f32:
                st.inc(nc.sync.dma_start(out=outf[m0:m0 + nm, t0:t0 + tt], in_=ot[ob][0:nm, 0:tt]), 16)
            if out_bf16:
                nc.scalar.wait_ge(prod.h, prod_mark)
                nc.scalar.wait_ge(st.h, st.n)
                p.S.copy(obt[ob][0:nm, 0:tt], ot[ob][0:nm, 0:tt])
                nc.sync.wait_ge(cs.h, cs.n)
                st.inc(nc.sync.dma_start(out=outb[m0:m0 + nm, t0:t0 + tt], in_=obt[ob][0:nm, 0:tt]), 16)
            q += 1
    nc.sync.wait_ge(st.h, st.n)


def build_ln(p, *, T, want_pmax=False, DMdim=DM):
    nc = p.nc
    TT = 512
    kcs = chunks(DMdim, 128)
    x = p.din("x", (DMdim, T))
    onesd = p.din("ones128", (128, 128))
    xnf = p.dout("xn_f", (DMdim, T))
    xnb = p.dout("xn_b", (DMdim, T), BF16)
    pmax = p.dout("pmax", (1, 8)) if want_pmax else None
    pmax_bounce = p.dint("pmax_bounce", (128, 1)) if want_pmax else None

    ld, cv, cs, ct, st = (p.sem(s) for s in ("ld", "cv", "cs", "ct", "st"))
    xt = p.sbuf("xt", [128, len(kcs) * TT], F32)
    sq = [p.sbuf(f"sq{i}", [128, TT], F32) for i in range(2)]
    ones = p.sbuf("ones", [128, 128], F32)
    statm = p.sbuf("statm", [1, TT], F32)
    statr = p.sbuf("statr", [1, TT], F32)
    tvar = p.sbuf("tvar", [1, TT], F32)
    tmsq = p.sbuf("tmsq", [1, TT], F32)
    xo = p.sbuf("xo", [128, TT], F32)
    xob = p.sbuf("xob", [128, TT], BF16)
    mx = p.sbuf("mx", [128, 1], F32)
    mx2 = p.sbuf("mx2", [128, 1], F32)
    mxr = p.sbuf("mxr", [1, 128], F32)
    out8 = p.sbuf("out8", [1, 8], F32)
    ps_s = p.psum("ps_s", [1, TT])
    ps_q = p.psum("ps_q", [1, TT])
    ps_b = p.psum("ps_b", [128, 2 * TT])

    ld.inc(nc.sync.dma_start(out=ones[:], in_=onesd), 16)
    first_max = True
    for ti, (t0, tt) in enumerate(chunks(T, TT)):
        nc.sync.wait_ge(cv.h, cv.n)
        nc.sync.wait_ge(cs.h, cs.n)
        nc.sync.wait_ge(ct.h, ct.n)
        for ki, (k0, nk) in enumerate(kcs):
            ld.inc(nc.sync.dma_start(out=xt[0:nk, ki * TT:ki * TT + tt], in_=x[k0:k0 + nk, t0:t0 + tt]), 16)
        tile_ld = ld.n
        nc.tensor.wait_ge(ld.h, tile_ld)
        ins = None
        for ki, (k0, nk) in enumerate(kcs):
            ins = nc.tensor.matmul(ps_s[0:1, 0:tt], ones[0:nk, 0:1], xt[0:nk, ki * TT:ki * TT + tt],
                                   start=(ki == 0), stop=(ki == len(kcs) - 1))
        ct.inc(ins, 1)
        mm_sq = []
        for ki, (k0, nk) in enumerate(kcs):
            sb = sq[ki % 2]
            nc.scalar.wait_ge(ld.h, tile_ld)
            if ki >= 2:
                nc.scalar.wait_ge(ct.h, mm_sq[ki - 2])
            p.S.activation(sb[0:nk, 0:tt], xt[0:nk, ki * TT:ki * TT + tt], AF.Square)
            nc.tensor.wait_ge(cs.h, cs.n)
            ins = nc.tensor.matmul(ps_q[0:1, 0:tt], ones[0:nk, 0:1], sb[0:nk, 0:tt],
                                   start=(ki == 0), stop=(ki == len(kcs) - 1))
            ct.inc(ins, 1)
            mm_sq.append(ct.n)
        nc.vector.wait_ge(ct.h, ct.n)
        p.V.tensor_scalar(statm[0:1, 0:tt], ps_s[0:1, 0:tt], 1.0 / DMdim, None, ALU.mult)
        p.V.tensor_scalar(tmsq[0:1, 0:tt], ps_q[0:1, 0:tt], 1.0 / DMdim, None, ALU.mult)
        p.V.tensor_tensor(tvar[0:1, 0:tt], statm[0:1, 0:tt], statm[0:1, 0:tt], ALU.mult)
        p.V.tensor_tensor(tvar[0:1, 0:tt], tmsq[0:1, 0:tt], tvar[0:1, 0:tt], ALU.subtract)
        p.V.tensor_scalar(tvar[0:1, 0:tt], tvar[0:1, 0:tt], 1e-5, None, ALU.add)
        nc.scalar.wait_ge(cv.h, cv.n)
        p.S.activation(tvar[0:1, 0:tt], tvar[0:1, 0:tt], AF.Ln)
        nc.scalar.wait_ge(cs.h, cs.n)
        p.S.activation(statr[0:1, 0:tt], tvar[0:1, 0:tt], AF.Exp, scale=-0.5)
        nc.tensor.wait_ge(cs.h, cs.n)
        nc.tensor.wait_ge(cv.h, cv.n)
        nc.tensor.matmul(ps_b[:, 0:tt], ones[0:1, :], statm[0:1, 0:tt], start=True, stop=True)
        ins2 = nc.tensor.matmul(ps_b[:, TT:TT + tt], ones[0:1, :], statr[0:1, 0:tt], start=True, stop=True)
        ct.inc(ins2, 1)
        bcast_mark = ct.n
        for ki, (k0, nk) in enumerate(kcs):
            nc.vector.wait_ge(ct.h, bcast_mark)
            nc.vector.wait_ge(st.h, st.n)
            p.V.tensor_tensor(xo[0:nk, 0:tt], xt[0:nk, ki * TT:ki * TT + tt], ps_b[0:nk, 0:tt], ALU.subtract)
            p.V.tensor_tensor(xo[0:nk, 0:tt], xo[0:nk, 0:tt], ps_b[0:nk, TT:TT + tt], ALU.mult)
            if want_pmax:
                if first_max:
                    p.V.reduce_max(mx[0:nk, 0:1], xo[0:nk, 0:tt], axis=AX, apply_absolute_value=True)
                    first_max = False
                else:
                    p.V.reduce_max(mx2[0:nk, 0:1], xo[0:nk, 0:tt], axis=AX, apply_absolute_value=True)
                    p.V.tensor_tensor(mx[0:nk, 0:1], mx[0:nk, 0:1], mx2[0:nk, 0:1], ALU.max)
            nc.scalar.wait_ge(cv.h, cv.n)
            nc.scalar.wait_ge(st.h, st.n)
            p.S.copy(xob[0:nk, 0:tt], xo[0:nk, 0:tt])
            nc.sync.wait_ge(cv.h, cv.n)
            nc.sync.wait_ge(cs.h, cs.n)
            st.inc(nc.sync.dma_start(out=xnf[k0:k0 + nk, t0:t0 + tt], in_=xo[0:nk, 0:tt]), 16)
            st.inc(nc.sync.dma_start(out=xnb[k0:k0 + nk, t0:t0 + tt], in_=xob[0:nk, 0:tt]), 16)
    if want_pmax:
        nc.sync.wait_ge(cv.h, cv.n)
        st.inc(nc.sync.dma_start(out=pmax_bounce.ap(), in_=mx[:]), 16)
        nc.sync.wait_ge(st.h, st.n)
        ld.inc(nc.sync.dma_start(out=mxr[0:1, :], in_=pmax_bounce.ap().rearrange("p x -> x p")), 16)
        nc.vector.wait_ge(ld.h, ld.n)
        p.V.memset(out8[0:1, :], 0.0)
        p.V.reduce_max(out8[0:1, 0:1], mxr[0:1, :], axis=AX)
        nc.sync.wait_ge(cv.h, cv.n)
        st.inc(nc.sync.dma_start(out=pmax, in_=out8[0:1, :]), 16)
    nc.sync.wait_ge(st.h, st.n)


def build_conv0(p, *, D=2048, T=TPC, TH=2560, HALO=3, off=0):
    nc = p.nc
    TT = 512
    xc = p.din("xc", (D, TH))
    w = p.din("cw", (D, 4))
    uf = p.dout("u_f", (D, T))
    ub = p.dout("u_b", (D, T), BF16)
    ld, cv, cs, st = (p.sem(s) for s in ("ld", "cv", "cs", "st"))
    xt = p.sbuf("xt", [128, TT + HALO], F32)
    wt = p.sbuf("wt", [128, 4], F32)
    acc = [p.sbuf(f"acc{i}", [128, TT], F32) for i in range(2)]
    ob = p.sbuf("ob", [128, TT], BF16)
    for ki, (k0, nk) in enumerate(chunks(D, 128)):
        nc.sync.wait_ge(cv.h, cv.n)
        ld.inc(nc.sync.dma_start(out=wt[0:nk, :], in_=w[k0:k0 + nk, :]), 16)
        for ti, (t0, tt) in enumerate(chunks(T, TT)):
            nc.sync.wait_ge(st.h, st.n)
            nc.sync.wait_ge(cv.h, cv.n)
            nc.sync.wait_ge(cs.h, p.sem("cs").n)
            ld.inc(nc.sync.dma_start(out=xt[0:nk, 0:tt + HALO], in_=xc[k0:k0 + nk, t0 + off:t0 + off + tt + HALO]), 16)
            nc.vector.wait_ge(ld.h, ld.n)
            p.V.tensor_scalar(acc[0][0:nk, 0:tt], xt[0:nk, 0:tt], wt[0:nk, 0:1], None, ALU.mult)
            cur = 0
            for j in range(1, 4):
                p.V.scalar_tensor_tensor(acc[1 - cur][0:nk, 0:tt], xt[0:nk, j:j + tt],
                                                      wt[0:nk, j:j + 1], acc[cur][0:nk, 0:tt],
                                                      ALU.mult, ALU.add)
                cur = 1 - cur
            nc.scalar.wait_ge(cv.h, cv.n)
            p.S.activation(acc[cur][0:nk, 0:tt], acc[cur][0:nk, 0:tt], AF.Silu)
            nc.scalar.wait_ge(cs.h, cs.n)
            p.S.copy(ob[0:nk, 0:tt], acc[cur][0:nk, 0:tt])
            nc.sync.wait_ge(cs.h, cs.n)
            st.inc(nc.sync.dma_start(out=uf[k0:k0 + nk, t0:t0 + tt], in_=acc[cur][0:nk, 0:tt]), 16)
            st.inc(nc.sync.dma_start(out=ub[k0:k0 + nk, t0:t0 + tt], in_=ob[0:nk, 0:tt]), 16)
    nc.sync.wait_ge(st.h, st.n)


def build_conv1(p, *, D=2048, T=TPC, TH=2560, HALO=3, off=0):
    nc = p.nc
    TT = 512
    xc = p.din("xc", (D, TH))
    w = p.din("cw", (D, 4))
    cbd = p.din("cb", (D, 1))
    uf = p.dout("u_f", (D, T))
    ub = p.dout("u_b", (D, T), BF16)
    ld, cv, cs, st = (p.sem(s) for s in ("ld", "cv", "cs", "st"))
    xt = p.sbuf("xt", [128, TT + HALO], F32)
    wt = p.sbuf("wt", [128, 4], F32)
    cbt = p.sbuf("cbt", [128, 1], F32)
    acc = [p.sbuf(f"acc{i}", [128, TT], F32) for i in range(2)]
    ob = p.sbuf("ob", [128, TT], BF16)
    for ki, (k0, nk) in enumerate(chunks(D, 128)):
        nc.sync.wait_ge(cv.h, cv.n)
        ld.inc(nc.sync.dma_start(out=wt[0:nk, :], in_=w[k0:k0 + nk, :]), 16)
        ld.inc(nc.sync.dma_start(out=cbt[0:nk, :], in_=cbd[k0:k0 + nk, :]), 16)
        for ti, (t0, tt) in enumerate(chunks(T, TT)):
            nc.sync.wait_ge(st.h, st.n)
            nc.sync.wait_ge(cv.h, cv.n)
            nc.sync.wait_ge(cs.h, p.sem("cs").n)
            ld.inc(nc.sync.dma_start(out=xt[0:nk, 0:tt + HALO], in_=xc[k0:k0 + nk, t0 + off:t0 + off + tt + HALO]), 16)
            nc.vector.wait_ge(ld.h, ld.n)
            p.V.tensor_scalar(acc[0][0:nk, 0:tt], xt[0:nk, 0:tt], wt[0:nk, 0:1], None, ALU.mult)
            cur = 0
            for j in range(1, 4):
                p.V.scalar_tensor_tensor(acc[1 - cur][0:nk, 0:tt], xt[0:nk, j:j + tt],
                                         wt[0:nk, j:j + 1], acc[cur][0:nk, 0:tt], ALU.mult, ALU.add)
                cur = 1 - cur
            p.V.tensor_scalar(acc[cur][0:nk, 0:tt], acc[cur][0:nk, 0:tt], cbt[0:nk, 0:1], None, ALU.add)
            nc.scalar.wait_ge(cv.h, cv.n)
            p.S.activation(acc[cur][0:nk, 0:tt], acc[cur][0:nk, 0:tt], AF.Silu)
            p.S.copy(ob[0:nk, 0:tt], acc[cur][0:nk, 0:tt])
            nc.sync.wait_ge(cs.h, p.sem("cs").n)
            st.inc(nc.sync.dma_start(out=uf[k0:k0 + nk, t0:t0 + tt], in_=acc[cur][0:nk, 0:tt]), 16)
            st.inc(nc.sync.dma_start(out=ub[k0:k0 + nk, t0:t0 + tt], in_=ob[0:nk, 0:tt]), 16)
    nc.sync.wait_ge(st.h, st.n)


def build_scan0(p, *, R=1024, T=L, ag=False):
    nc = p.nc
    DC = R // 128
    if ag:
        a = p.din("ag_a", (2 * 2048, TPC))
        ubi = p.din("ag_ub", (2 * 2048, TPC))
        idxd = p.din("sc0_idx", (128, 2 * DC), I32)
        ys = p.dout("ys", (2 * R, T // 2))
        ysv = ys.rearrange("(d h) t -> d (h t)", h=2)
    else:
        a = p.din("a", (R, T))
        ubi = p.din("ub", (R, T))
        ys = p.dout("ys", (R, T))
        ysv = ys
    ld, cv, st, cg = (p.sem(s) for s in ("ld", "cv", "st", "cg"))
    at = p.sbuf("at", [128, T], F32)
    bt = p.sbuf("bt", [128, T], F32)
    yt = p.sbuf("yt", [128, T], F32)
    idxt = p.sbuf("idxt", [128, 2 * DC], I32) if ag else None
    if ag:
        ld.inc(nc.sync.dma_start(out=idxt[:], in_=idxd), 16)
        nc.gpsimd.wait_ge(ld.h, ld.n)
    for ki, (k0, nk) in enumerate(chunks(R, 128)):
        nc.sync.wait_ge(cv.h, cv.n)
        nc.sync.wait_ge(st.h, st.n)
        if ag:
            nc.gpsimd.wait_ge(cv.h, cv.n)
            nc.gpsimd.wait_ge(st.h, st.n)
            for half in range(2):
                cg.inc(nc.gpsimd.indirect_dma_start(
                    out=at[:, half * TPC:(half + 1) * TPC], out_offset=None, in_=a,
                    in_offset=bass.IndirectOffsetOnAxis(ap=idxt[:, half * DC + ki:half * DC + ki + 1], axis=0)), 16)
                cg.inc(nc.gpsimd.indirect_dma_start(
                    out=bt[:, half * TPC:(half + 1) * TPC], out_offset=None, in_=ubi,
                    in_offset=bass.IndirectOffsetOnAxis(ap=idxt[:, half * DC + ki:half * DC + ki + 1], axis=0)), 16)
            nc.vector.wait_ge(cg.h, cg.n)
        else:
            ld.inc(nc.sync.dma_start(out=at[0:nk, :], in_=a[k0:k0 + nk, :]), 16)
            ld.inc(nc.sync.dma_start(out=bt[0:nk, :], in_=ubi[k0:k0 + nk, :]), 16)
            nc.vector.wait_ge(ld.h, ld.n)
        p.V.tensor_tensor_scan(yt[0:nk, :], at[0:nk, :], bt[0:nk, :], 0.0, ALU.mult, ALU.add)
        nc.sync.wait_ge(cv.h, cv.n)
        st.inc(nc.sync.dma_start(out=ysv[k0:k0 + nk, :], in_=yt[0:nk, :]), 16)
    nc.sync.wait_ge(st.h, st.n)


def build_mul(p, *, R, T):
    nc = p.nc
    TT = 512
    a = p.din("a", (R, T))
    b = p.din("b", (R, T))
    o = p.dout("o", (R, T))
    ld, cv, st = (p.sem(s) for s in ("ld", "cv", "st"))
    ta = p.sbuf("ta", [128, TT], F32)
    tb = p.sbuf("tb", [128, TT], F32)
    to = p.sbuf("to", [128, TT], F32)
    for ki, (k0, nk) in enumerate(chunks(R, 128)):
        for ti, (t0, tt) in enumerate(chunks(T, TT)):
            nc.sync.wait_ge(cv.h, cv.n)
            nc.sync.wait_ge(st.h, st.n)
            ld.inc(nc.sync.dma_start(out=ta[0:nk, 0:tt], in_=a[k0:k0 + nk, t0:t0 + tt]), 16)
            ld.inc(nc.sync.dma_start(out=tb[0:nk, 0:tt], in_=b[k0:k0 + nk, t0:t0 + tt]), 16)
            nc.vector.wait_ge(ld.h, ld.n)
            p.V.tensor_tensor(to[0:nk, 0:tt], ta[0:nk, 0:tt], tb[0:nk, 0:tt], ALU.mult)
            nc.sync.wait_ge(cv.h, cv.n)
            st.inc(nc.sync.dma_start(out=o[k0:k0 + nk, t0:t0 + tt], in_=to[0:nk, 0:tt]), 16)
    nc.sync.wait_ge(st.h, st.n)


def build_l0post(p, *, R=2048, T=TPC):
    nc = p.nc
    TT = 512
    ysd = p.din("ys", (R, T))
    ud = p.din("u", (R, T))
    szd = p.din("sz", (R, T))
    Dd = p.din("Dp", (R, 1))
    ob = p.dout("y_b", (R, T), BF16)
    ld, cv, cs, st = (p.sem(s) for s in ("ld", "cv", "cs", "st"))
    t1 = p.sbuf("t1", [128, TT], F32)
    t2 = p.sbuf("t2", [128, TT], F32)
    t3 = p.sbuf("t3", [128, TT], F32)
    dt_ = p.sbuf("dt", [128, 1], F32)
    tb = p.sbuf("tb", [128, TT], BF16)
    for ki, (k0, nk) in enumerate(chunks(R, 128)):
        nc.sync.wait_ge(cv.h, cv.n)
        ld.inc(nc.sync.dma_start(out=dt_[0:nk, :], in_=Dd[k0:k0 + nk, :]), 16)
        for ti, (t0, tt) in enumerate(chunks(T, TT)):
            nc.sync.wait_ge(cv.h, cv.n)
            nc.sync.wait_ge(st.h, st.n)
            ld.inc(nc.sync.dma_start(out=t1[0:nk, 0:tt], in_=ysd[k0:k0 + nk, t0:t0 + tt]), 16)
            ld.inc(nc.sync.dma_start(out=t2[0:nk, 0:tt], in_=ud[k0:k0 + nk, t0:t0 + tt]), 16)
            ld.inc(nc.sync.dma_start(out=t3[0:nk, 0:tt], in_=szd[k0:k0 + nk, t0:t0 + tt]), 16)
            nc.vector.wait_ge(ld.h, ld.n)
            p.V.scalar_tensor_tensor(t2[0:nk, 0:tt], t2[0:nk, 0:tt], dt_[0:nk, 0:1],
                                                  t1[0:nk, 0:tt], ALU.mult, ALU.add)
            p.V.tensor_tensor(t2[0:nk, 0:tt], t2[0:nk, 0:tt], t3[0:nk, 0:tt], ALU.mult)
            nc.scalar.wait_ge(cv.h, cv.n)
            p.S.copy(tb[0:nk, 0:tt], t2[0:nk, 0:tt])
            nc.sync.wait_ge(cs.h, cs.n)
            st.inc(nc.sync.dma_start(out=ob[k0:k0 + nk, t0:t0 + tt], in_=tb[0:nk, 0:tt]), 16)
    nc.sync.wait_ge(st.h, st.n)


def build_dequant(p, *, O, I):
    nc = p.nc
    NB = I // 32
    w = p.din("w", (O, I))
    ident = p.din("ident", (128, 128))
    wt_ = p.dout("w_dq_T", (I, O), BF16)
    ld, cv, cs, ct, st = (p.sem(s) for s in ("ld", "cv", "cs", "ct", "st"))
    wtile = p.sbuf("wtile", [128, I], F32)
    bm = p.sbuf("bm", [128, NB], F32)
    rs = p.sbuf("rs", [128, NB], F32)
    wq = p.sbuf("wq", [128, I], BF16)
    idb = p.sbuf("idb", [128, 128], BF16)
    idf = p.sbuf("idf", [128, 128], F32)
    trb = p.sbuf("trb", [128, 128], BF16)
    pst = p.psum("pst", [128, 128], BF16)
    ld.inc(nc.sync.dma_start(out=idf[:], in_=ident), 16)
    nc.vector.wait_ge(ld.h, ld.n)
    p.V.tensor_copy(idb[:], idf[:])
    for oi, (o0, no) in enumerate(chunks(O, 128)):
        nc.sync.wait_ge(cv.h, cv.n)
        ld.inc(nc.sync.dma_start(out=wtile[0:no, :], in_=w[o0:o0 + no, :]), 16)
        nc.vector.wait_ge(ct.h, ct.n)
        nc.vector.wait_ge(ld.h, ld.n)
        p.V.reduce_max(bm.ap().unsqueeze(2)[0:no], wtile.ap().rearrange("p (a b) -> p a b", b=32)[0:no],
                                    axis=AX, apply_absolute_value=True)
        nc.vector.wait_ge(cv.h, cv.n)
        p.V.tensor_scalar(bm[0:no, :], bm[0:no, :], 1.0 / 7.0, 1e-12, ALU.mult, ALU.max)
        nc.vector.wait_ge(cv.h, cv.n)
        p.V.reciprocal(rs[0:no, :], bm[0:no, :])
        nc.vector.wait_ge(cv.h, cv.n)
        wv = wtile.ap().rearrange("p (a b) -> p a b", b=32)
        p.V.tensor_tensor(wv[0:no], wv[0:no], rs.ap().unsqueeze(2).broadcast_to([128, NB, 32])[0:no], ALU.mult)
        nc.vector.wait_ge(cv.h, cv.n)
        p.V.tensor_scalar(wtile[0:no, :], wtile[0:no, :], RND, RND, ALU.add, ALU.subtract)
        nc.vector.wait_ge(cv.h, cv.n)
        p.V.tensor_scalar(wtile[0:no, :], wtile[0:no, :], 7.0, -8.0, ALU.min, ALU.max)
        nc.vector.wait_ge(cv.h, cv.n)
        p.V.tensor_tensor(wq.ap().rearrange("p (a b) -> p a b", b=32)[0:no], wv[0:no],
                                       bm.ap().unsqueeze(2).broadcast_to([128, NB, 32])[0:no], ALU.mult)
        q_mark = cv.n
        for ii, (i0, ni) in enumerate(chunks(I, 128)):
            nc.tensor.wait_ge(cv.h, cv.n)
            ct.inc(nc.tensor.transpose(pst[:, :], wq[:, i0:i0 + 128], idb[:]), 1)
            nc.vector.wait_ge(ct.h, ct.n)
            nc.vector.wait_ge(st.h, st.n)
            p.V.tensor_copy(trb[0:ni, 0:no], pst[0:ni, 0:no])
            nc.sync.wait_ge(cv.h, cv.n)
            st.inc(nc.sync.dma_start(out=wt_[i0:i0 + ni, o0:o0 + no], in_=trb[0:ni, 0:no]), 16)
    nc.sync.wait_ge(st.h, st.n)


def build_qdq_x(p, *, R, T, out_f32=False, out_name="xq_b"):
    nc = p.nc
    TT = 512
    QMAX = 32767.0
    x = p.din("x", (R, T))
    pmax8 = p.din("pmax8", (1, 8))
    ob = p.dout(out_name, (R, T), BF16)
    of = p.dout(out_name + "_f", (R, T)) if out_f32 else None
    sc_b = p.dint("sc_b", (1, 2))
    ld, cv, cs, st = (p.sem(s) for s in ("ld", "cv", "cs", "st"))
    pm = p.sbuf("pm", [1, 8], F32)
    sst = p.sbuf("sst", [1, 2], F32)
    qm = p.sbuf("qm", [1, 1], F32)
    sbc = p.sbuf("sbc", [128, 2], F32)
    xt = p.sbuf("xt", [128, TT], F32)
    xb = p.sbuf("xb", [128, TT], BF16)
    ld.inc(nc.sync.dma_start(out=pm[:], in_=pmax8), 16)
    nc.vector.wait_ge(ld.h, ld.n)
    p.V.reduce_max(qm[0:1, 0:1], pm[0:1, :], axis=AX)
    p.V.tensor_scalar(sst[0:1, 0:1], qm[0:1, 0:1], 1.0 / QMAX, 1e-12, ALU.mult, ALU.max)
    p.V.reciprocal(sst[0:1, 1:2], sst[0:1, 0:1])
    nc.sync.wait_ge(cv.h, cv.n)
    st.inc(nc.sync.dma_start(out=sc_b.ap(), in_=sst[:]), 16)
    nc.sync.wait_ge(st.h, st.n)
    ld.inc(nc.sync.dma_start(out=sbc[:], in_=sc_b.ap().partition_broadcast(128)), 16)
    sbc_d = ld.n
    for ki, (k0, nk) in enumerate(chunks(R, 128)):
        for ti, (t0, tt) in enumerate(chunks(T, TT)):
            nc.sync.wait_ge(cv.h, cv.n)
            nc.sync.wait_ge(cs.h, cs.n)
            nc.sync.wait_ge(st.h, st.n)
            ld.inc(nc.sync.dma_start(out=xt[0:nk, 0:tt], in_=x[k0:k0 + nk, t0:t0 + tt]), 16)
            nc.vector.wait_ge(ld.h, ld.n)
            p.V.tensor_scalar(xt[0:nk, 0:tt], xt[0:nk, 0:tt], sbc[0:nk, 1:2], RND, ALU.mult, ALU.add)
            p.V.tensor_scalar(xt[0:nk, 0:tt], xt[0:nk, 0:tt], RND, QMAX, ALU.subtract, ALU.min)
            p.V.tensor_scalar(xt[0:nk, 0:tt], xt[0:nk, 0:tt], -QMAX - 1.0, sbc[0:nk, 0:1], ALU.max, ALU.mult)
            nc.scalar.wait_ge(cv.h, cv.n)
            nc.scalar.wait_ge(st.h, st.n)
            p.S.copy(xb[0:nk, 0:tt], xt[0:nk, 0:tt])
            nc.sync.wait_ge(cs.h, cs.n)
            st.inc(nc.sync.dma_start(out=ob[k0:k0 + nk, t0:t0 + tt], in_=xb[0:nk, 0:tt]), 16)
            if out_f32:
                nc.sync.wait_ge(cv.h, cv.n)
                st.inc(nc.sync.dma_start(out=of[k0:k0 + nk, t0:t0 + tt], in_=xt[0:nk, 0:tt]), 16)
    nc.sync.wait_ge(st.h, st.n)


def build_ffn_mid(p, *, R=3072, T=TPC):
    nc = p.nc
    TT = 512
    QMAX = 32767.0
    up = p.din("up", (R, T))
    gate = p.din("gate", (R, T))
    pm_u = p.din("pm_up8", (1, 8))
    pm_g = p.din("pm_gate8", (1, 8))
    upq = p.dout("upq", (R, T))
    g2p = p.dout("g2p", (R, T))
    pmo = p.dout("pm_g2", (1, 8))
    sc_b = p.dint("sc_b", (1, 4))
    pmax_bounce = p.dint("pmax_bounce", (128, 1))
    ld, cv, cs, st = (p.sem(s) for s in ("ld", "cv", "cs", "st"))
    pm = p.sbuf("pm", [1, 8], F32)
    sst = p.sbuf("sst", [1, 4], F32)
    qm = p.sbuf("qm", [1, 1], F32)
    sbc = p.sbuf("sbc", [128, 4], F32)
    tu = p.sbuf("tu", [128, TT], F32)
    tg = p.sbuf("tg", [128, TT], F32)
    ts = p.sbuf("ts", [128, TT], F32)
    mx = p.sbuf("mx", [128, 1], F32)
    mx2 = p.sbuf("mx2", [128, 1], F32)
    mxr = p.sbuf("mxr", [1, 128], F32)
    mxr8 = p.sbuf("mxr8", [1, 8], F32)
    for j, pmd in ((0, pm_u), (2, pm_g)):
        ld.inc(nc.sync.dma_start(out=pm[:], in_=pmd), 16)
        nc.vector.wait_ge(ld.h, ld.n)
        p.V.reduce_max(qm[0:1, 0:1], pm[0:1, :], axis=AX)
        p.V.tensor_scalar(sst[0:1, j:j + 1], qm[0:1, 0:1], 1.0 / QMAX, 1e-12, ALU.mult, ALU.max)
        p.V.reciprocal(sst[0:1, j + 1:j + 2], sst[0:1, j:j + 1])
    nc.sync.wait_ge(cv.h, cv.n)
    st.inc(nc.sync.dma_start(out=sc_b.ap(), in_=sst[:]), 16)
    nc.sync.wait_ge(st.h, st.n)
    ld.inc(nc.sync.dma_start(out=sbc[:], in_=sc_b.ap().partition_broadcast(128)), 16)
    first = True
    for ki, (k0, nk) in enumerate(chunks(R, 128)):
        for ti, (t0, tt) in enumerate(chunks(T, TT)):
            nc.sync.wait_ge(cv.h, cv.n)
            nc.sync.wait_ge(cs.h, cs.n)
            nc.sync.wait_ge(st.h, st.n)
            ld.inc(nc.sync.dma_start(out=tu[0:nk, 0:tt], in_=up[k0:k0 + nk, t0:t0 + tt]), 16)
            ld.inc(nc.sync.dma_start(out=tg[0:nk, 0:tt], in_=gate[k0:k0 + nk, t0:t0 + tt]), 16)
            nc.vector.wait_ge(ld.h, ld.n)
            p.V.tensor_scalar(tu[0:nk, 0:tt], tu[0:nk, 0:tt], sbc[0:nk, 1:2], RND, ALU.mult, ALU.add)
            p.V.tensor_scalar(tu[0:nk, 0:tt], tu[0:nk, 0:tt], RND, QMAX, ALU.subtract, ALU.min)
            p.V.tensor_scalar(tu[0:nk, 0:tt], tu[0:nk, 0:tt], -QMAX - 1.0, sbc[0:nk, 0:1], ALU.max, ALU.mult)
            p.V.tensor_scalar(tg[0:nk, 0:tt], tg[0:nk, 0:tt], sbc[0:nk, 3:4], RND, ALU.mult, ALU.add)
            p.V.tensor_scalar(tg[0:nk, 0:tt], tg[0:nk, 0:tt], RND, QMAX, ALU.subtract, ALU.min)
            p.V.tensor_scalar(tg[0:nk, 0:tt], tg[0:nk, 0:tt], -QMAX - 1.0, sbc[0:nk, 2:3], ALU.max, ALU.mult)
            nc.scalar.wait_ge(cv.h, cv.n)
            p.S.activation(ts[0:nk, 0:tt], tg[0:nk, 0:tt], AF.Sigmoid)
            nc.vector.wait_ge(cs.h, cs.n)
            p.V.tensor_scalar(ts[0:nk, 0:tt], ts[0:nk, 0:tt], 65536.0, RND, ALU.mult, ALU.add)
            p.V.tensor_scalar(ts[0:nk, 0:tt], ts[0:nk, 0:tt], RND, 65535.0, ALU.subtract, ALU.min)
            p.V.tensor_scalar(ts[0:nk, 0:tt], ts[0:nk, 0:tt], 0.0, 1.0 / 65536.0, ALU.max, ALU.mult)
            p.V.tensor_tensor(tg[0:nk, 0:tt], tg[0:nk, 0:tt], ts[0:nk, 0:tt], ALU.mult)
            if first:
                p.V.reduce_max(mx[0:nk, 0:1], tg[0:nk, 0:tt], axis=AX, apply_absolute_value=True)
                first = False
            else:
                p.V.reduce_max(mx2[0:nk, 0:1], tg[0:nk, 0:tt], axis=AX, apply_absolute_value=True)
                p.V.tensor_tensor(mx[0:nk, 0:1], mx[0:nk, 0:1], mx2[0:nk, 0:1], ALU.max)
            nc.sync.wait_ge(cv.h, cv.n)
            st.inc(nc.sync.dma_start(out=upq[k0:k0 + nk, t0:t0 + tt], in_=tu[0:nk, 0:tt]), 16)
            st.inc(nc.sync.dma_start(out=g2p[k0:k0 + nk, t0:t0 + tt], in_=tg[0:nk, 0:tt]), 16)
    nc.sync.wait_ge(cv.h, cv.n)
    st.inc(nc.sync.dma_start(out=pmax_bounce.ap(), in_=mx[:]), 16)
    nc.sync.wait_ge(st.h, st.n)
    ld.inc(nc.sync.dma_start(out=mxr[0:1, :], in_=pmax_bounce.ap().rearrange("p x -> x p")), 16)
    nc.vector.wait_ge(ld.h, ld.n)
    p.V.memset(mxr8[0:1, :], 0.0)
    p.V.reduce_max(mxr8[0:1, 0:1], mxr[0:1, :], axis=AX)
    nc.sync.wait_ge(cv.h, cv.n)
    st.inc(nc.sync.dma_start(out=pmo, in_=mxr8[0:1, :]), 16)
    nc.sync.wait_ge(st.h, st.n)


def build_ffn_o(p, *, R=3072, T=TPC):
    nc = p.nc
    TT = 512
    QMAX = 32767.0
    g2pd = p.din("g2p", (R, T))
    upqd = p.din("upq", (R, T))
    pm8 = p.din("pm_g28", (1, 8))
    opre = p.dout("o_pre", (R, T))
    pmo = p.dout("pm_o", (1, 8))
    sc_b = p.dint("sc_b", (1, 2))
    pmax_bounce = p.dint("pmax_bounce", (128, 1))
    ld, cv, st = (p.sem(s) for s in ("ld", "cv", "st"))
    pm = p.sbuf("pm", [1, 8], F32)
    sst = p.sbuf("sst", [1, 2], F32)
    qm = p.sbuf("qm", [1, 1], F32)
    sbc = p.sbuf("sbc", [128, 2], F32)
    tg = p.sbuf("tg", [128, TT], F32)
    tu = p.sbuf("tu", [128, TT], F32)
    mx = p.sbuf("mx", [128, 1], F32)
    mx2 = p.sbuf("mx2", [128, 1], F32)
    mxr = p.sbuf("mxr", [1, 128], F32)
    mxr8 = p.sbuf("mxr8", [1, 8], F32)
    ld.inc(nc.sync.dma_start(out=pm[:], in_=pm8), 16)
    nc.vector.wait_ge(ld.h, ld.n)
    p.V.reduce_max(qm[0:1, 0:1], pm[0:1, :], axis=AX)
    p.V.tensor_scalar(sst[0:1, 0:1], qm[0:1, 0:1], 1.0 / QMAX, 1e-12, ALU.mult, ALU.max)
    p.V.reciprocal(sst[0:1, 1:2], sst[0:1, 0:1])
    nc.sync.wait_ge(cv.h, cv.n)
    st.inc(nc.sync.dma_start(out=sc_b.ap(), in_=sst[:]), 16)
    nc.sync.wait_ge(st.h, st.n)
    ld.inc(nc.sync.dma_start(out=sbc[:], in_=sc_b.ap().partition_broadcast(128)), 16)
    first = True
    for ki, (k0, nk) in enumerate(chunks(R, 128)):
        for ti, (t0, tt) in enumerate(chunks(T, TT)):
            nc.sync.wait_ge(cv.h, cv.n)
            nc.sync.wait_ge(st.h, st.n)
            ld.inc(nc.sync.dma_start(out=tg[0:nk, 0:tt], in_=g2pd[k0:k0 + nk, t0:t0 + tt]), 16)
            ld.inc(nc.sync.dma_start(out=tu[0:nk, 0:tt], in_=upqd[k0:k0 + nk, t0:t0 + tt]), 16)
            nc.vector.wait_ge(ld.h, ld.n)
            p.V.tensor_scalar(tg[0:nk, 0:tt], tg[0:nk, 0:tt], sbc[0:nk, 1:2], RND, ALU.mult, ALU.add)
            p.V.tensor_scalar(tg[0:nk, 0:tt], tg[0:nk, 0:tt], RND, QMAX, ALU.subtract, ALU.min)
            p.V.tensor_scalar(tg[0:nk, 0:tt], tg[0:nk, 0:tt], -QMAX - 1.0, sbc[0:nk, 0:1], ALU.max, ALU.mult)
            p.V.tensor_tensor(tg[0:nk, 0:tt], tg[0:nk, 0:tt], tu[0:nk, 0:tt], ALU.mult)
            if first:
                p.V.reduce_max(mx[0:nk, 0:1], tg[0:nk, 0:tt], axis=AX, apply_absolute_value=True)
                first = False
            else:
                p.V.reduce_max(mx2[0:nk, 0:1], tg[0:nk, 0:tt], axis=AX, apply_absolute_value=True)
                p.V.tensor_tensor(mx[0:nk, 0:1], mx[0:nk, 0:1], mx2[0:nk, 0:1], ALU.max)
            nc.sync.wait_ge(cv.h, cv.n)
            st.inc(nc.sync.dma_start(out=opre[k0:k0 + nk, t0:t0 + tt], in_=tg[0:nk, 0:tt]), 16)
    nc.sync.wait_ge(cv.h, cv.n)
    st.inc(nc.sync.dma_start(out=pmax_bounce.ap(), in_=mx[:]), 16)
    nc.sync.wait_ge(st.h, st.n)
    ld.inc(nc.sync.dma_start(out=mxr[0:1, :], in_=pmax_bounce.ap().rearrange("p x -> x p")), 16)
    nc.vector.wait_ge(ld.h, ld.n)
    p.V.memset(mxr8[0:1, :], 0.0)
    p.V.reduce_max(mxr8[0:1, 0:1], mxr[0:1, :], axis=AX)
    nc.sync.wait_ge(cv.h, cv.n)
    st.inc(nc.sync.dma_start(out=pmo, in_=mxr8[0:1, :]), 16)
    nc.sync.wait_ge(st.h, st.n)



def build_scan1(p, *, R=1024, T=L, NST=16, ag=False):
    nc = p.nc
    TT = 2048
    DC = R // 128
    ins = {}
    for br in ("f", "b"):
        if ag:
            ins[f"u_{br}"] = p.din(f"ag_u{br}", (2 * 2048, TPC))
            ins[f"xdbl_{br}"] = p.din(f"ag_xdbl_{br}", (192, TPC))
        else:
            ins[f"u_{br}"] = p.din(f"u_{br}", (R, T))
            ins[f"dtr_{br}"] = p.din(f"dtr_{br}", (64, T))
        ins[f"wdt_{br}"] = p.din(f"wdt_{br}", (64, R))
        ins[f"bdt_{br}"] = p.din(f"bdt_{br}", (128, R // 128))
        ins[f"Alog_{br}"] = p.din(f"Alog_{br}", (128, DC * NST))
        ins[f"Dp_{br}"] = p.din(f"Dp_{br}", (128, R // 128))
    if ag:
        zd = p.din("ag_z", (2 * 2048, TPC))
        idxd = p.din("sc1_idx", (128, 2 * DC), I32)
        ysum = p.dout("y_sum", (2 * R, T // 2))
        ysv = ysum.rearrange("(d h) t -> d (h t)", h=2)
    else:
        bcd = p.din("bc_all", (64, T))
        zd = p.din("z", (R, T))
        ysum = p.dout("y_sum", (R, T))
        ysv = ysum

    ld, lw, cv, cs, ct, st = (p.sem(s) for s in ("ld", "lw", "cv", "cs", "ct", "st"))
    wdtb = {br: p.sbuf(f"wdtb{br}", [64, R], BF16) for br in ("f", "b")}
    bdt = {br: p.sbuf(f"bdt{br}", [128, DC], F32) for br in ("f", "b")}
    Atl = {br: p.sbuf(f"Atl{br}", [128, DC * NST], F32) for br in ("f", "b")}
    Dpt = {br: p.sbuf(f"Dpt{br}", [128, DC], F32) for br in ("f", "b")}
    dtrb = {br: p.sbuf(f"dtrb{br}", [64, T], BF16) for br in ("f", "b")}

    with p.nc.sbuf_tensor(p.prefix + "tmp64", [64, T], F32) as tmp64:
        for br in ("f", "b"):
            lw.inc(nc.sync.dma_start(out=bdt[br][:, :], in_=ins[f"bdt_{br}"]), 16)
            lw.inc(nc.sync.dma_start(out=Atl[br][:], in_=ins[f"Alog_{br}"]), 16)
            lw.inc(nc.sync.dma_start(out=Dpt[br][:, :], in_=ins[f"Dp_{br}"]), 16)
            nc.sync.wait_ge(cv.h, cv.n)
            if ag:
                lw.inc(nc.sync.dma_start(out=tmp64[:, 0:TPC], in_=ins[f"xdbl_{br}"][0:64, :]), 16)
                lw.inc(nc.sync.dma_start(out=tmp64[:, TPC:T], in_=ins[f"xdbl_{br}"][96:160, :]), 16)
            else:
                lw.inc(nc.sync.dma_start(out=tmp64[:], in_=ins[f"dtr_{br}"]), 16)
            nc.vector.wait_ge(lw.h, lw.n)
            p.V.tensor_copy(dtrb[br][:], tmp64[:])
            nc.sync.wait_ge(cv.h, cv.n)
            lw.inc(nc.sync.dma_start(out=tmp64[0:64, 0:R], in_=ins[f"wdt_{br}"]), 16)
            nc.vector.wait_ge(lw.h, lw.n)
            p.V.tensor_copy(wdtb[br][:], tmp64[0:64, 0:R])
        nc.scalar.wait_ge(lw.h, lw.n)
        for br in ("f", "b"):
            p.S.activation(Atl[br][:], Atl[br][:], AF.Exp)
            nc.vector.wait_ge(cs.h, p.sem("cs").n)
            p.V.tensor_scalar(Atl[br][:], Atl[br][:], -1.0, None, ALU.mult)
    prep_v = cv.n

    scr = [p.sbuf(f"scr{i}", [128, 512], F32) for i in range(2)]
    du = p.sbuf("du", [128, T], F32)
    dl = p.sbuf("dl", [128, T], F32)
    ut = p.sbuf("ut", [128, T], F32)
    at = p.sbuf("at", [128, TT], F32)
    bt = p.sbuf("bt", [128, TT], F32)
    ht = p.sbuf("ht", [128, TT], F32)
    Brep = p.sbuf("Brep", [128, TT], F32)
    Crep = p.sbuf("Crep", [128, TT], F32)
    carry = p.sbuf("carry", [128, 1], F32)
    idxt = p.sbuf("idxs", [128, 2 * DC], I32) if ag else None
    yac = p.sbuf("yac", [128, T], F32)
    ytot = p.sbuf("ytot", [128, T], F32)
    zt = p.sbuf("zt", [128, T], F32)
    ps = [p.psum(f"ps{i}", [128, 512]) for i in range(2)]

    if ag:
        ld.inc(nc.sync.dma_start(out=idxt[:], in_=idxd), 16)
        nc.gpsimd.wait_ge(ld.h, ld.n)

    def load_rows(dst, srcd, ci):
        if ag:
            nc.gpsimd.wait_ge(cv.h, cv.n)
            nc.gpsimd.wait_ge(cs.h, p.sem("cs").n)
            nc.gpsimd.wait_ge(st.h, st.n)
            cg = p.sem("cg")
            for half in range(2):
                cg.inc(nc.gpsimd.indirect_dma_start(
                    out=dst[:, half * TPC:(half + 1) * TPC], out_offset=None, in_=srcd,
                    in_offset=bass.IndirectOffsetOnAxis(ap=idxt[:, half * DC + ci:half * DC + ci + 1], axis=0)), 16)
            return cg
        ld.inc(nc.sync.dma_start(out=dst[:], in_=srcd[ci * 128:(ci + 1) * 128, :]), 16)
        return ld

    for ci in range(DC):
        nc.sync.wait_ge(cv.h, cv.n)
        nc.sync.wait_ge(cs.h, p.sem("cs").n)
        nc.sync.wait_ge(st.h, st.n)
        zsem = load_rows(zt, zd, ci)
        nc.scalar.wait_ge(zsem.h, zsem.n)
        p.S.activation(zt[:], zt[:], AF.Silu)
        sz_mark = p.sem("cs").n
        first_br = True
        for br in ("f", "b"):
            rev = (br == "b")
            nc.sync.wait_ge(cv.h, cv.n)
            usem = load_rows(ut, ins[f"u_{br}"], ci)
            u_sem, u_ld = usem, usem.n
            for ti, (t0, tt) in enumerate(chunks(T, 512)):
                pb = ti % 2
                nc.tensor.wait_ge(cv.h, cv.n)
                ct.inc(nc.tensor.matmul(ps[pb][:, 0:tt], wdtb[br][:, ci * 128:(ci + 1) * 128],
                                        dtrb[br][:, t0:t0 + tt], start=True, stop=True), 1)
                nc.vector.wait_ge(ct.h, ct.n)
                p.V.tensor_scalar(dl[:, t0:t0 + tt], ps[pb][:, 0:tt], bdt[br][:, ci:ci + 1], None, ALU.add)
                nc.scalar.wait_ge(cv.h, cv.n)
                p.S.activation(scr[0][:, 0:tt], dl[:, t0:t0 + tt], AF.Abs)
                p.S.activation(scr[0][:, 0:tt], scr[0][:, 0:tt], AF.Exp, scale=-1.0)
                p.S.activation(scr[0][:, 0:tt], scr[0][:, 0:tt], AF.Ln, bias=1.0)
                nc.vector.wait_ge(cs.h, p.sem("cs").n)
                p.V.tensor_scalar(scr[1][:, 0:tt], dl[:, t0:t0 + tt], 0.0, None, ALU.max)
                p.V.tensor_add(dl[:, t0:t0 + tt], scr[0][:, 0:tt], scr[1][:, 0:tt])
            nc.vector.wait_ge(u_sem.h, u_ld)
            p.V.tensor_tensor(du[:], dl[:], ut[:], ALU.mult)
            bi = 0 if br == "f" else 1
            for n in range(NST):
                tcl = list(chunks(T, TT))
                if rev:
                    tcl = tcl[::-1]
                for tj, (t0, tt) in enumerate(tcl):
                    nc.sync.wait_ge(cv.h, cv.n)
                    if ag:
                        xb = ins[f"xdbl_{br}"]
                        r0 = (t0 // TPC) * 96
                        ld.inc(nc.sync.dma_start(out=Brep[:, 0:tt],
                                                 in_=xb[r0 + 64 + n:r0 + 64 + n + 1, 0:tt].partition_broadcast(128)), 16)
                        ld.inc(nc.sync.dma_start(out=Crep[:, 0:tt],
                                                 in_=xb[r0 + 80 + n:r0 + 80 + n + 1, 0:tt].partition_broadcast(128)), 16)
                    else:
                        ld.inc(nc.sync.dma_start(out=Brep[:, 0:tt],
                                                 in_=bcd[bi * 32 + n:bi * 32 + n + 1, t0:t0 + tt].partition_broadcast(128)), 16)
                        ld.inc(nc.sync.dma_start(out=Crep[:, 0:tt],
                                                 in_=bcd[bi * 32 + 16 + n:bi * 32 + 16 + n + 1, t0:t0 + tt].partition_broadcast(128)), 16)
                    nc.scalar.wait_ge(cv.h, cv.n)
                    p.S.activation(at[:, 0:tt], dl[:, t0:t0 + tt], AF.Exp,
                                   scale=Atl[br][:, ci * NST + n:ci * NST + n + 1])
                    nc.vector.wait_ge(ld.h, ld.n)
                    nc.vector.wait_ge(cs.h, p.sem("cs").n)
                    p.V.tensor_tensor(bt[:, 0:tt], du[:, t0:t0 + tt], Brep[:, 0:tt], ALU.mult)
                    init = 0.0 if tj == 0 else carry[:, 0:1]
                    if rev:
                        p.V.tensor_tensor_scan(ht[:, tt - 1::-1], at[:, tt - 1::-1], bt[:, tt - 1::-1],
                                               init, ALU.mult, ALU.add)
                        p.V.tensor_copy(carry[:, 0:1], ht[:, 0:1])
                    else:
                        p.V.tensor_tensor_scan(ht[:, 0:tt], at[:, 0:tt], bt[:, 0:tt],
                                               init, ALU.mult, ALU.add)
                        p.V.tensor_copy(carry[:, 0:1], ht[:, tt - 1:tt])
                    p.V.tensor_tensor(ht[:, 0:tt], ht[:, 0:tt], Crep[:, 0:tt], ALU.mult)
                    if n == 0:
                        p.V.tensor_copy(yac[:, t0:t0 + tt], ht[:, 0:tt])
                    else:
                        p.V.tensor_add(yac[:, t0:t0 + tt], yac[:, t0:t0 + tt], ht[:, 0:tt])
            nc.vector.wait_ge(cs.h, sz_mark)
            p.V.scalar_tensor_tensor(yac[:], ut[:], Dpt[br][:, ci:ci + 1], yac[:], ALU.mult, ALU.add)
            p.V.tensor_tensor(yac[:], yac[:], zt[:], ALU.mult)
            if first_br:
                p.V.tensor_copy(ytot[:], yac[:])
                first_br = False
            else:
                p.V.tensor_add(ytot[:], ytot[:], yac[:])
        nc.sync.wait_ge(cv.h, cv.n)
        st.inc(nc.sync.dma_start(out=ysv[ci * 128:(ci + 1) * 128, :], in_=ytot[:]), 16)
    nc.sync.wait_ge(st.h, st.n)


def build_f1qkv(p, *, TH=2560, NTOK=2304, QOFF=257):
    nc = p.nc
    TT = 512
    xd = p.din("x", (DM, TH))
    wqk = p.din("wqk", (DM, 2048))
    wv = p.din("wv", (DM, DM))
    cosq = p.din("cosq", (128, 2048))
    sinq = p.din("sinq", (128, 2048))
    cosk = p.din("cosk", (128, NTOK))
    sink = p.din("sink", (128, NTOK))
    sclq = p.din("sclq", (2, 128))
    sclk = p.din("sclk", (2, 128))
    indd = p.din("ind2", (128, 2))
    rotm = p.din("rotm", (128, 128))
    q2b = p.dout("q2b", (DM, 2048), BF16)
    k2b = p.dout("k2b", (DM, NTOK), BF16)
    v_tok = p.dout("v_tok", (NTOK, DM), BF16)

    ld, lw, cv, cs, ct, st = (p.sem(s) for s in ("ld", "lw", "cv", "cs", "ct", "st"))
    xt = p.sbuf("xt", [128, 8 * TH], F32)
    sxb = p.sbuf("sxb", [128, 8 * TH], BF16)
    wtb = [p.sbuf(f"wtb{i}", [128, 8 * 128], BF16) for i in range(2)]
    wtf = p.sbuf("wtf", [128, 8 * 128], F32)
    vt_w = p.sbuf("vt_w", [128, 8 * DM], BF16)
    ind = p.sbuf("ind", [128, 2], F32)
    scl = p.sbuf("scl", [2, 2 * 128], F32)
    rot = p.sbuf("rot", [128, 128], BF16)
    rotf = p.sbuf("rotf", [128, 128], F32)
    tabc = p.sbuf("tabc", [128, TT], F32)
    tabs_ = p.sbuf("tabs", [128, TT], F32)
    qf = p.sbuf("qf", [128, TT], F32)
    qsq = p.sbuf("qsq", [128, TT], F32)
    q1b = p.sbuf("q1b", [128, TT], BF16)
    nrm = p.sbuf("nrm", [2, TT], F32)
    q2t = p.sbuf("q2t", [128, TT], F32)
    q2c = p.sbuf("q2c", [128, TT], BF16)
    vevac = p.sbuf("vevac", [128, TT], BF16)
    ps_a = p.psum("ps_a", [128, TT])
    ps_n = p.psum("ps_n", [2, TT])
    ps_r = p.psum("ps_r", [128, TT])
    ps_b = p.psum("ps_b", [128, TT])

    for ki in range(8):
        ld.inc(nc.sync.dma_start(out=xt[:, ki * TH:(ki + 1) * TH], in_=xd[ki * 128:(ki + 1) * 128, :]), 16)
    ld.inc(nc.sync.dma_start(out=rotf[:], in_=rotm), 16)
    ld.inc(nc.sync.dma_start(out=ind[:], in_=indd), 16)
    ld.inc(nc.sync.dma_start(out=scl[0:2, 0:128], in_=sclq), 16)
    ld.inc(nc.sync.dma_start(out=scl[0:2, 128:256], in_=sclk), 16)
    nc.vector.wait_ge(ld.h, ld.n)
    p.V.tensor_copy(rot[:], rotf[:])
    nc.scalar.wait_ge(ld.h, ld.n)
    for ki in range(8):
        src_off = ki * TH + (1 if ki < 4 else 0)
        p.S.copy(sxb[:, ki * TH + 1:ki * TH + 1 + NTOK], xt[:, src_off:src_off + NTOK])
    sx_mark = p.sem("cs").n

    for ki in range(8):
        nc.scalar.wait_ge(lw.h, lw.n)
        lw.inc(nc.scalar.dma_start(out=wtf[:, 0:8 * 128], in_=wv[ki * 128:(ki + 1) * 128, :].rearrange("p (a b) -> p (a b)", a=8) if False else wv[ki * 128:(ki + 1) * 128, :]), 16)
        nc.scalar.wait_ge(lw.h, lw.n)
        p.S.copy(vt_w[:, ki * DM:(ki + 1) * DM], wtf[:, 0:DM])
    vt_wmark = p.sem("cs").n

    nc.tensor.wait_ge(cs.h, vt_wmark)
    for tc, (tk0, ntk) in enumerate(chunks(NTOK, 128)):
        for nt in range(2):
            nc.tensor.wait_ge(cv.h, cv.n)
            ins_mm = None
            for ki in range(8):
                ins_mm = nc.tensor.matmul(ps_a[0:ntk, 0:512],
                                          sxb[:, ki * TH + 1 + tk0:ki * TH + 1 + tk0 + ntk],
                                          vt_w[:, ki * DM + nt * 512:ki * DM + nt * 512 + 512],
                                          start=(ki == 0), stop=(ki == 7))
            ct.inc(ins_mm, 1)
            nc.vector.wait_ge(ct.h, ct.n)
            nc.vector.wait_ge(st.h, st.n)
            p.V.tensor_copy(vevac[0:ntk, 0:512], ps_a[0:ntk, 0:512])
            nc.sync.wait_ge(cv.h, cv.n)
            st.inc(nc.sync.dma_start(out=v_tok[tk0:tk0 + ntk, nt * 512:nt * 512 + 512], in_=vevac[0:ntk, 0:512]), 16)

    for qk in range(2):
        ncols = 2048 if qk == 0 else NTOK
        coff = QOFF if qk == 0 else 1
        cosd, sind = (cosq, sinq) if qk == 0 else (cosk, sink)
        outd = q2b if qk == 0 else k2b
        for ti, (t0, tt) in enumerate(chunks(ncols, TT)):
            for ci in range(8):
                buf = ci % 2
                nc.vector.wait_ge(cs.h, p.sem("cs").n)
                nc.scalar.wait_ge(ct.h, ct.n)
                nc.scalar.wait_ge(cv.h, cv.n)
                for kj in range(8):
                    lw.inc(nc.scalar.dma_start(out=wtf[0:128, kj * 128:kj * 128 + 128],
                                               in_=wqk[kj * 128:(kj + 1) * 128,
                                                       qk * 1024 + ci * 128:qk * 1024 + ci * 128 + 128]), 16)
                nc.scalar.wait_ge(lw.h, lw.n)
                p.S.copy(wtb[buf][:, :], wtf[:, :])
                wmark = p.sem("cs").n
                nc.tensor.wait_ge(cs.h, wmark)
                nc.tensor.wait_ge(cv.h, cv.n)
                ins_mm = None
                for kj in range(8):
                    ins_mm = nc.tensor.matmul(ps_a[:, 0:tt], wtb[buf][:, kj * 128:kj * 128 + 128],
                                              sxb[:, kj * TH + coff + t0:kj * TH + coff + t0 + tt],
                                              start=(kj == 0), stop=(kj == 7))
                ct.inc(ins_mm, 1)
                nc.vector.wait_ge(ct.h, ct.n)
                p.V.tensor_copy(qf[:, 0:tt], ps_a[:, 0:tt])
                nc.scalar.wait_ge(cv.h, cv.n)
                p.S.activation(qsq[:, 0:tt], qf[:, 0:tt], AF.Square)
                nc.tensor.wait_ge(cs.h, p.sem("cs").n)
                ct.inc(nc.tensor.matmul(ps_n[0:2, 0:tt], ind[:, 0:2], qsq[:, 0:tt], start=True, stop=True), 1)
                nc.vector.wait_ge(ct.h, ct.n)
                p.V.tensor_scalar(nrm[0:2, 0:tt], ps_n[0:2, 0:tt], 1e-30, None, ALU.max)
                nc.scalar.wait_ge(cv.h, cv.n)
                p.S.activation(nrm[0:2, 0:tt], nrm[0:2, 0:tt], AF.Ln)
                p.S.activation(nrm[0:2, 0:tt], nrm[0:2, 0:tt], AF.Exp, scale=0.5)
                nc.vector.wait_ge(cs.h, p.sem("cs").n)
                p.V.tensor_scalar(nrm[0:2, 0:tt], nrm[0:2, 0:tt], 1e-12, None, ALU.max)
                p.V.reciprocal(nrm[0:2, 0:tt], nrm[0:2, 0:tt])
                nc.tensor.wait_ge(cv.h, cv.n)
                ct.inc(nc.tensor.matmul(ps_r[:, 0:tt], scl[0:2, qk * 128:qk * 128 + 128], nrm[0:2, 0:tt],
                                        start=True, stop=True), 1)
                nc.vector.wait_ge(ct.h, ct.n)
                p.V.tensor_tensor(qf[:, 0:tt], qf[:, 0:tt], ps_r[:, 0:tt], ALU.mult)
                nc.scalar.wait_ge(cv.h, cv.n)
                p.S.copy(q1b[:, 0:tt], qf[:, 0:tt])
                nc.tensor.wait_ge(cs.h, p.sem("cs").n)
                ct.inc(nc.tensor.matmul(ps_b[:, 0:tt], rot[:, :], q1b[:, 0:tt], start=True, stop=True), 1)
                rot_mark = ct.n
                nc.sync.wait_ge(cv.h, cv.n)
                ld.inc(nc.sync.dma_start(out=tabc[:, 0:tt], in_=cosd[:, t0:t0 + tt]), 16)
                ld.inc(nc.sync.dma_start(out=tabs_[:, 0:tt], in_=sind[:, t0:t0 + tt]), 16)
                nc.vector.wait_ge(ld.h, ld.n)
                nc.vector.wait_ge(ct.h, rot_mark)
                p.V.tensor_tensor(q2t[:, 0:tt], qf[:, 0:tt], tabc[:, 0:tt], ALU.mult)
                p.V.tensor_tensor(qsq[:, 0:tt], ps_b[:, 0:tt], tabs_[:, 0:tt], ALU.mult)
                p.V.tensor_add(q2t[:, 0:tt], q2t[:, 0:tt], qsq[:, 0:tt])
                nc.scalar.wait_ge(cv.h, cv.n)
                nc.scalar.wait_ge(st.h, st.n)
                p.S.copy(q2c[:, 0:tt], q2t[:, 0:tt])
                nc.sync.wait_ge(cs.h, p.sem("cs").n)
                st.inc(nc.sync.dma_start(out=outd[ci * 128:(ci + 1) * 128, t0:t0 + tt], in_=q2c[:, 0:tt]), 16)
    nc.sync.wait_ge(st.h, st.n)


def build_f1attn(p, *, NTOK=2304):
    nc = p.nc
    q2 = p.din("q2b", (DM, 2048), BF16)
    k2 = p.din("k2b", (DM, NTOK), BF16)
    vt = p.din("v_tok", (NTOK, DM), BF16)
    maskd = p.din("masks", (8 * 2 * 128, 512))
    wod = p.din("wo", (DM, DM))
    xres = p.din("xres", (DM, 2048))
    xo = p.dout("xo", (DM, 2048))

    ld, lw, cv, cs, ct, st = (p.sem(s) for s in ("ld", "lw", "cv", "cs", "ct", "st"))
    ts_ = p.sem("ts")
    q2t = p.sbuf("q2t", [128, 8 * 2048], BF16)
    k2t = p.sbuf("k2t", [128, 8 * NTOK], BF16)
    vtt = p.sbuf("vtt", [128, 18 * DM], BF16)
    mt = [p.sbuf(f"mt{i}", [128, 512], F32) for i in range(2)]
    sm = [p.sbuf(f"sm{i}", [128, 512], F32) for i in range(2)]
    mx = [p.sbuf(f"mx{i}", [128, 1], F32) for i in range(2)]
    den = [p.sbuf(f"den{i}", [128, 1], F32) for i in range(2)]
    pb_ = [p.sbuf(f"pb{i}", [128, 512], BF16) for i in range(2)]
    pT = [p.sbuf(f"pT{i}", [128, 4 * 128], BF16) for i in range(2)]
    aot = p.sbuf("aot", [128, 8 * 2048], BF16)
    wtf = p.sbuf("wtf", [128, 8 * 128], F32)
    wtb = [p.sbuf(f"wtb{i}", [128, 8 * 128], BF16) for i in range(2)]
    rtile = p.sbuf("rtile", [128, 512], F32)
    otile = [p.sbuf(f"otile{i}", [128, 512], F32) for i in range(2)]
    ps_s = [p.psum(f"ps_s{i}", [128, 512]) for i in range(2)]
    ps_o = [p.psum(f"ps_o{i}", [64, 128]) for i in range(2)]
    ps_w = [p.psum(f"ps_w{i}", [128, 512]) for i in range(2)]

    for ki in range(8):
        ld.inc(nc.sync.dma_start(out=q2t[:, ki * 2048:(ki + 1) * 2048], in_=q2[ki * 128:(ki + 1) * 128, :]), 16)
        ld.inc(nc.sync.dma_start(out=k2t[:, ki * NTOK:(ki + 1) * NTOK], in_=k2[ki * 128:(ki + 1) * 128, :]), 16)
    for tc in range(18):
        ld.inc(nc.sync.dma_start(out=vtt[:, tc * DM:(tc + 1) * DM], in_=vt[tc * 128:(tc + 1) * 128, :]), 16)
    in_ld = ld.n
    nc.tensor.wait_ge(ld.h, in_ld)

    NB = 2
    sim_mark = [0] * NB
    stt_mark = [0] * NB
    exp_mark = [0] * NB
    scl_mark = [0] * NB
    ts_mark = [0] * NB
    av_mark = [0] * NB
    evac_mark = [0] * NB
    it = 0
    m_lds = {}
    for w in range(8):
        for ih in range(2):
            mb = mt[(w * 2 + ih) % 2]
            nc.sync.wait_ge(cv.h, max(stt_mark))
            ld.inc(nc.sync.dma_start(out=mb[:], in_=maskd[(w * 2 + ih) * 128:(w * 2 + ih) * 128 + 128, :]), 16)
            m_ld = ld.n
            for h in range(16):
                j = it % NB
                ci, hh = h // 2, h % 2
                po = 64 * hh
                kvb = w * 256
                qcol = w * 256 + ih * 128
                nc.tensor.wait_ge(ld.h, in_ld)
                nc.tensor.wait_ge(cv.h, stt_mark[j])
                ct.inc(nc.tensor.matmul(ps_s[j][0:128, 0:512],
                                        q2t[po:po + 64, ci * 2048 + qcol:ci * 2048 + qcol + 128],
                                        k2t[po:po + 64, ci * NTOK + kvb:ci * NTOK + kvb + 512],
                                        start=True, stop=True), 1)
                sim_mark[j] = ct.n
                nc.vector.wait_ge(ld.h, m_ld)
                nc.vector.wait_ge(ct.h, sim_mark[j])
                nc.vector.wait_ge(cs.h, exp_mark[j])
                p.V.scalar_tensor_tensor(sm[j][:], ps_s[j][0:128, 0:512], 8.0, mb[:], ALU.mult, ALU.add)
                p.V.reduce_max(mx[j][:], sm[j][:], axis=AX)
                p.V.tensor_scalar(mx[j][:], mx[j][:], -1.0, None, ALU.mult)
                stt_mark[j] = cv.n
                nc.scalar.wait_ge(cv.h, stt_mark[j])
                nc.scalar.wait_ge(ts_.h, ts_mark[j])
                p.S.activation(pb_[j][:], sm[j][:], AF.Exp, bias=mx[j][:, 0:1], accum_out=den[j][:])
                exp_mark[j] = p.sem("cs").n
                nc.vector.wait_ge(cs.h, exp_mark[j])
                p.V.reciprocal(den[j][:], den[j][:])
                p.V.tensor_scalar(pb_[j][:], pb_[j][:], den[j][:, 0:1], None, ALU.mult)
                scl_mark[j] = cv.n
                nc.scalar.wait_ge(cv.h, scl_mark[j])
                nc.scalar.wait_ge(ct.h, av_mark[j])
                for jc in range(4):
                    ts_.inc(nc.scalar.dma_start_transpose(pT[j][:, jc * 128:jc * 128 + 128],
                                                          pb_[j][:, jc * 128:jc * 128 + 128]), 16)
                ts_mark[j] = ts_.n
                nc.tensor.wait_ge(ts_.h, ts_mark[j])
                nc.tensor.wait_ge(cv.h, evac_mark[j])
                ins_mm = None
                for jc in range(4):
                    tc = w * 2 + jc
                    ins_mm = nc.tensor.matmul(ps_o[j][0:64, 0:128],
                                              vtt[:, tc * DM + h * 64:tc * DM + h * 64 + 64],
                                              pT[j][:, jc * 128:jc * 128 + 128],
                                              start=(jc == 0), stop=(jc == 3))
                ct.inc(ins_mm, 1)
                av_mark[j] = ct.n
                nc.vector.wait_ge(ct.h, av_mark[j])
                p.V.tensor_copy(aot[po:po + 64, ci * 2048 + qcol:ci * 2048 + qcol + 128], ps_o[j][0:64, 0:128])
                evac_mark[j] = cv.n
                it += 1
    ao_mark = cv.n

    TT = 512
    q = 0
    ps_free = [0, 0]
    wt_last = [0, 0]
    for mi in range(8):
        buf = mi % 2
        nc.scalar.wait_ge(ct.h, wt_last[buf])
        for kj in range(8):
            lw.inc(nc.scalar.dma_start(out=wtf[:, kj * 128:kj * 128 + 128],
                                       in_=wod[kj * 128:(kj + 1) * 128, mi * 128:mi * 128 + 128]), 16)
        nc.scalar.wait_ge(lw.h, lw.n)
        p.S.copy(wtb[buf][:, :], wtf[:, :])
        wmark = p.sem("cs").n
        for ti in range(4):
            t0 = ti * TT
            pbk = q % 2
            nc.sync.wait_ge(cv.h, cv.n)
            ld.inc(nc.sync.dma_start(out=rtile[:, :], in_=xres[mi * 128:mi * 128 + 128, t0:t0 + TT]), 16)
            r_ld = ld.n
            nc.tensor.wait_ge(cs.h, wmark)
            nc.tensor.wait_ge(cv.h, max(ao_mark, ps_free[pbk]))
            ins_mm = None
            for kj in range(8):
                ins_mm = nc.tensor.matmul(ps_w[pbk][:, 0:TT], wtb[buf][:, kj * 128:kj * 128 + 128],
                                          aot[:, kj * 2048 + t0:kj * 2048 + t0 + TT],
                                          start=(kj == 0), stop=(kj == 7))
            ct.inc(ins_mm, 1)
            wt_last[buf] = ct.n
            nc.vector.wait_ge(ct.h, ct.n)
            nc.vector.wait_ge(ld.h, r_ld)
            nc.vector.wait_ge(st.h, st.n)
            p.V.tensor_tensor(otile[pbk][:, :], ps_w[pbk][:, 0:TT], rtile[:, :], ALU.add)
            ps_free[pbk] = cv.n
            nc.sync.wait_ge(cv.h, cv.n)
            st.inc(nc.sync.dma_start(out=xo[mi * 128:mi * 128 + 128, t0:t0 + TT], in_=otile[pbk][:, :]), 16)
            q += 1
    nc.sync.wait_ge(st.h, st.n)


def build_f1h(p, *, T=TPC):
    nc = p.nc
    TT = 512
    xd = p.din("x", (DM, T + 1))
    w1a = p.din("w1a", (DM, 2730))
    w1g = p.din("w1g", (DM, 2730))
    b1a = p.din("b1a", (2730, 1))
    b1g = p.din("b1g", (2730, 1))
    lngd = p.din("lng", (128, 8))
    lnbd = p.din("lnb", (128, 8))
    onesd = p.din("ones128", (128, 128))
    hgate = p.dout("hgate", (2730, T), BF16)

    ld, lw, cv, cs, ct, st = (p.sem(s) for s in ("ld", "lw", "cv", "cs", "ct", "st"))
    xt = p.sbuf("xt", [128, 8 * (T + 1)], F32)
    xnb = p.sbuf("xnb", [128, 8 * T], BF16)
    sq = [p.sbuf(f"sq{i}", [128, TT], F32) for i in range(2)]
    ones = p.sbuf("ones", [128, 128], F32)
    lng = p.sbuf("lngt", [128, 8], F32)
    lnb = p.sbuf("lnbt", [128, 8], F32)
    statm = p.sbuf("statm", [1, TT], F32)
    statr = p.sbuf("statr", [1, TT], F32)
    tvar = p.sbuf("tvar", [1, TT], F32)
    tmsq = p.sbuf("tmsq", [1, TT], F32)
    xo = p.sbuf("xo", [128, TT], F32)
    wtfa = p.sbuf("wtfa", [128, 8 * 128], F32)
    wtba = [p.sbuf(f"wtba{i}", [128, 8 * 128], BF16) for i in range(2)]
    wtbg = [p.sbuf(f"wtbg{i}", [128, 8 * 128], BF16) for i in range(2)]
    p.S.memzero(wtfa[:, :])
    bta = p.sbuf("bta", [128, 1], F32)
    btg = p.sbuf("btg", [128, 1], F32)
    gel = p.sbuf("gel", [128, TT], F32)
    hb = p.sbuf("hb", [128, TT], BF16)
    ps_s = p.psum("ps_s", [1, TT])
    ps_q = p.psum("ps_q", [1, TT])
    ps_b = p.psum("ps_b", [128, 2 * TT])
    ps_a = [p.psum(f"ps_a{i}", [128, TT]) for i in range(2)]
    ps_g = [p.psum(f"ps_g{i}", [128, TT]) for i in range(2)]

    ld.inc(nc.sync.dma_start(out=ones[:], in_=onesd), 16)
    ld.inc(nc.sync.dma_start(out=lng[:, :], in_=lngd), 16)
    ld.inc(nc.sync.dma_start(out=lnb[:, :], in_=lnbd), 16)
    for ki in range(8):
        ld.inc(nc.sync.dma_start(out=xt[:, ki * (T + 1):(ki + 1) * (T + 1)], in_=xd[ki * 128:(ki + 1) * 128, :]), 16)
    x_ld = ld.n

    for ti, (t0, tt) in enumerate(chunks(T, TT)):
        def sx(ki):
            off = ki * (T + 1) + (1 if ki < 4 else 0)
            return xt[:, off + t0:off + t0 + tt]
        nc.tensor.wait_ge(ld.h, x_ld)
        nc.tensor.wait_ge(cv.h, cv.n)
        ins_mm = None
        for ki in range(8):
            ins_mm = nc.tensor.matmul(ps_s[0:1, 0:tt], ones[:, 0:1], sx(ki), start=(ki == 0), stop=(ki == 7))
        ct.inc(ins_mm, 1)
        mm_sq = []
        for ki in range(8):
            sb = sq[ki % 2]
            nc.scalar.wait_ge(ld.h, x_ld)
            if ki >= 2:
                nc.scalar.wait_ge(ct.h, mm_sq[ki - 2])
            p.S.activation(sb[:, 0:tt], sx(ki), AF.Square)
            nc.tensor.wait_ge(cs.h, p.sem("cs").n)
            ct.inc(nc.tensor.matmul(ps_q[0:1, 0:tt], ones[:, 0:1], sb[:, 0:tt],
                                    start=(ki == 0), stop=(ki == 7)), 1)
            mm_sq.append(ct.n)
        nc.vector.wait_ge(ct.h, ct.n)
        p.V.tensor_scalar(statm[0:1, 0:tt], ps_s[0:1, 0:tt], 1.0 / DM, None, ALU.mult)
        p.V.tensor_scalar(tmsq[0:1, 0:tt], ps_q[0:1, 0:tt], 1.0 / DM, None, ALU.mult)
        p.V.tensor_tensor(tvar[0:1, 0:tt], statm[0:1, 0:tt], statm[0:1, 0:tt], ALU.mult)
        p.V.tensor_tensor(tvar[0:1, 0:tt], tmsq[0:1, 0:tt], tvar[0:1, 0:tt], ALU.subtract)
        p.V.tensor_scalar(tvar[0:1, 0:tt], tvar[0:1, 0:tt], 1e-5, None, ALU.add)
        nc.scalar.wait_ge(cv.h, cv.n)
        p.S.activation(tvar[0:1, 0:tt], tvar[0:1, 0:tt], AF.Ln)
        p.S.activation(statr[0:1, 0:tt], tvar[0:1, 0:tt], AF.Exp, scale=-0.5)
        nc.tensor.wait_ge(cs.h, p.sem("cs").n)
        nc.tensor.wait_ge(cv.h, cv.n)
        nc.tensor.matmul(ps_b[:, 0:tt], ones[0:1, :], statm[0:1, 0:tt], start=True, stop=True)
        ct.inc(nc.tensor.matmul(ps_b[:, TT:TT + tt], ones[0:1, :], statr[0:1, 0:tt], start=True, stop=True), 1)
        bmark = ct.n
        for ki in range(8):
            nc.vector.wait_ge(ct.h, bmark)
            nc.vector.wait_ge(ld.h, x_ld)
            nc.vector.wait_ge(cs.h, p.sem("cs").n)
            p.V.tensor_tensor(xo[:, 0:tt], sx(ki), ps_b[:, 0:tt], ALU.subtract)
            p.V.tensor_tensor(xo[:, 0:tt], xo[:, 0:tt], ps_b[:, TT:TT + tt], ALU.mult)
            p.V.tensor_scalar(xo[:, 0:tt], xo[:, 0:tt], lng[:, ki:ki + 1], lnb[:, ki:ki + 1], ALU.mult, ALU.add)
            nc.scalar.wait_ge(cv.h, cv.n)
            p.S.copy(xnb[:, ki * T + t0:ki * T + t0 + tt], xo[:, 0:tt])
    xn_mark = p.sem("cs").n

    mcs = chunks(2730, 128)
    wt_last = [0, 0]
    q = 0
    for mi, (m0, nm) in enumerate(mcs):
        buf = mi % 2
        nc.scalar.wait_ge(ct.h, wt_last[buf])
        for kj in range(8):
            lw.inc(nc.scalar.dma_start(out=wtfa[0:128, kj * 128:kj * 128 + nm],
                                       in_=w1a[kj * 128:(kj + 1) * 128, m0:m0 + nm]), 16)
        nc.scalar.wait_ge(lw.h, lw.n)
        p.S.copy(wtba[buf][:, :], wtfa[:, :])
        for kj in range(8):
            lw.inc(nc.scalar.dma_start(out=wtfa[0:128, kj * 128:kj * 128 + nm],
                                       in_=w1g[kj * 128:(kj + 1) * 128, m0:m0 + nm]), 16)
        nc.scalar.wait_ge(lw.h, lw.n)
        p.S.copy(wtbg[buf][:, :], wtfa[:, :])
        wmark = p.sem("cs").n
        nc.sync.wait_ge(cv.h, cv.n)
        nc.sync.wait_ge(cs.h, p.sem("cs").n)
        lw.inc(nc.sync.dma_start(out=bta[0:nm, :], in_=b1a[m0:m0 + nm, :]), 16)
        lw.inc(nc.sync.dma_start(out=btg[0:nm, :], in_=b1g[m0:m0 + nm, :]), 16)
        b_ld = lw.n
        for ti, (t0, tt) in enumerate(chunks(T, TT)):
            pbk = q % 2
            nc.tensor.wait_ge(cs.h, p.sem("cs").n)
            nc.tensor.wait_ge(cv.h, cv.n)
            for kj in range(8):
                nc.tensor.matmul(ps_a[pbk][0:nm, 0:tt], wtba[buf][:, kj * 128:kj * 128 + nm],
                                 xnb[:, kj * T + t0:kj * T + t0 + tt], start=(kj == 0), stop=(kj == 7))
            ins_mm = None
            for kj in range(8):
                ins_mm = nc.tensor.matmul(ps_g[pbk][0:nm, 0:tt], wtbg[buf][:, kj * 128:kj * 128 + nm],
                                          xnb[:, kj * T + t0:kj * T + t0 + tt], start=(kj == 0), stop=(kj == 7))
            ct.inc(ins_mm, 1)
            wt_last[buf] = ct.n
            mmk = ct.n
            nc.scalar.wait_ge(ct.h, mmk)
            nc.scalar.wait_ge(lw.h, b_ld)
            nc.scalar.wait_ge(cv.h, cv.n)
            p.S.activation(gel[0:nm, 0:tt], ps_g[pbk][0:nm, 0:tt], AF.Gelu, bias=btg[0:nm, 0:1])
            nc.vector.wait_ge(cs.h, p.sem("cs").n)
            nc.vector.wait_ge(st.h, st.n)
            p.V.tensor_scalar(sq[0][0:nm, 0:tt], ps_a[pbk][0:nm, 0:tt], bta[0:nm, 0:1], None, ALU.add)
            p.V.tensor_tensor(sq[0][0:nm, 0:tt], sq[0][0:nm, 0:tt], gel[0:nm, 0:tt], ALU.mult)
            nc.scalar.wait_ge(cv.h, cv.n)
            nc.scalar.wait_ge(st.h, st.n)
            p.S.copy(hb[0:nm, 0:tt], sq[0][0:nm, 0:tt])
            nc.sync.wait_ge(cs.h, p.sem("cs").n)
            st.inc(nc.sync.dma_start(out=hgate[m0:m0 + nm, t0:t0 + tt], in_=hb[0:nm, 0:tt]), 16)
            q += 1
    nc.sync.wait_ge(st.h, st.n)


def _cc(a):
    return np.ascontiguousarray(a)


def _pad_t(a, Tpad):
    out = np.zeros((a.shape[0], Tpad), dtype=a.dtype)
    out[:, :a.shape[1]] = a
    return out


def run_l0(inputs, store):
    x = np.asarray(inputs["L0_attn_x"])
    cores = [(i // 2, i % 2) for i in range(NC)]
    ones128 = np.ones((128, 128), np.float32)
    ident = np.eye(128, dtype=np.float32)

    TH = 2560
    xs = []
    for b, h in cores:
        t0 = h * TPC
        xa = np.zeros((DM, TH), np.float32)
        if h == 1:
            xa[:, 0:3] = x[b, t0 - 3:t0, :].T
        xa[:, 3:3 + TPC] = x[b, t0:t0 + TPC, :].T
        xs.append(_cc(xa))
    store["x_fm"] = [_cc(a[:, 3:3 + TPC]) for a in xs]
    r = run_launch("ln0", build_ln, [{"x": xs[i], "ones128": ones128} for i in range(NC)],
                   T=TH, want_pmax=True)
    store["xn_f"] = [r[i]["xn_f"] for i in range(NC)]
    store["xn_b"] = [r[i]["xn_b"] for i in range(NC)]
    pmax8 = np.concatenate([r[i]["pmax"][0:1, 0] for i in range(NC)])[None, :]

    W_in = np.asarray(inputs["L0_attn_W_in"], np.float32)
    r = run_launch("mm_xz", build_mm,
                   [{"lhsT": _cc(W_in[:, :2048]), "rhs": store["xn_b"][i]} for i in range(NC)],
                   K=1024, M=2048, T=TH, rhs_bf16=True, out_f32=True, out_bf16=False)
    xc = [r[i]["out_f"] for i in range(NC)]
    r = run_launch("mm_xz_z", build_mm,
                   [{"lhsT": _cc(W_in[:, 2048:]), "rhs": store["xn_b"][i]} for i in range(NC)],
                   K=1024, M=2048, T=TH, rhs_bf16=True, out_f32=True, epilogue=epi_silu)
    sz = [_cc(r[i]["out_f"][:, 3:3 + TPC]) for i in range(NC)]
    store["sz0"] = sz

    cw = np.asarray(inputs["L0_attn_conv_w"], np.float32)[:, 0, :]
    r = run_launch("conv0", build_conv0,
                   [{"xc": xc[i], "cw": cw} for i in range(NC)], T=TPC, TH=TH)
    u_f = [r[i]["u_f"] for i in range(NC)]
    u_b = [r[i]["u_b"] for i in range(NC)]
    store["u0"] = u_f

    W_x = np.asarray(inputs["L0_attn_W_x"], np.float32)
    r = run_launch("mm_xp", build_mm,
                   [{"lhsT": W_x, "rhs": u_b[i]} for i in range(NC)],
                   K=2048, M=32, T=TPC, rhs_bf16=True, out_f32=True)
    xp = [r[i]["out_f"] for i in range(NC)]

    W_dt = np.asarray(inputs["L0_attn_W_dt"], np.float32)
    b_dt = np.asarray(inputs["L0_attn_b_dt"], np.float32).reshape(-1, 1)
    r = run_launch("mm_dtA", build_mm,
                   [{"lhsT": W_dt, "rhs": _cc(xp[i][:16]), "bias": b_dt} for i in range(NC)],
                   K=16, M=2048, T=TPC, out_f32=True, bias=True, epilogue="dt_a")
    a0 = [r[i]["out_f"] for i in range(NC)]

    W_Bp = np.asarray(inputs["L0_attn_W_Bp"], np.float32)
    r = run_launch("mm_Bp", build_mm,
                   [{"lhsT": W_Bp, "rhs": _cc(xp[i][16:])} for i in range(NC)],
                   K=16, M=2048, T=TPC, out_f32=True)
    Bp = [r[i]["out_f"] for i in range(NC)]
    r = run_launch("mul0", build_mul,
                   [{"a": u_f[i], "b": Bp[i]} for i in range(NC)], R=2048, T=TPC)
    ub = [r[i]["o"] for i in range(NC)]

    sc_in = []
    for b, h in cores:
        lo, hi = h * 1024, (h + 1) * 1024
        a_full = np.concatenate([a0[2 * b][lo:hi], a0[2 * b + 1][lo:hi]], axis=1)
        ub_full = np.concatenate([ub[2 * b][lo:hi], ub[2 * b + 1][lo:hi]], axis=1)
        sc_in.append({"a": _cc(a_full), "ub": _cc(ub_full)})
    r = run_launch("scan0", build_scan0, sc_in, R=1024, T=L)
    ys_sc = [r[i]["ys"] for i in range(NC)]

    ys_tok = []
    for b, h in cores:
        t0, t1 = h * TPC, (h + 1) * TPC
        ys_tok.append(_cc(np.concatenate([ys_sc[2 * b][:, t0:t1], ys_sc[2 * b + 1][:, t0:t1]], axis=0)))

    Dp = np.asarray(inputs["L0_attn_D_param"], np.float32).reshape(-1, 1)
    r = run_launch("l0post", build_l0post,
                   [{"ys": ys_tok[i], "u": u_f[i], "sz": sz[i], "Dp": Dp} for i in range(NC)],
                   R=2048, T=TPC)
    y_b = [r[i]["y_b"] for i in range(NC)]
    W_out = np.asarray(inputs["L0_attn_W_out"], np.float32)
    r = run_launch("mm_attn0", build_mm,
                   [{"lhsT": W_out, "rhs": y_b[i]} for i in range(NC)],
                   K=2048, M=1024, T=TPC, rhs_bf16=True, out_f32=True)
    attn0 = [r[i]["out_f"] for i in range(NC)]
    store["attn0"] = attn0

    wg = np.asarray(inputs["L0_ffn_w_gate"], np.float32)
    wu = np.asarray(inputs["L0_ffn_w_up"], np.float32)
    wd = np.asarray(inputs["L0_ffn_w_down"], np.float32)
    r = run_launch("dq_3072_1024", build_dequant,
                   [{"w": wu, "ident": ident} for _ in range(NC)], O=3072, I=1024)
    wuT = r[0]["w_dq_T"]
    r = run_launch("dq_3072_1024", build_dequant,
                   [{"w": wg, "ident": ident} for _ in range(NC)], O=3072, I=1024)
    wgT = r[0]["w_dq_T"]
    r = run_launch("dq_1024_3072", build_dequant,
                   [{"w": wd, "ident": ident} for _ in range(NC)], O=1024, I=3072)
    wdT = r[0]["w_dq_T"]

    xn_tok = [_cc(store["xn_f"][i][:, 3:3 + TPC]) for i in range(NC)]
    store["xn0_tok"] = xn_tok
    r = run_launch("qdq_x", build_qdq_x,
                   [{"x": xn_tok[i], "pmax8": pmax8} for i in range(NC)], R=1024, T=TPC)
    xq_b = [r[i]["xq_b"] for i in range(NC)]
    store["xq_b"] = xq_b

    r = run_launch("mm_upgate", build_mm,
                   [{"lhsT": wuT, "rhs": xq_b[i]} for i in range(NC)],
                   K=1024, M=3072, T=TPC, rhs_bf16=True, lhsT_bf16=True, out_f32=True)
    up = [r[i]["out_f"] for i in range(NC)]
    r = run_launch("mm_upgate", build_mm,
                   [{"lhsT": wgT, "rhs": xq_b[i]} for i in range(NC)],
                   K=1024, M=3072, T=TPC, rhs_bf16=True, lhsT_bf16=True, out_f32=True)
    gate = [r[i]["out_f"] for i in range(NC)]
    store["up"] = up
    store["gate"] = gate

    r = run_launch("pm_R3072", build_pmax, [{"x": up[i]} for i in range(NC)], R=3072, T=TPC)
    pm_up8 = np.concatenate([r[i]["pmax"][0:1, 0] for i in range(NC)])[None, :]
    r = run_launch("pm_R3072", build_pmax, [{"x": gate[i]} for i in range(NC)], R=3072, T=TPC)
    pm_gate8 = np.concatenate([r[i]["pmax"][0:1, 0] for i in range(NC)])[None, :]

    r = run_launch("ffn_mid", build_ffn_mid,
                   [{"up": up[i], "gate": gate[i], "pm_up8": pm_up8, "pm_gate8": pm_gate8}
                    for i in range(NC)], R=3072, T=TPC)
    upq = [r[i]["upq"] for i in range(NC)]
    g2p = [r[i]["g2p"] for i in range(NC)]
    store["upq"] = upq
    store["g2p"] = g2p
    pm_g28 = np.concatenate([r[i]["pm_g2"][0:1, 0] for i in range(NC)])[None, :]

    r = run_launch("ffn_o", build_ffn_o,
                   [{"g2p": g2p[i], "upq": upq[i], "pm_g28": pm_g28} for i in range(NC)],
                   R=3072, T=TPC)
    o_pre = [r[i]["o_pre"] for i in range(NC)]
    pm_o8 = np.concatenate([r[i]["pm_o"][0:1, 0] for i in range(NC)])[None, :]

    r = run_launch("qdq_o", build_qdq_x,
                   [{"x": o_pre[i], "pmax8": pm_o8} for i in range(NC)], R=3072, T=TPC)
    oq_b = [r[i]["xq_b"] for i in range(NC)]

    r = run_launch("mm_down", build_mm,
                   [{"lhsT": wdT, "rhs": oq_b[i], "res0": attn0[i], "res1": store["x_fm"][i]}
                    for i in range(NC)],
                   K=3072, M=1024, T=TPC, rhs_bf16=True, lhsT_bf16=True, out_f32=True, nres=2)
    store["x1"] = [r[i]["out_f"] for i in range(NC)]
    return store


def _rotary_tables():
    n, DH, WIN = 4096, 64, 256
    half = np.arange(0, DH, 2, dtype=np.float64)
    inv_freq = 1.0 / 10000.0 ** (half / DH)
    t = np.arange(n, dtype=np.float64)
    freqs = np.concatenate([t[:, None] * inv_freq[None, :]] * 2, axis=1)
    base = (half + 0.4 * DH) / (1.4 * DH)
    power = (t - n // 2) / (WIN // 2)
    scale = np.concatenate([base[None, :] ** power[:, None]] * 2, axis=1)
    return freqs, scale


def _table_tile(vals, gpos):
    v = vals[np.clip(gpos, 0, 4095), :]
    out = np.concatenate([v.T, v.T], axis=0)
    return np.ascontiguousarray(out.astype(np.float32))


def run_l1(inputs, store):
    cores = [(i // 2, i % 2) for i in range(NC)]
    ones128 = np.ones((128, 128), np.float32)
    x1 = store["x1"]

    TH = 2560
    xs = []
    for i, (b, h) in enumerate(cores):
        xa = np.zeros((DM, TH), np.float32)
        if h == 1:
            xa[:, 0:3] = x1[2 * b][:, -3:]
        xa[:, 3:3 + TPC] = x1[i]
        if h == 0:
            xa[:, 3 + TPC:6 + TPC] = x1[2 * b + 1][:, 0:3]
        xs.append(_cc(xa))
    r = run_launch("ln1", build_ln, [{"x": xs[i], "ones128": ones128} for i in range(NC)],
                   T=TH, want_pmax=False)
    xn1_f = [r[i]["xn_f"] for i in range(NC)]
    xn1_b = [r[i]["xn_b"] for i in range(NC)]
    store["xn1_tok"] = [_cc(xn1_f[i][:, 3:3 + TPC]) for i in range(NC)]

    W_inp = _cc(np.asarray(inputs["L1_attn_in_proj_w"], np.float32).T)
    r = run_launch("mm_inproj", build_mm,
                   [{"lhsT": W_inp, "rhs": xn1_b[i]} for i in range(NC)],
                   K=1024, M=4096, T=TH, rhs_bf16=True, out_f32=True)
    xz1 = [r[i]["out_f"] for i in range(NC)]
    xc1 = [_cc(xz1[i][:2048]) for i in range(NC)]
    z1 = [_cc(xz1[i][2048:, 3:3 + TPC]) for i in range(NC)]

    cw_f = _cc(np.asarray(inputs["L1_attn_conv_w"], np.float32)[:, 0, :])
    cw_b = _cc(np.asarray(inputs["L1_attn_conv_w_b"], np.float32)[:, 0, ::-1])
    cb_f = np.asarray(inputs["L1_attn_conv_b"], np.float32)
    cb_b = np.asarray(inputs["L1_attn_conv_b_b"], np.float32)
    r = run_launch("conv1f", build_conv1,
                   [{"xc": xc1[i], "cw": cw_f, "cb": _cc(cb_f.reshape(-1, 1))} for i in range(NC)],
                   T=TPC, TH=TH, off=0)
    u1f_f = [r[i]["u_f"] for i in range(NC)]
    u1f_b = [r[i]["u_b"] for i in range(NC)]
    r = run_launch("conv1b", build_conv1,
                   [{"xc": xc1[i], "cw": cw_b, "cb": _cc(cb_b.reshape(-1, 1))} for i in range(NC)],
                   T=TPC, TH=TH, off=3)
    u1b_f = [r[i]["u_f"] for i in range(NC)]
    u1b_b = [r[i]["u_b"] for i in range(NC)]
    store["u1f"] = u1f_f
    store["u1b"] = u1b_f

    xpw_f = _cc(np.asarray(inputs["L1_attn_x_proj_w"], np.float32).T)
    xpw_b = _cc(np.asarray(inputs["L1_attn_x_proj_w_b"], np.float32).T)
    r = run_launch("mm_xdbl", build_mm,
                   [{"lhsT": xpw_f, "rhs": u1f_b[i]} for i in range(NC)],
                   K=2048, M=96, T=TPC, rhs_bf16=True, out_f32=True)
    xdbl_f = [r[i]["out_f"] for i in range(NC)]
    r = run_launch("mm_xdbl", build_mm,
                   [{"lhsT": xpw_b, "rhs": u1b_b[i]} for i in range(NC)],
                   K=2048, M=96, T=TPC, rhs_bf16=True, out_f32=True)
    xdbl_b = [r[i]["out_f"] for i in range(NC)]
    store["xdbl_f"] = xdbl_f
    store["xdbl_b"] = xdbl_b

    wdt_f = np.asarray(inputs["L1_attn_dt_proj_w"], np.float32).T
    wdt_b = np.asarray(inputs["L1_attn_dt_proj_w_b"], np.float32).T
    bdt_f = np.asarray(inputs["L1_attn_dt_proj_b"], np.float32)
    bdt_b = np.asarray(inputs["L1_attn_dt_proj_b_b"], np.float32)
    Alog_f = np.asarray(inputs["L1_attn_A_log"], np.float32)
    Alog_b = np.asarray(inputs["L1_attn_A_b_log"], np.float32)
    Dp_f = np.asarray(inputs["L1_attn_D"], np.float32)
    Dp_b = np.asarray(inputs["L1_attn_D_b"], np.float32)

    def stage_vec(v, rows):
        return _cc(v[rows].reshape(8, 128).T)

    def stage_alog(A, rows):
        a = A[rows].reshape(8, 128, 16)
        return _cc(np.transpose(a, (1, 0, 2)).reshape(128, 128))

    sc_in = []
    for b, dh in cores:
        rows = slice(dh * 1024, (dh + 1) * 1024)
        tc0, tc1 = 2 * b, 2 * b + 1
        d = {}
        d["u_f"] = _cc(np.concatenate([u1f_f[tc0][rows], u1f_f[tc1][rows]], axis=1))
        d["u_b"] = _cc(np.concatenate([u1b_f[tc0][rows], u1b_f[tc1][rows]], axis=1))
        d["dtr_f"] = _cc(np.concatenate([xdbl_f[tc0][:64], xdbl_f[tc1][:64]], axis=1))
        d["dtr_b"] = _cc(np.concatenate([xdbl_b[tc0][:64], xdbl_b[tc1][:64]], axis=1))
        bc = np.zeros((64, L), np.float32)
        bc[0:16] = np.concatenate([xdbl_f[tc0][64:80], xdbl_f[tc1][64:80]], axis=1)
        bc[16:32] = np.concatenate([xdbl_f[tc0][80:96], xdbl_f[tc1][80:96]], axis=1)
        bc[32:48] = np.concatenate([xdbl_b[tc0][64:80], xdbl_b[tc1][64:80]], axis=1)
        bc[48:64] = np.concatenate([xdbl_b[tc0][80:96], xdbl_b[tc1][80:96]], axis=1)
        d["bc_all"] = _cc(bc)
        d["wdt_f"] = _cc(wdt_f[:, rows])
        d["wdt_b"] = _cc(wdt_b[:, rows])
        d["bdt_f"] = stage_vec(bdt_f, rows)
        d["bdt_b"] = stage_vec(bdt_b, rows)
        d["Alog_f"] = stage_alog(Alog_f, rows)
        d["Alog_b"] = stage_alog(Alog_b, rows)
        d["Dp_f"] = stage_vec(Dp_f, rows)
        d["Dp_b"] = stage_vec(Dp_b, rows)
        d["z"] = _cc(np.concatenate([z1[tc0][rows], z1[tc1][rows]], axis=1))
        sc_in.append(d)
    r = run_launch("scan1", build_scan1, sc_in, R=1024, T=L)
    ysum = [r[i]["y_sum"] for i in range(NC)]
    store["ysum"] = ysum

    y_tok = []
    for b, h in cores:
        t0, t1 = h * TPC, (h + 1) * TPC
        y_tok.append(_cc(np.concatenate([ysum[2 * b][:, t0:t1], ysum[2 * b + 1][:, t0:t1]], axis=0)))
    Wop = _cc(np.asarray(inputs["L1_attn_out_proj_w"], np.float32).T)
    r = run_launch("mm_outproj", build_mm,
                   [{"lhsT": Wop, "rhs": y_tok[i]} for i in range(NC)],
                   K=2048, M=1024, T=TPC, out_f32=True)
    attn1 = [r[i]["out_f"] for i in range(NC)]
    store["attn1"] = attn1

    freqs, xsc = _rotary_tables()
    cosv, sinv = np.cos(freqs), np.sin(freqs)
    Wqkv = np.asarray(inputs["L1_ffn_Wqkv"], np.float32)
    q_scale = np.asarray(inputs["L1_ffn_q_scale"], np.float32)
    k_scale = np.asarray(inputs["L1_ffn_k_scale"], np.float32)
    Wo = np.asarray(inputs["L1_ffn_Wo"], np.float32)
    ln_g = np.asarray(inputs["L1_ffn_ln_g"], np.float32)
    ln_b = np.asarray(inputs["L1_ffn_ln_b"], np.float32)
    W1 = np.asarray(inputs["L1_ffn_W1"], np.float32)
    b1 = np.asarray(inputs["L1_ffn_b1"], np.float32)
    W2 = np.asarray(inputs["L1_ffn_W2"], np.float32)
    b2 = np.asarray(inputs["L1_ffn_b2"], np.float32)

    ind2 = np.zeros((128, 2), np.float32)
    ind2[0:64, 0] = 1.0
    ind2[64:128, 1] = 1.0
    rotm = np.zeros((128, 128), np.float32)
    for m in range(128):
        if m % 64 < 32:
            rotm[m + 32, m] = -1.0
        else:
            rotm[m - 32, m] = 1.0

    percore = []
    for b, h in cores:
        g_my = h * TPC + np.arange(TPC)
        g_kv = h * TPC - 256 + np.arange(2304)
        pc = {
            "cosq": _table_tile(cosv * xsc, g_my), "sinq": _table_tile(sinv * xsc, g_my),
            "cosk": _table_tile(cosv / xsc, g_kv), "sink": _table_tile(sinv / xsc, g_kv),
        }
        masks = np.zeros((8 * 2 * 128, 512), np.float32)
        for w in range(8):
            for ih in range(2):
                i_loc = ih * 128 + np.arange(128)[:, None] + (0 if True else 0)
                i_in_win = ih * 128 + np.arange(128)[:, None]
                j = np.arange(512)[None, :]
                ok = (j >= i_in_win) & (j <= i_in_win + 256)
                if h == 0 and w == 0:
                    ok &= (j >= 256)
                m = np.where(ok, 0.0, NEG_HUGE).astype(np.float32)
                masks[(w * 2 + ih) * 128:(w * 2 + ih) * 128 + 128] = m
        pc["masks"] = _cc(masks)
        percore.append(pc)

    x_cur = store["xn1_tok"]
    for li in range(6):
        wqk = _cc(Wqkv[li][:, :2048])
        wv = _cc(Wqkv[li][:, 2048:])
        sclq = np.zeros((2, 128), np.float32)
        sclk = np.zeros((2, 128), np.float32)
        for j in range(2):
            sclq[j, j * 64:(j + 1) * 64] = q_scale[li]
            sclk[j, j * 64:(j + 1) * 64] = k_scale[li]
        TH1 = 2560
        qkv_in = []
        for i, (b, h) in enumerate(cores):
            xa = np.zeros((DM, TH1), np.float32)
            if h == 1:
                xa[:, 0:257] = x_cur[2 * b][:, -257:]
            xa[:, 257:257 + TPC] = x_cur[i]
            qkv_in.append({"x": _cc(xa), "wqk": wqk, "wv": wv,
                           "cosq": percore[i]["cosq"], "sinq": percore[i]["sinq"],
                           "cosk": percore[i]["cosk"], "sink": percore[i]["sink"],
                           "sclq": _cc(sclq), "sclk": _cc(sclk), "ind2": ind2, "rotm": rotm})
        r = run_launch("f1qkv", build_f1qkv, qkv_in)
        q2b = [r[i]["q2b"] for i in range(NC)]
        k2b = [r[i]["k2b"] for i in range(NC)]
        v_tok = [r[i]["v_tok"] for i in range(NC)]

        r = run_launch("f1attn", build_f1attn,
                       [{"q2b": q2b[i], "k2b": k2b[i], "v_tok": v_tok[i],
                         "masks": percore[i]["masks"], "wo": _cc(Wo[li]), "xres": x_cur[i]}
                        for i in range(NC)])
        x_mid = [r[i]["xo"] for i in range(NC)]

        f1h_in = []
        for i, (b, h) in enumerate(cores):
            xa = np.zeros((DM, TPC + 1), np.float32)
            if h == 1:
                xa[:, 0:1] = x_mid[2 * b][:, -1:]
            xa[:, 1:] = x_mid[i]
            f1h_in.append({"x": _cc(xa),
                           "w1a": _cc(W1[li][:, :2730]), "w1g": _cc(W1[li][:, 2730:]),
                           "b1a": _cc(b1[li][:2730].reshape(-1, 1)), "b1g": _cc(b1[li][2730:].reshape(-1, 1)),
                           "lng": _cc(ln_g[li].reshape(8, 128).T), "lnb": _cc(ln_b[li].reshape(8, 128).T),
                           "ones128": ones128})
        r = run_launch("f1h", build_f1h, f1h_in)
        hgate = [r[i]["hgate"] for i in range(NC)]

        if li < 5:
            r = run_launch("mm_w2", build_mm,
                           [{"lhsT": _cc(W2[li]), "rhs": hgate[i], "bias": _cc(b2[li].reshape(-1, 1)),
                             "res0": x_mid[i]} for i in range(NC)],
                           K=2730, M=1024, T=TPC, rhs_bf16=True, bias=True, nres=1)
        else:
            r = run_launch("mm_w2_last", build_mm,
                           [{"lhsT": _cc(W2[li]), "rhs": hgate[i], "bias": _cc(b2[li].reshape(-1, 1)),
                             "res0": x_mid[i], "res1": x1[i], "res2": attn1[i]} for i in range(NC)],
                           K=2730, M=1024, T=TPC, rhs_bf16=True, bias=True, nres=3)
        x_cur = [r[i]["out_f"] for i in range(NC)]
        store[f"stk{li}"] = x_cur
        store[f"xmid{li}"] = x_mid
    store["x2"] = x_cur
    return store


def build_pmax(p, *, R, T):
    nc = p.nc
    TT = 512
    x = p.din("x", (R, T))
    pmax = p.dout("pmax", (1, 8))
    pb = p.dint("pb", (128, 1))
    ld, cv, st = (p.sem(s) for s in ("ld", "cv", "st"))
    xt = p.sbuf("xt", [128, TT], F32)
    mx = p.sbuf("mx", [128, 1], F32)
    mx2 = p.sbuf("mx2", [128, 1], F32)
    mxr = p.sbuf("mxr", [1, 128], F32)
    out8 = p.sbuf("out8", [1, 8], F32)
    first = True
    for ki, (k0, nk) in enumerate(chunks(R, 128)):
        for ti, (t0, tt) in enumerate(chunks(T, TT)):
            nc.sync.wait_ge(cv.h, cv.n)
            ld.inc(nc.sync.dma_start(out=xt[0:nk, 0:tt], in_=x[k0:k0 + nk, t0:t0 + tt]), 16)
            nc.vector.wait_ge(ld.h, ld.n)
            if first:
                p.V.reduce_max(mx[0:nk, 0:1], xt[0:nk, 0:tt], axis=AX, apply_absolute_value=True)
                first = False
            else:
                p.V.reduce_max(mx2[0:nk, 0:1], xt[0:nk, 0:tt], axis=AX, apply_absolute_value=True)
                p.V.tensor_tensor(mx[0:nk, 0:1], mx[0:nk, 0:1], mx2[0:nk, 0:1], ALU.max)
    nc.sync.wait_ge(cv.h, cv.n)
    st.inc(nc.sync.dma_start(out=pb.ap(), in_=mx[:]), 16)
    nc.sync.wait_ge(st.h, st.n)
    ld.inc(nc.sync.dma_start(out=mxr[0:1, :], in_=pb.ap().rearrange("p x -> x p")), 16)
    nc.vector.wait_ge(ld.h, ld.n)
    p.V.memset(out8[0:1, :], 0.0)
    p.V.reduce_max(out8[0:1, 0:1], mxr[0:1, :], axis=AX)
    nc.sync.wait_ge(cv.h, cv.n)
    st.inc(nc.sync.dma_start(out=pmax, in_=out8[0:1, :]), 16)
    nc.sync.wait_ge(st.h, st.n)


MEGA = False


def kernel_mega(inputs):
    p = Prog("mega", mega=True)
    stage = {}
    cores = [(i // 2, i % 2) for i in range(NC)]

    def run_phase(prefix, builder, feeds, vals, **kw):
        mark = len(p.ext_list)
        p.begin_phase(prefix, {k: (v if isinstance(v, bass.AP) else v.ap())
                               for k, v in (feeds or {}).items()})
        builder(p, **kw)
        p.end_phase()
        for full, local in p.ext_list[mark:]:
            assert local in vals, (prefix, local, list(vals))
            stage[full] = vals[local]

    def mk(name, shape, dt=F32):
        return p.nc.dram_tensor(name, list(shape), dt)

    def ag_pair(tag, src, dst):
        p.begin_phase(tag)
        emit_collective(p, "AllGather", ALU.bypass, PAIRS, src, dst)
        p.end_phase()

    def ar_max(tag, src, dst):
        p.begin_phase(tag)
        emit_collective(p, "AllReduce", ALU.max, ALL8, src, dst)
        p.end_phase()

    def copy_phase(tag, dst_ap, src_ap):
        p.begin_phase(tag)
        emit_dram_copy(p, dst_ap, src_ap)
        p.end_phase()

    x = np.asarray(inputs["L0_attn_x"])
    ones128 = np.ones((128, 128), np.float32)
    ident = np.eye(128, dtype=np.float32)
    x_pads = []
    for b, h in cores:
        t0 = h * TPC
        xa = np.zeros((DM, 2560), np.float32)
        if h == 1:
            xa[:, 0:3] = x[b, t0 - 3:t0, :].T
        xa[:, 3:3 + TPC] = x[b, t0:t0 + TPC, :].T
        x_pads.append(_cc(xa))
    mhl = [np.full((128, 1), float(h), np.float32) for b, h in cores]
    mhr = [np.full((128, 1), float(1 - h), np.float32) for b, h in cores]
    sc_idx = []
    ys_idx = []
    for b, h in cores:
        idx = np.zeros((128, 16), np.int32)
        for ki in range(8):
            idx[:, ki] = h * 1024 + ki * 128 + np.arange(128)
            idx[:, 8 + ki] = 2048 + h * 1024 + ki * 128 + np.arange(128)
        sc_idx.append(_cc(idx))
        yi = np.zeros((128, 16), np.int32)
        for ki in range(16):
            d = ki * 128 + np.arange(128)
            yi[:, ki] = (d // 1024) * 2048 + (d % 1024) * 2 + h
        ys_idx.append(_cc(yi))

    W_in = np.asarray(inputs["L0_attn_W_in"], np.float32)
    cw0 = _cc(np.asarray(inputs["L0_attn_conv_w"], np.float32)[:, 0, :])
    W_x = np.asarray(inputs["L0_attn_W_x"], np.float32)
    W_dt = np.asarray(inputs["L0_attn_W_dt"], np.float32)
    b_dt = _cc(np.asarray(inputs["L0_attn_b_dt"], np.float32).reshape(-1, 1))
    W_Bp = np.asarray(inputs["L0_attn_W_Bp"], np.float32)
    Dp0 = _cc(np.asarray(inputs["L0_attn_D_param"], np.float32).reshape(-1, 1))
    W_out = np.asarray(inputs["L0_attn_W_out"], np.float32)
    wu = np.asarray(inputs["L0_ffn_w_up"], np.float32)
    wg = np.asarray(inputs["L0_ffn_w_gate"], np.float32)
    wd = np.asarray(inputs["L0_ffn_w_down"], np.float32)

    with p.stack:
        with p.nc.Block():
            x_in = mk("x_in", (DM, 2560))
            p.ext_list.append(("x_in", "x_in"))
    p = Prog("mega", mega=True)
    stage = {}

    with p.stack:
        with p.nc.Block():
            nc = p.nc
            x_in = nc.dram_tensor("x_in", [DM, 2560], F32, kind="ExternalInput")
            stage["x_in"] = x_pads
            ones_e = nc.dram_tensor("ones_e", [128, 128], F32, kind="ExternalInput")
            stage["ones_e"] = ones128
            out_e = nc.dram_tensor("out_e", [DM, TPC], F32, kind="ExternalOutput")

            def run_phase2(prefix, builder, feeds, vals, **kw):
                mark = len(p.ext_list)
                p.begin_phase(prefix, {k: (v if isinstance(v, bass.AP) else v.ap())
                                       for k, v in (feeds or {}).items()})
                builder(p, **kw)
                p.end_phase()
                for full, local in p.ext_list[mark:]:
                    assert local in vals, (prefix, local, sorted(vals))
                    stage[full] = vals[local]

            def mk2(name, shape, dt=F32):
                return p.nc.dram_tensor(name, list(shape), dt)

            xn0f = mk2("xn0f", (DM, 2560)); xn0b = mk2("xn0b", (DM, 2560), BF16)
            pm0 = mk2("pm0", (1, 8)); pm0g = mk2("pm0g", (1, 8))
            run_phase2("ln0", build_ln,
                       {"x": x_in, "ones128": ones_e, "xn_f": xn0f, "xn_b": xn0b, "pmax": pm0},
                       {}, T=2560, want_pmax=True)
            ar_max("ar0", pm0, pm0g)

            xc0 = mk2("xc0", (2048, 2560))
            run_phase2("mxz", build_mm, {"rhs": xn0b, "out_f": xc0}, {"lhsT": _cc(W_in[:, :2048])},
                       K=1024, M=2048, T=2560, rhs_bf16=True)
            sz0 = mk2("sz0", (2048, 2560))
            run_phase2("mxzz", build_mm, {"rhs": xn0b, "out_f": sz0}, {"lhsT": _cc(W_in[:, 2048:])},
                       K=1024, M=2048, T=2560, rhs_bf16=True, epilogue=epi_silu)
            u0f = mk2("u0f", (2048, TPC)); u0b = mk2("u0b", (2048, TPC), BF16)
            run_phase2("cv0", build_conv0, {"xc": xc0, "u_f": u0f, "u_b": u0b}, {"cw": cw0}, T=TPC, TH=2560)
            xp0 = mk2("xp0", (32, TPC))
            run_phase2("mxp", build_mm, {"rhs": u0b, "out_f": xp0}, {"lhsT": W_x},
                       K=2048, M=32, T=TPC, rhs_bf16=True)
            a0 = mk2("a0", (2048, TPC))
            run_phase2("mdt", build_mm, {"rhs": xp0.ap()[0:16, :], "out_f": a0},
                       {"lhsT": W_dt, "bias": b_dt},
                       K=16, M=2048, T=TPC, bias=True, epilogue="dt_a")
            Bp0 = mk2("Bp0", (2048, TPC))
            run_phase2("mBp", build_mm, {"rhs": xp0.ap()[16:32, :], "out_f": Bp0}, {"lhsT": W_Bp},
                       K=16, M=2048, T=TPC)
            ub0 = mk2("ub0", (2048, TPC))
            run_phase2("mu0", build_mul, {"a": u0f, "b": Bp0, "o": ub0}, {}, R=2048, T=TPC)

            aga0 = mk2("aga0", (4096, TPC)); agub0 = mk2("agub0", (4096, TPC))
            ag_pair("aga", a0, aga0)
            ag_pair("agb", ub0, agub0)
            ys0w = mk2("ys0w", (2048, TPC))
            run_phase2("sc0", build_scan0, {"ag_a": aga0, "ag_ub": agub0, "ys": ys0w},
                       {"sc0_idx": sc_idx}, R=1024, T=L, ag=True)
            agys0 = mk2("agys0", (4096, TPC))
            ag_pair("agy0", ys0w, agys0)
            ysr0 = mk2("ysr0", (2048, TPC))
            run_phase2("gy0", build_gather_rows, {"dst": None} if False else {},
                       {"ys_idx": ys_idx}, src=agys0.ap(), idx_name="ys_idx", dst=ysr0.ap(),
                       R=2048, C=TPC)
            y0b = mk2("y0b", (2048, TPC), BF16)
            run_phase2("l0p", build_l0post,
                       {"ys": ysr0, "u": u0f, "sz": sz0.ap()[:, 3:3 + TPC], "y_b": y0b},
                       {"Dp": Dp0}, R=2048, T=TPC)
            attn0 = mk2("attn0", (DM, TPC))
            run_phase2("ma0", build_mm, {"rhs": y0b, "out_f": attn0}, {"lhsT": W_out},
                       K=2048, M=1024, T=TPC, rhs_bf16=True)

            wuT = mk2("wuT", (1024, 3072), BF16)
            run_phase2("dqu", build_dequant, {"w_dq_T": wuT}, {"w": wu, "ident": ident}, O=3072, I=1024)
            wgT = mk2("wgT", (1024, 3072), BF16)
            run_phase2("dqg", build_dequant, {"w_dq_T": wgT}, {"w": wg, "ident": ident}, O=3072, I=1024)
            wdT = mk2("wdT", (3072, 1024), BF16)
            run_phase2("dqd", build_dequant, {"w_dq_T": wdT}, {"w": wd, "ident": ident}, O=1024, I=3072)

            xq0b = mk2("xq0b", (DM, TPC), BF16)
            run_phase2("qdx", build_qdq_x,
                       {"x": xn0f.ap()[:, 3:3 + TPC], "pmax8": pm0g, "xq_b": xq0b}, {},
                       R=1024, T=TPC)
            up0 = mk2("up0", (3072, TPC)); gate0 = mk2("gate0", (3072, TPC))
            run_phase2("mup", build_mm, {"rhs": xq0b, "lhsT": wuT, "out_f": up0}, {},
                       K=1024, M=3072, T=TPC, rhs_bf16=True, lhsT_bf16=True)
            run_phase2("mgt", build_mm, {"rhs": xq0b, "lhsT": wgT, "out_f": gate0}, {},
                       K=1024, M=3072, T=TPC, rhs_bf16=True, lhsT_bf16=True)
            pmu = mk2("pmu", (1, 8)); pmug = mk2("pmug", (1, 8))
            run_phase2("pmu", build_pmax, {"x": up0, "pmax": pmu}, {}, R=3072, T=TPC)
            ar_max("aru", pmu, pmug)
            pmg = mk2("pmg", (1, 8)); pmgg = mk2("pmgg", (1, 8))
            run_phase2("pmg", build_pmax, {"x": gate0, "pmax": pmg}, {}, R=3072, T=TPC)
            ar_max("arg", pmg, pmgg)
            upq = mk2("upq", (3072, TPC)); g2p = mk2("g2p", (3072, TPC))
            pmg2 = mk2("pmg2", (1, 8)); pmg2g = mk2("pmg2g", (1, 8))
            run_phase2("fmd", build_ffn_mid,
                       {"up": up0, "gate": gate0, "pm_up8": pmug, "pm_gate8": pmgg,
                        "upq": upq, "g2p": g2p, "pm_g2": pmg2}, {}, R=3072, T=TPC)
            ar_max("ar2", pmg2, pmg2g)
            o_pre = mk2("o_pre", (3072, TPC)); pmo = mk2("pmo", (1, 8)); pmog = mk2("pmog", (1, 8))
            run_phase2("fo", build_ffn_o,
                       {"g2p": g2p, "upq": upq, "pm_g28": pmg2g, "o_pre": o_pre, "pm_o": pmo},
                       {}, R=3072, T=TPC)
            ar_max("aro", pmo, pmog)
            oq0b = mk2("oq0b", (3072, TPC), BF16)
            run_phase2("qdo", build_qdq_x, {"x": o_pre, "pmax8": pmog, "xq_b": oq0b}, {},
                       R=3072, T=TPC)
            x1 = mk2("x1", (DM, TPC))
            run_phase2("mdn", build_mm,
                       {"rhs": oq0b, "lhsT": wdT, "res0": attn0, "res1": x_in.ap()[:, 3:3 + TPC],
                        "out_f": x1}, {},
                       K=3072, M=1024, T=TPC, rhs_bf16=True, lhsT_bf16=True, nres=2)

            x1b = mk2("x1b", (DM, 6)); agx1b = mk2("agx1b", (2 * DM, 6))
            copy_phase("cx1a", x1b.ap()[:, 0:3], x1.ap()[:, 0:3])
            copy_phase("cx1b", x1b.ap()[:, 3:6], x1.ap()[:, TPC - 3:TPC])
            ag_pair("agxb", x1b, agx1b)
            x1p = mk2("x1p", (DM, 2560))
            run_phase2("zf1", build_zero_fill, {}, {}, dst_ap=x1p.ap()[:, 2054:2560], rows=DM, cols=506)
            copy_phase("cc1", x1p.ap()[:, 3:3 + TPC], x1.ap())
            run_phase2("mh1", build_masked_halo,
                       {"dst": None} if False else {}, {"mhl": mhl},
                       agsrc=agx1b.ap(), dst=x1p.ap()[:, 0:3], rows=DM, cols=3,
                       mask_name="mhl", src_r0=0, src_c0=3)
            run_phase2("mh2", build_masked_halo, {}, {"mhr": mhr},
                       agsrc=agx1b.ap(), dst=x1p.ap()[:, 2051:2054], rows=DM, cols=3,
                       mask_name="mhr", src_r0=DM, src_c0=0)
            xn1f = mk2("xn1f", (DM, 2560)); xn1b = mk2("xn1b", (DM, 2560), BF16)
            run_phase2("ln1", build_ln, {"x": x1p, "ones128": ones_e, "xn_f": xn1f, "xn_b": xn1b},
                       {}, T=2560, want_pmax=False)
            xz1 = mk2("xz1", (4096, 2560))
            run_phase2("mip", build_mm, {"rhs": xn1b, "out_f": xz1},
                       {"lhsT": _cc(np.asarray(inputs["L1_attn_in_proj_w"], np.float32).T)},
                       K=1024, M=4096, T=2560, rhs_bf16=True)
            u1f = mk2("u1f", (2048, TPC)); u1fb = mk2("u1fb", (2048, TPC), BF16)
            u1b = mk2("u1b", (2048, TPC)); u1bb = mk2("u1bb", (2048, TPC), BF16)
            cw1f = _cc(np.asarray(inputs["L1_attn_conv_w"], np.float32)[:, 0, :])
            cw1b = _cc(np.asarray(inputs["L1_attn_conv_w_b"], np.float32)[:, 0, ::-1])
            run_phase2("c1f", build_conv1,
                       {"xc": xz1.ap()[0:2048, :], "u_f": u1f, "u_b": u1fb},
                       {"cw": cw1f, "cb": _cc(np.asarray(inputs["L1_attn_conv_b"], np.float32).reshape(-1, 1))},
                       T=TPC, TH=2560, off=0)
            run_phase2("c1b", build_conv1,
                       {"xc": xz1.ap()[0:2048, :], "u_f": u1b, "u_b": u1bb},
                       {"cw": cw1b, "cb": _cc(np.asarray(inputs["L1_attn_conv_b_b"], np.float32).reshape(-1, 1))},
                       T=TPC, TH=2560, off=3)
            xdf = mk2("xdf", (96, TPC)); xdb = mk2("xdb", (96, TPC))
            run_phase2("xdf", build_mm, {"rhs": u1fb, "out_f": xdf},
                       {"lhsT": _cc(np.asarray(inputs["L1_attn_x_proj_w"], np.float32).T)},
                       K=2048, M=96, T=TPC, rhs_bf16=True)
            run_phase2("xdb", build_mm, {"rhs": u1bb, "out_f": xdb},
                       {"lhsT": _cc(np.asarray(inputs["L1_attn_x_proj_w_b"], np.float32).T)},
                       K=2048, M=96, T=TPC, rhs_bf16=True)
            agu1f = mk2("agu1f", (4096, TPC)); agu1b = mk2("agu1b", (4096, TPC))
            ag_pair("aguf", u1f, agu1f)
            ag_pair("agub2", u1b, agu1b)
            zcon = mk2("zcon", (2048, TPC))
            copy_phase("czc", zcon.ap(), xz1.ap()[2048:4096, 3:3 + TPC])
            agz1 = mk2("agz1", (4096, TPC))
            ag_pair("agz", zcon, agz1)
            agxdf = mk2("agxdf", (192, TPC)); agxdb = mk2("agxdb", (192, TPC))
            ag_pair("agdf", xdf, agxdf)
            ag_pair("agdb", xdb, agxdb)

            wdt_f = np.asarray(inputs["L1_attn_dt_proj_w"], np.float32).T
            wdt_b = np.asarray(inputs["L1_attn_dt_proj_w_b"], np.float32).T
            bdt_f = np.asarray(inputs["L1_attn_dt_proj_b"], np.float32)
            bdt_b = np.asarray(inputs["L1_attn_dt_proj_b_b"], np.float32)
            Alog_f = np.asarray(inputs["L1_attn_A_log"], np.float32)
            Alog_b = np.asarray(inputs["L1_attn_A_b_log"], np.float32)
            Dp_f = np.asarray(inputs["L1_attn_D"], np.float32)
            Dp_b = np.asarray(inputs["L1_attn_D_b"], np.float32)

            def stage_vec(v):
                return [_cc(v[slice(h * 1024, (h + 1) * 1024)].reshape(8, 128).T) for b, h in cores]

            def stage_alog(A):
                out = []
                for b, h in cores:
                    a = A[h * 1024:(h + 1) * 1024].reshape(8, 128, 16)
                    out.append(_cc(np.transpose(a, (1, 0, 2)).reshape(128, 128)))
                return out

            ys1w = mk2("ys1w", (2048, TPC))
            run_phase2("sc1", build_scan1,
                       {"ag_uf": agu1f, "ag_ub": agu1b, "ag_xdbl_f": agxdf, "ag_xdbl_b": agxdb,
                        "ag_z": agz1, "y_sum": ys1w},
                       {"sc1_idx": sc_idx,
                        "wdt_f": [_cc(wdt_f[:, h * 1024:(h + 1) * 1024]) for b, h in cores],
                        "wdt_b": [_cc(wdt_b[:, h * 1024:(h + 1) * 1024]) for b, h in cores],
                        "bdt_f": stage_vec(bdt_f), "bdt_b": stage_vec(bdt_b),
                        "Alog_f": stage_alog(Alog_f), "Alog_b": stage_alog(Alog_b),
                        "Dp_f": stage_vec(Dp_f), "Dp_b": stage_vec(Dp_b)},
                       R=1024, T=L, ag=True)
            agys1 = mk2("agys1", (4096, TPC))
            ag_pair("agy1", ys1w, agys1)
            y1r = mk2("y1r", (2048, TPC))
            run_phase2("gy1", build_gather_rows, {}, {"ys1_idx": ys_idx},
                       src=agys1.ap(), idx_name="ys1_idx", dst=y1r.ap(), R=2048, C=TPC)
            attn1 = mk2("attn1", (DM, TPC))
            run_phase2("mop", build_mm, {"rhs": y1r, "out_f": attn1},
                       {"lhsT": _cc(np.asarray(inputs["L1_attn_out_proj_w"], np.float32).T)},
                       K=2048, M=1024, T=TPC)

            freqs, xsc = _rotary_tables()
            cosv, sinv = np.cos(freqs), np.sin(freqs)
            Wqkv = np.asarray(inputs["L1_ffn_Wqkv"], np.float32)
            q_scale = np.asarray(inputs["L1_ffn_q_scale"], np.float32)
            k_scale = np.asarray(inputs["L1_ffn_k_scale"], np.float32)
            Wo = np.asarray(inputs["L1_ffn_Wo"], np.float32)
            ln_g = np.asarray(inputs["L1_ffn_ln_g"], np.float32)
            ln_b = np.asarray(inputs["L1_ffn_ln_b"], np.float32)
            W1 = np.asarray(inputs["L1_ffn_W1"], np.float32)
            b1 = np.asarray(inputs["L1_ffn_b1"], np.float32)
            W2 = np.asarray(inputs["L1_ffn_W2"], np.float32)
            b2 = np.asarray(inputs["L1_ffn_b2"], np.float32)
            ind2 = np.zeros((128, 2), np.float32)
            ind2[0:64, 0] = 1.0
            ind2[64:128, 1] = 1.0
            rotm = np.zeros((128, 128), np.float32)
            for m in range(128):
                if m % 64 < 32:
                    rotm[m + 32, m] = -1.0
                else:
                    rotm[m - 32, m] = 1.0
            tabs = []
            masksl = []
            for b, h in cores:
                g_my = h * TPC + np.arange(TPC)
                g_kv = h * TPC - 256 + np.arange(2304)
                tabs.append({"cosq": _table_tile(cosv * xsc, g_my), "sinq": _table_tile(sinv * xsc, g_my),
                             "cosk": _table_tile(cosv / xsc, g_kv), "sink": _table_tile(sinv / xsc, g_kv)})
                masks = np.zeros((8 * 2 * 128, 512), np.float32)
                for w in range(8):
                    for ih in range(2):
                        iw = ih * 128 + np.arange(128)[:, None]
                        j = np.arange(512)[None, :]
                        okm = (j >= iw) & (j <= iw + 256)
                        if h == 0 and w == 0:
                            okm &= (j >= 256)
                        masks[(w * 2 + ih) * 128:(w * 2 + ih) * 128 + 128] = np.where(okm, 0.0, NEG_HUGE)
                masksl.append(_cc(masks))

            x_cur_ap = xn1f.ap()[:, 3:3 + TPC]
            for li in range(6):
                sclq = np.zeros((2, 128), np.float32)
                sclk = np.zeros((2, 128), np.float32)
                for j in range(2):
                    sclq[j, j * 64:(j + 1) * 64] = q_scale[li]
                    sclk[j, j * 64:(j + 1) * 64] = k_scale[li]
                xcb = mk2(f"xcb{li}", (DM, 257)); agxcb = mk2(f"agxcb{li}", (2 * DM, 257))
                copy_phase(f"cb{li}", xcb.ap(), x_cur_ap[:, TPC - 257:TPC])
                ag_pair(f"agc{li}", xcb, agxcb)
                xqin = mk2(f"xqin{li}", (DM, 2560))
                run_phase2(f"zq{li}", build_zero_fill, {}, {}, dst_ap=xqin.ap()[:, 2305:2560], rows=DM, cols=255)
                copy_phase(f"cq{li}", xqin.ap()[:, 257:257 + TPC], x_cur_ap)
                run_phase2(f"mq{li}", build_masked_halo, {}, {"mhl": mhl},
                           agsrc=agxcb.ap(), dst=xqin.ap()[:, 0:257], rows=DM, cols=257,
                           mask_name="mhl", src_r0=0, src_c0=0)
                q2b = mk2(f"q2b{li}", (DM, 2048), BF16)
                k2b = mk2(f"k2b{li}", (DM, 2304), BF16)
                vtk = mk2(f"vtk{li}", (2304, DM), BF16)
                run_phase2(f"qk{li}", build_f1qkv,
                           {"x": xqin, "q2b": q2b, "k2b": k2b, "v_tok": vtk},
                           {"wqk": _cc(Wqkv[li][:, :2048]), "wv": _cc(Wqkv[li][:, 2048:]),
                            "cosq": [t["cosq"] for t in tabs], "sinq": [t["sinq"] for t in tabs],
                            "cosk": [t["cosk"] for t in tabs], "sink": [t["sink"] for t in tabs],
                            "sclq": _cc(sclq), "sclk": _cc(sclk), "ind2": ind2, "rotm": rotm})
                xmid = mk2(f"xmid{li}", (DM, TPC))
                run_phase2(f"at{li}", build_f1attn,
                           {"q2b": q2b, "k2b": k2b, "v_tok": vtk, "xres": x_cur_ap, "xo": xmid},
                           {"masks": masksl, "wo": _cc(Wo[li])})
                xmb = mk2(f"xmb{li}", (DM, 1)); agxmb = mk2(f"agxmb{li}", (2 * DM, 1))
                copy_phase(f"cm{li}", xmb.ap(), xmid.ap()[:, TPC - 1:TPC])
                ag_pair(f"agm{li}", xmb, agxmb)
                xhin = mk2(f"xhin{li}", (DM, TPC + 1))
                copy_phase(f"ch{li}", xhin.ap()[:, 1:TPC + 1], xmid.ap())
                run_phase2(f"mx{li}", build_masked_halo, {}, {"mhl": mhl},
                           agsrc=agxmb.ap(), dst=xhin.ap()[:, 0:1], rows=DM, cols=1,
                           mask_name="mhl", src_r0=0, src_c0=0)
                hg = mk2(f"hg{li}", (2730, TPC), BF16)
                run_phase2(f"fh{li}", build_f1h, {"x": xhin, "ones128": ones_e, "hgate": hg},
                           {"w1a": _cc(W1[li][:, :2730]), "w1g": _cc(W1[li][:, 2730:]),
                            "b1a": _cc(b1[li][:2730].reshape(-1, 1)), "b1g": _cc(b1[li][2730:].reshape(-1, 1)),
                            "lng": _cc(ln_g[li].reshape(8, 128).T), "lnb": _cc(ln_b[li].reshape(8, 128).T)})
                if li < 5:
                    xnew = mk2(f"xnew{li}", (DM, TPC))
                    run_phase2(f"w2{li}", build_mm,
                               {"rhs": hg, "res0": xmid, "out_f": xnew},
                               {"lhsT": _cc(W2[li]), "bias": _cc(b2[li].reshape(-1, 1))},
                               K=2730, M=1024, T=TPC, rhs_bf16=True, bias=True, nres=1)
                    x_cur_ap = xnew.ap()
                else:
                    run_phase2(f"w2{li}", build_mm,
                               {"rhs": hg, "res0": xmid, "res1": x1, "res2": attn1,
                                "out_f": out_e},
                               {"lhsT": _cc(W2[li]), "bias": _cc(b2[li].reshape(-1, 1))},
                               K=2730, M=1024, T=TPC, rhs_bf16=True, bias=True, nres=3)

        in_maps = []
        for i in range(NC):
            m = {}
            for full, val in stage.items():
                m[full] = val[i] if isinstance(val, list) else val
            in_maps.append(m)
        res = run_bass_kernel_spmd(p.nc, in_maps, core_ids=list(range(NC)))
    out = np.zeros((B, L, DM), np.float32)
    for i in range(NC):
        b, h = i // 2, i % 2
        out[b, h * TPC:(h + 1) * TPC, :] = np.asarray(res.results[i]["out_e"], np.float32).T
    return out


def kernel(**inputs):
    if MEGA:
        try:
            return kernel_mega(inputs)
        except Exception:
            pass
    store = {}
    run_l0(inputs, store)
    run_l1(inputs, store)
    out = np.zeros((B, L, DM), np.float32)
    for i in range(NC):
        b, h = i // 2, i % 2
        out[b, h * TPC:(h + 1) * TPC, :] = np.asarray(store["x2"][i], np.float32).T
    return out


# version 05
# speedup vs baseline: 1.0237x; 1.0237x over previous
import numpy as np
from contextlib import ExitStack

import concourse.bass as bass
import concourse.mybir as mybir
from concourse.bass_utils import run_bass_kernel_spmd

F32 = mybir.dt.float32
BF16 = mybir.dt.bfloat16
I32 = mybir.dt.int32
AF = mybir.ActivationFunctionType
ALU = mybir.AluOpType
AX = mybir.AxisListType.X

RND = 12582912.0
NEG_HUGE = -3.4028234663852886e38
B, L, DM = 4, 4096, 1024
NC = 8
TPC = 2048


def chunks(n, c):
    return [(i, min(c, n - i)) for i in range(0, n, c)]


class Sem:
    def __init__(self, stack, nc, name):
        self.h = stack.enter_context(nc.semaphore(name))
        self.n = 0

    def inc(self, instr, k):
        instr.then_inc(self.h, k)
        self.n += k
        return instr


class AEng:

    def __init__(self, eng, sem):
        self._e = eng
        self._s = sem

    def __getattr__(self, name):
        fn = getattr(self._e, name)

        def call(*a, **kw):
            self._e.wait_ge(self._s.h, self._s.n)
            self._s.inc(fn(*a, **kw), 1)
            return self._s.n
        return call


class Prog:
    def __init__(self, name, mega=False):
        self.nc = bass.Bass("TRN2", target_bir_lowering=False, detect_race_conditions=False)
        self.stack = ExitStack()
        self.name = name
        self._sems = {}
        self.mega = mega
        self.prefix = ""
        self.feeds = {}
        self.ext_list = []
        self.phase_stack = None

    def begin_phase(self, prefix, feeds=None):
        self.prefix = prefix + "_"
        self.feeds = feeds or {}
        self.phase_stack = ExitStack()

    def end_phase(self):
        phase_barrier(self)
        self.phase_stack.close()
        self.phase_stack = None
        self.prefix = ""
        self.feeds = {}

    @property
    def V(self):
        return AEng(self.nc.vector, self.sem("cv"))

    @property
    def S(self):
        return AEng(self.nc.scalar, self.sem("cs"))

    @property
    def G(self):
        return AEng(self.nc.gpsimd, self.sem("cg"))

    def sem(self, name):
        if name not in self._sems:
            self._sems[name] = Sem(self.stack, self.nc, name)
        return self._sems[name]

    def sbuf(self, name, shape, dt):
        stk = self.phase_stack if self.phase_stack is not None else self.stack
        return stk.enter_context(self.nc.sbuf_tensor(self.prefix + name, shape, dt))

    def psum(self, name, shape, dt=F32):
        stk = self.phase_stack if self.phase_stack is not None else self.stack
        return stk.enter_context(self.nc.psum_tensor(self.prefix + name, shape, dt))

    def din(self, name, shape, dt=F32):
        if name in self.feeds:
            return self.feeds[name]
        full = self.prefix + name
        self.ext_list.append((full, name))
        return self.nc.dram_tensor(full, list(shape), dt, kind="ExternalInput").ap()

    def dout(self, name, shape, dt=F32):
        if name in self.feeds:
            return self.feeds[name]
        if self.mega:
            return self.nc.dram_tensor(self.prefix + name, list(shape), dt).ap()
        return self.nc.dram_tensor(name, list(shape), dt, kind="ExternalOutput").ap()

    def dint(self, name, shape, dt=F32):
        return self.nc.dram_tensor(self.prefix + name, list(shape), dt)


BARRIER_SEMS = ("ld", "lw", "cv", "cs", "ct", "st", "cc", "ts", "cg")


def phase_barrier(p):
    nc = p.nc
    engs = [nc.sync, nc.vector, nc.scalar, nc.tensor, nc.gpsimd]
    for s in BARRIER_SEMS:
        sem = p.sem(s)
        if sem.n == 0:
            continue
        for e in engs:
            e.wait_ge(sem.h, sem.n)


PAIRS = [[0, 1], [2, 3], [4, 5], [6, 7]]
ALL8 = [list(range(8))]


def emit_collective(p, kind, op, groups, src, dst):
    nc = p.nc
    cc = p.sem("cc")
    cc.inc(nc.gpsimd.collective_compute(kind, op, replica_groups=groups,
                                        ins=[src.ap()], outs=[dst.ap()]), 16)
    nc.gpsimd.wait_ge(cc.h, cc.n)


def emit_dram_copy(p, dst_ap, src_ap):
    st = p.sem("st")
    st.inc(p.nc.sync.dma_start(out=dst_ap, in_=src_ap, allow_slow_non_contiguous=True), 16)
    p.nc.sync.wait_ge(st.h, st.n)


def build_gather_rows(p, *, src, idx_name, dst, R, C, dst_c0=0, dt=F32):
    nc = p.nc
    idx = p.din(idx_name, (128, R // 128), I32)
    ld, cg, st = p.sem("ld"), p.sem("cg"), p.sem("st")
    idxt = p.sbuf("g_idx", [128, R // 128], I32)
    tile = p.sbuf("g_tile", [128, C], dt)
    ld.inc(nc.sync.dma_start(out=idxt[:], in_=idx), 16)
    nc.gpsimd.wait_ge(ld.h, ld.n)
    for ki in range(R // 128):
        nc.gpsimd.wait_ge(st.h, st.n)
        cg.inc(nc.gpsimd.indirect_dma_start(
            out=tile[:], out_offset=None, in_=src,
            in_offset=bass.IndirectOffsetOnAxis(ap=idxt[:, ki:ki + 1], axis=0)), 16)
        nc.sync.wait_ge(cg.h, cg.n)
        st.inc(nc.sync.dma_start(out=dst[ki * 128:(ki + 1) * 128, dst_c0:dst_c0 + C], in_=tile[:]), 16)
    nc.sync.wait_ge(st.h, st.n)


def build_zero_fill(p, dst_ap, rows, cols):
    nc = p.nc
    st = p.sem("st")
    cw = min(cols, 2048)
    z = p.sbuf("zf", [128, cw], F32)
    p.V.memset(z[:], 0.0)
    nc.sync.wait_ge(p.sem("cv").h, p.sem("cv").n)
    for k0, nk in chunks(rows, 128):
        for c0, ncn in chunks(cols, cw):
            st.inc(nc.sync.dma_start(out=dst_ap[k0:k0 + nk, c0:c0 + ncn], in_=z[0:nk, 0:ncn]), 16)
    nc.sync.wait_ge(st.h, st.n)


def build_masked_halo(p, *, agsrc, dst, rows, cols, mask_name, src_r0=0, src_c0=0):
    nc = p.nc
    maskd = p.din(mask_name, (128, 1))
    ld, st, cv = p.sem("ld"), p.sem("st"), p.sem("cv")
    mt = p.sbuf("mh_m", [128, 1], F32)
    tile = p.sbuf("mh_t", [128, cols], F32)
    ld.inc(nc.sync.dma_start(out=mt[:], in_=maskd), 16)
    for ki, (k0, nk) in enumerate(chunks(rows, 128)):
        nc.sync.wait_ge(cv.h, cv.n)
        ld.inc(nc.sync.dma_start(out=tile[0:nk, :], in_=agsrc[src_r0 + k0:src_r0 + k0 + nk, src_c0:src_c0 + cols],
                                 allow_slow_non_contiguous=True), 16)
        nc.vector.wait_ge(ld.h, ld.n)
        nc.vector.wait_ge(st.h, st.n)
        p.V.tensor_scalar(tile[0:nk, :], tile[0:nk, :], mt[0:nk, 0:1], None, ALU.mult)
        nc.sync.wait_ge(cv.h, cv.n)
        st.inc(nc.sync.dma_start(out=dst[k0:k0 + nk, 0:cols], in_=tile[0:nk, :],
                                 allow_slow_non_contiguous=True), 16)
    nc.sync.wait_ge(st.h, st.n)


_BUILT = {}


def run_launch(key, builder, in_maps, **kw):
    if key not in _BUILT:
        p = Prog(key)
        with p.stack:
            with p.nc.Block():
                builder(p, **kw)
            p.nc.finalize()
            _BUILT[key] = p.nc
    import os, time
    t0 = time.time()
    res = run_bass_kernel_spmd(_BUILT[key], in_maps, core_ids=list(range(NC)))
    if os.environ.get("KDEV"):
        print(f"[launch {key}: {time.time()-t0:.1f}s]", flush=True)
    return res.results


def epi_silu(p, ot, nm, tt, cv, cs):
    nc = p.nc
    nc.scalar.wait_ge(cv.h, cv.n)
    p.S.activation(ot[0:nm, 0:tt], ot[0:nm, 0:tt], AF.Silu)
    return cs


def epi_dt_a(p, ot, nm, tt, cv, cs, scratch):
    nc = p.nc
    s1, s2 = scratch
    nc.scalar.wait_ge(cv.h, cv.n)
    p.S.activation(s1[0:nm, 0:tt], ot[0:nm, 0:tt], AF.Abs)
    nc.scalar.wait_ge(cs.h, cs.n)
    p.S.activation(s1[0:nm, 0:tt], s1[0:nm, 0:tt], AF.Exp, scale=-1.0)
    nc.scalar.wait_ge(cs.h, cs.n)
    p.S.activation(s1[0:nm, 0:tt], s1[0:nm, 0:tt], AF.Ln, bias=1.0)
    nc.vector.wait_ge(cs.h, cs.n)
    p.V.tensor_scalar(s2[0:nm, 0:tt], ot[0:nm, 0:tt], 0.0, None, ALU.max)
    p.V.tensor_add(ot[0:nm, 0:tt], s1[0:nm, 0:tt], s2[0:nm, 0:tt])
    p.V.tensor_scalar(ot[0:nm, 0:tt], ot[0:nm, 0:tt], 1.0, 1e-4, ALU.min, ALU.max)
    p.V.tensor_scalar(ot[0:nm, 0:tt], ot[0:nm, 0:tt], -1.0, 1.0, ALU.mult, ALU.add)
    return cv


def build_mm(p, *, K, M, T, TT=512, rhs_bf16=False, lhsT_bf16=False,
             out_f32=True, out_bf16=False, nres=0, bias=False, epilogue=None):
    nc = p.nc
    kcs, mcs, tts = chunks(K, 128), chunks(M, 128), chunks(T, TT)
    KCn = len(kcs)
    lhsT = p.din("lhsT", (K, M), BF16 if lhsT_bf16 else F32)
    rhs = p.din("rhs", (K, T), BF16 if rhs_bf16 else F32)
    biasd = p.din("bias", (M, 1)) if bias else None
    resd = [p.din(f"res{j}", (M, T)) for j in range(nres)]
    outf = p.dout("out_f", (M, T)) if out_f32 else None
    outb = p.dout("out_b", (M, T), BF16) if out_bf16 else None

    ld, lw, cv, cs, ct, st = (p.sem(s) for s in ("ld", "lw", "cv", "cs", "ct", "st"))

    rhsb = p.sbuf("rhsb", [128, KCn * T], BF16)
    rhsf = None if rhs_bf16 else p.sbuf("rhsf", [128, T], F32)
    wtb = [p.sbuf(f"wtb{i}", [128, KCn * 128], BF16) for i in range(2)]
    wtf = None if lhsT_bf16 else [p.sbuf(f"wtf{i}", [128, KCn * 128], F32) for i in range(2)]
    if wtf is not None:
        p.S.memzero(wtf[0][:, :])
        p.S.memzero(wtf[1][:, :])
    ot = [p.sbuf(f"ot{i}", [128, TT], F32) for i in range(2)]
    obt = [p.sbuf(f"obt{i}", [128, TT], BF16) for i in range(2)] if out_bf16 else None
    rtl = [p.sbuf(f"rt{j}", [128, TT], F32) for j in range(nres)]
    btile = p.sbuf("btile", [128, 1], F32) if bias else None
    scr = (p.sbuf("scr1", [128, TT], F32), p.sbuf("scr2", [128, TT], F32))
    ps = [p.psum(f"ps{i}", [128, TT]) for i in range(2)]

    for ki, (k0, nk) in enumerate(kcs):
        if rhs_bf16:
            ld.inc(nc.sync.dma_start(out=rhsb[0:nk, ki * T:ki * T + T], in_=rhs[k0:k0 + nk, :]), 16)
        else:
            nc.sync.wait_ge(cv.h, cv.n)
            ld.inc(nc.sync.dma_start(out=rhsf[0:nk, :], in_=rhs[k0:k0 + nk, :]), 16)
            nc.vector.wait_ge(ld.h, ld.n)
            p.V.tensor_copy(rhsb[0:nk, ki * T:ki * T + T], rhsf[0:nk, :])
    rhs_v, rhs_d = cv.n, ld.n

    wt_last_use = [0, 0]
    ps_free = [0, 0]
    q = 0
    for mi, (m0, nm) in enumerate(mcs):
        buf = mi % 2
        nc.scalar.wait_ge(ct.h, wt_last_use[buf])
        for ki, (k0, nk) in enumerate(kcs):
            lw.inc(nc.scalar.dma_start(out=(wtb if lhsT_bf16 else wtf)[buf][0:nk, ki * 128:ki * 128 + nm],
                                       in_=lhsT[k0:k0 + nk, m0:m0 + nm]), 16)
        if not lhsT_bf16:
            nc.scalar.wait_ge(lw.h, lw.n)
            p.S.copy(wtb[buf][:, :], wtf[buf][:, :])
        w_lw, w_cs = lw.n, cs.n
        if bias:
            nc.sync.wait_ge(cv.h, cv.n)
            lw.inc(nc.sync.dma_start(out=btile[0:nm, :], in_=biasd[m0:m0 + nm, :]), 16)
            w_lw = lw.n

        for ti, (t0, tt) in enumerate(tts):
            pb = q % 2
            ob = q % 2
            for j in range(nres):
                nc.sync.wait_ge(cv.h, cv.n)
                ld.inc(nc.sync.dma_start(out=rtl[j][0:nm, 0:tt], in_=resd[j][m0:m0 + nm, t0:t0 + tt]), 16)
            res_d = ld.n
            nc.tensor.wait_ge(lw.h, w_lw)
            nc.tensor.wait_ge(cs.h, w_cs)
            nc.tensor.wait_ge(ld.h, rhs_d)
            nc.tensor.wait_ge(cv.h, max(ps_free[pb], rhs_v))
            ins = None
            for ki, (k0, nk) in enumerate(kcs):
                ins = nc.tensor.matmul(ps[pb][0:nm, 0:tt],
                                       wtb[buf][0:nk, ki * 128:ki * 128 + nm],
                                       rhsb[0:nk, ki * T + t0:ki * T + t0 + tt],
                                       start=(ki == 0), stop=(ki == KCn - 1))
            ct.inc(ins, 1)
            wt_last_use[buf] = ct.n
            mm_mark = ct.n

            nc.vector.wait_ge(ct.h, mm_mark)
            nc.vector.wait_ge(st.h, st.n)
            if bias:
                nc.vector.wait_ge(lw.h, w_lw)
                p.V.tensor_scalar(ot[ob][0:nm, 0:tt], ps[pb][0:nm, 0:tt],
                                               btile[0:nm, 0:1], None, ALU.add)
            else:
                p.V.tensor_copy(ot[ob][0:nm, 0:tt], ps[pb][0:nm, 0:tt])
            ps_free[pb] = cv.n
            for j in range(nres):
                nc.vector.wait_ge(ld.h, res_d)
                p.V.tensor_add(ot[ob][0:nm, 0:tt], ot[ob][0:nm, 0:tt], rtl[j][0:nm, 0:tt])
            prod = cv
            if epilogue is not None:
                prod = epi_dt_a(p, ot[ob], nm, tt, cv, cs, scr) if epilogue == "dt_a" else epilogue(p, ot[ob], nm, tt, cv, cs)
            prod_mark = prod.n

            nc.sync.wait_ge(prod.h, prod_mark)
            if out_f32:
                st.inc(nc.sync.dma_start(out=outf[m0:m0 + nm, t0:t0 + tt], in_=ot[ob][0:nm, 0:tt]), 16)
            if out_bf16:
                nc.scalar.wait_ge(prod.h, prod_mark)
                nc.scalar.wait_ge(st.h, st.n)
                p.S.copy(obt[ob][0:nm, 0:tt], ot[ob][0:nm, 0:tt])
                nc.sync.wait_ge(cs.h, cs.n)
                st.inc(nc.sync.dma_start(out=outb[m0:m0 + nm, t0:t0 + tt], in_=obt[ob][0:nm, 0:tt]), 16)
            q += 1
    nc.sync.wait_ge(st.h, st.n)


def build_ln(p, *, T, want_pmax=False, DMdim=DM):
    nc = p.nc
    TT = 512
    kcs = chunks(DMdim, 128)
    x = p.din("x", (DMdim, T))
    onesd = p.din("ones128", (128, 128))
    xnf = p.dout("xn_f", (DMdim, T))
    xnb = p.dout("xn_b", (DMdim, T), BF16)
    pmax = p.dout("pmax", (1, 8)) if want_pmax else None
    pmax_bounce = p.dint("pmax_bounce", (128, 1)) if want_pmax else None

    ld, cv, cs, ct, st = (p.sem(s) for s in ("ld", "cv", "cs", "ct", "st"))
    xt = p.sbuf("xt", [128, len(kcs) * TT], F32)
    sq = [p.sbuf(f"sq{i}", [128, TT], F32) for i in range(2)]
    ones = p.sbuf("ones", [128, 128], F32)
    statm = p.sbuf("statm", [1, TT], F32)
    statr = p.sbuf("statr", [1, TT], F32)
    tvar = p.sbuf("tvar", [1, TT], F32)
    tmsq = p.sbuf("tmsq", [1, TT], F32)
    xo = p.sbuf("xo", [128, TT], F32)
    xob = p.sbuf("xob", [128, TT], BF16)
    mx = p.sbuf("mx", [128, 1], F32)
    mx2 = p.sbuf("mx2", [128, 1], F32)
    mxr = p.sbuf("mxr", [1, 128], F32)
    out8 = p.sbuf("out8", [1, 8], F32)
    ps_s = p.psum("ps_s", [1, TT])
    ps_q = p.psum("ps_q", [1, TT])
    ps_b = p.psum("ps_b", [128, 2 * TT])

    ld.inc(nc.sync.dma_start(out=ones[:], in_=onesd), 16)
    first_max = True
    for ti, (t0, tt) in enumerate(chunks(T, TT)):
        nc.sync.wait_ge(cv.h, cv.n)
        nc.sync.wait_ge(cs.h, cs.n)
        nc.sync.wait_ge(ct.h, ct.n)
        for ki, (k0, nk) in enumerate(kcs):
            ld.inc(nc.sync.dma_start(out=xt[0:nk, ki * TT:ki * TT + tt], in_=x[k0:k0 + nk, t0:t0 + tt]), 16)
        tile_ld = ld.n
        nc.tensor.wait_ge(ld.h, tile_ld)
        ins = None
        for ki, (k0, nk) in enumerate(kcs):
            ins = nc.tensor.matmul(ps_s[0:1, 0:tt], ones[0:nk, 0:1], xt[0:nk, ki * TT:ki * TT + tt],
                                   start=(ki == 0), stop=(ki == len(kcs) - 1))
        ct.inc(ins, 1)
        mm_sq = []
        for ki, (k0, nk) in enumerate(kcs):
            sb = sq[ki % 2]
            nc.scalar.wait_ge(ld.h, tile_ld)
            if ki >= 2:
                nc.scalar.wait_ge(ct.h, mm_sq[ki - 2])
            p.S.activation(sb[0:nk, 0:tt], xt[0:nk, ki * TT:ki * TT + tt], AF.Square)
            nc.tensor.wait_ge(cs.h, cs.n)
            ins = nc.tensor.matmul(ps_q[0:1, 0:tt], ones[0:nk, 0:1], sb[0:nk, 0:tt],
                                   start=(ki == 0), stop=(ki == len(kcs) - 1))
            ct.inc(ins, 1)
            mm_sq.append(ct.n)
        nc.vector.wait_ge(ct.h, ct.n)
        p.V.tensor_scalar(statm[0:1, 0:tt], ps_s[0:1, 0:tt], 1.0 / DMdim, None, ALU.mult)
        p.V.tensor_scalar(tmsq[0:1, 0:tt], ps_q[0:1, 0:tt], 1.0 / DMdim, None, ALU.mult)
        p.V.tensor_tensor(tvar[0:1, 0:tt], statm[0:1, 0:tt], statm[0:1, 0:tt], ALU.mult)
        p.V.tensor_tensor(tvar[0:1, 0:tt], tmsq[0:1, 0:tt], tvar[0:1, 0:tt], ALU.subtract)
        p.V.tensor_scalar(tvar[0:1, 0:tt], tvar[0:1, 0:tt], 1e-5, None, ALU.add)
        nc.scalar.wait_ge(cv.h, cv.n)
        p.S.activation(tvar[0:1, 0:tt], tvar[0:1, 0:tt], AF.Ln)
        nc.scalar.wait_ge(cs.h, cs.n)
        p.S.activation(statr[0:1, 0:tt], tvar[0:1, 0:tt], AF.Exp, scale=-0.5)
        nc.tensor.wait_ge(cs.h, cs.n)
        nc.tensor.wait_ge(cv.h, cv.n)
        nc.tensor.matmul(ps_b[:, 0:tt], ones[0:1, :], statm[0:1, 0:tt], start=True, stop=True)
        ins2 = nc.tensor.matmul(ps_b[:, TT:TT + tt], ones[0:1, :], statr[0:1, 0:tt], start=True, stop=True)
        ct.inc(ins2, 1)
        bcast_mark = ct.n
        for ki, (k0, nk) in enumerate(kcs):
            nc.vector.wait_ge(ct.h, bcast_mark)
            nc.vector.wait_ge(st.h, st.n)
            p.V.tensor_tensor(xo[0:nk, 0:tt], xt[0:nk, ki * TT:ki * TT + tt], ps_b[0:nk, 0:tt], ALU.subtract)
            p.V.tensor_tensor(xo[0:nk, 0:tt], xo[0:nk, 0:tt], ps_b[0:nk, TT:TT + tt], ALU.mult)
            if want_pmax:
                if first_max:
                    p.V.reduce_max(mx[0:nk, 0:1], xo[0:nk, 0:tt], axis=AX, apply_absolute_value=True)
                    first_max = False
                else:
                    p.V.reduce_max(mx2[0:nk, 0:1], xo[0:nk, 0:tt], axis=AX, apply_absolute_value=True)
                    p.V.tensor_tensor(mx[0:nk, 0:1], mx[0:nk, 0:1], mx2[0:nk, 0:1], ALU.max)
            nc.scalar.wait_ge(cv.h, cv.n)
            nc.scalar.wait_ge(st.h, st.n)
            p.S.copy(xob[0:nk, 0:tt], xo[0:nk, 0:tt])
            nc.sync.wait_ge(cv.h, cv.n)
            nc.sync.wait_ge(cs.h, cs.n)
            st.inc(nc.sync.dma_start(out=xnf[k0:k0 + nk, t0:t0 + tt], in_=xo[0:nk, 0:tt]), 16)
            st.inc(nc.sync.dma_start(out=xnb[k0:k0 + nk, t0:t0 + tt], in_=xob[0:nk, 0:tt]), 16)
    if want_pmax:
        nc.sync.wait_ge(cv.h, cv.n)
        st.inc(nc.sync.dma_start(out=pmax_bounce.ap(), in_=mx[:]), 16)
        nc.sync.wait_ge(st.h, st.n)
        ld.inc(nc.sync.dma_start(out=mxr[0:1, :], in_=pmax_bounce.ap().rearrange("p x -> x p")), 16)
        nc.vector.wait_ge(ld.h, ld.n)
        p.V.memset(out8[0:1, :], 0.0)
        p.V.reduce_max(out8[0:1, 0:1], mxr[0:1, :], axis=AX)
        nc.sync.wait_ge(cv.h, cv.n)
        st.inc(nc.sync.dma_start(out=pmax, in_=out8[0:1, :]), 16)
    nc.sync.wait_ge(st.h, st.n)


def build_conv0(p, *, D=2048, T=TPC, TH=2560, HALO=3, off=0):
    nc = p.nc
    TT = 512
    xc = p.din("xc", (D, TH))
    w = p.din("cw", (D, 4))
    uf = p.dout("u_f", (D, T))
    ub = p.dout("u_b", (D, T), BF16)
    ld, cv, cs, st = (p.sem(s) for s in ("ld", "cv", "cs", "st"))
    xt = p.sbuf("xt", [128, TT + HALO], F32)
    wt = p.sbuf("wt", [128, 4], F32)
    acc = [p.sbuf(f"acc{i}", [128, TT], F32) for i in range(2)]
    ob = p.sbuf("ob", [128, TT], BF16)
    for ki, (k0, nk) in enumerate(chunks(D, 128)):
        nc.sync.wait_ge(cv.h, cv.n)
        ld.inc(nc.sync.dma_start(out=wt[0:nk, :], in_=w[k0:k0 + nk, :]), 16)
        for ti, (t0, tt) in enumerate(chunks(T, TT)):
            nc.sync.wait_ge(st.h, st.n)
            nc.sync.wait_ge(cv.h, cv.n)
            nc.sync.wait_ge(cs.h, p.sem("cs").n)
            ld.inc(nc.sync.dma_start(out=xt[0:nk, 0:tt + HALO], in_=xc[k0:k0 + nk, t0 + off:t0 + off + tt + HALO]), 16)
            nc.vector.wait_ge(ld.h, ld.n)
            p.V.tensor_scalar(acc[0][0:nk, 0:tt], xt[0:nk, 0:tt], wt[0:nk, 0:1], None, ALU.mult)
            cur = 0
            for j in range(1, 4):
                p.V.scalar_tensor_tensor(acc[1 - cur][0:nk, 0:tt], xt[0:nk, j:j + tt],
                                                      wt[0:nk, j:j + 1], acc[cur][0:nk, 0:tt],
                                                      ALU.mult, ALU.add)
                cur = 1 - cur
            nc.scalar.wait_ge(cv.h, cv.n)
            p.S.activation(acc[cur][0:nk, 0:tt], acc[cur][0:nk, 0:tt], AF.Silu)
            nc.scalar.wait_ge(cs.h, cs.n)
            p.S.copy(ob[0:nk, 0:tt], acc[cur][0:nk, 0:tt])
            nc.sync.wait_ge(cs.h, cs.n)
            st.inc(nc.sync.dma_start(out=uf[k0:k0 + nk, t0:t0 + tt], in_=acc[cur][0:nk, 0:tt]), 16)
            st.inc(nc.sync.dma_start(out=ub[k0:k0 + nk, t0:t0 + tt], in_=ob[0:nk, 0:tt]), 16)
    nc.sync.wait_ge(st.h, st.n)


def build_conv1(p, *, D=2048, T=TPC, TH=2560, HALO=3, off=0):
    nc = p.nc
    TT = 512
    xc = p.din("xc", (D, TH))
    w = p.din("cw", (D, 4))
    cbd = p.din("cb", (D, 1))
    uf = p.dout("u_f", (D, T))
    ub = p.dout("u_b", (D, T), BF16)
    ld, cv, cs, st = (p.sem(s) for s in ("ld", "cv", "cs", "st"))
    xt = p.sbuf("xt", [128, TT + HALO], F32)
    wt = p.sbuf("wt", [128, 4], F32)
    cbt = p.sbuf("cbt", [128, 1], F32)
    acc = [p.sbuf(f"acc{i}", [128, TT], F32) for i in range(2)]
    ob = p.sbuf("ob", [128, TT], BF16)
    for ki, (k0, nk) in enumerate(chunks(D, 128)):
        nc.sync.wait_ge(cv.h, cv.n)
        ld.inc(nc.sync.dma_start(out=wt[0:nk, :], in_=w[k0:k0 + nk, :]), 16)
        ld.inc(nc.sync.dma_start(out=cbt[0:nk, :], in_=cbd[k0:k0 + nk, :]), 16)
        for ti, (t0, tt) in enumerate(chunks(T, TT)):
            nc.sync.wait_ge(st.h, st.n)
            nc.sync.wait_ge(cv.h, cv.n)
            nc.sync.wait_ge(cs.h, p.sem("cs").n)
            ld.inc(nc.sync.dma_start(out=xt[0:nk, 0:tt + HALO], in_=xc[k0:k0 + nk, t0 + off:t0 + off + tt + HALO]), 16)
            nc.vector.wait_ge(ld.h, ld.n)
            p.V.tensor_scalar(acc[0][0:nk, 0:tt], xt[0:nk, 0:tt], wt[0:nk, 0:1], None, ALU.mult)
            cur = 0
            for j in range(1, 4):
                p.V.scalar_tensor_tensor(acc[1 - cur][0:nk, 0:tt], xt[0:nk, j:j + tt],
                                         wt[0:nk, j:j + 1], acc[cur][0:nk, 0:tt], ALU.mult, ALU.add)
                cur = 1 - cur
            p.V.tensor_scalar(acc[cur][0:nk, 0:tt], acc[cur][0:nk, 0:tt], cbt[0:nk, 0:1], None, ALU.add)
            nc.scalar.wait_ge(cv.h, cv.n)
            p.S.activation(acc[cur][0:nk, 0:tt], acc[cur][0:nk, 0:tt], AF.Silu)
            p.S.copy(ob[0:nk, 0:tt], acc[cur][0:nk, 0:tt])
            nc.sync.wait_ge(cs.h, p.sem("cs").n)
            st.inc(nc.sync.dma_start(out=uf[k0:k0 + nk, t0:t0 + tt], in_=acc[cur][0:nk, 0:tt]), 16)
            st.inc(nc.sync.dma_start(out=ub[k0:k0 + nk, t0:t0 + tt], in_=ob[0:nk, 0:tt]), 16)
    nc.sync.wait_ge(st.h, st.n)


def build_scan0(p, *, R=1024, T=L, ag=False):
    nc = p.nc
    DC = R // 128
    if ag:
        a = p.din("ag_a", (2 * 2048, TPC))
        ubi = p.din("ag_ub", (2 * 2048, TPC))
        idxd = p.din("sc0_idx", (128, 2 * DC), I32)
        ys = p.dout("ys", (2 * R, T // 2))
        ysv = ys.rearrange("(d h) t -> d (h t)", h=2)
    else:
        a = p.din("a", (R, T))
        ubi = p.din("ub", (R, T))
        ys = p.dout("ys", (R, T))
        ysv = ys
    ld, cv, st, cg = (p.sem(s) for s in ("ld", "cv", "st", "cg"))
    at = p.sbuf("at", [128, T], F32)
    bt = p.sbuf("bt", [128, T], F32)
    yt = p.sbuf("yt", [128, T], F32)
    idxt = p.sbuf("idxt", [128, 2 * DC], I32) if ag else None
    if ag:
        ld.inc(nc.sync.dma_start(out=idxt[:], in_=idxd), 16)
        nc.gpsimd.wait_ge(ld.h, ld.n)
    for ki, (k0, nk) in enumerate(chunks(R, 128)):
        nc.sync.wait_ge(cv.h, cv.n)
        nc.sync.wait_ge(st.h, st.n)
        if ag:
            nc.gpsimd.wait_ge(cv.h, cv.n)
            nc.gpsimd.wait_ge(st.h, st.n)
            for half in range(2):
                cg.inc(nc.gpsimd.indirect_dma_start(
                    out=at[:, half * TPC:(half + 1) * TPC], out_offset=None, in_=a,
                    in_offset=bass.IndirectOffsetOnAxis(ap=idxt[:, half * DC + ki:half * DC + ki + 1], axis=0)), 16)
                cg.inc(nc.gpsimd.indirect_dma_start(
                    out=bt[:, half * TPC:(half + 1) * TPC], out_offset=None, in_=ubi,
                    in_offset=bass.IndirectOffsetOnAxis(ap=idxt[:, half * DC + ki:half * DC + ki + 1], axis=0)), 16)
            nc.vector.wait_ge(cg.h, cg.n)
        else:
            ld.inc(nc.sync.dma_start(out=at[0:nk, :], in_=a[k0:k0 + nk, :]), 16)
            ld.inc(nc.sync.dma_start(out=bt[0:nk, :], in_=ubi[k0:k0 + nk, :]), 16)
            nc.vector.wait_ge(ld.h, ld.n)
        p.V.tensor_tensor_scan(yt[0:nk, :], at[0:nk, :], bt[0:nk, :], 0.0, ALU.mult, ALU.add)
        nc.sync.wait_ge(cv.h, cv.n)
        st.inc(nc.sync.dma_start(out=ysv[k0:k0 + nk, :], in_=yt[0:nk, :]), 16)
    nc.sync.wait_ge(st.h, st.n)


def build_mul(p, *, R, T):
    nc = p.nc
    TT = 512
    a = p.din("a", (R, T))
    b = p.din("b", (R, T))
    o = p.dout("o", (R, T))
    ld, cv, st = (p.sem(s) for s in ("ld", "cv", "st"))
    ta = p.sbuf("ta", [128, TT], F32)
    tb = p.sbuf("tb", [128, TT], F32)
    to = p.sbuf("to", [128, TT], F32)
    for ki, (k0, nk) in enumerate(chunks(R, 128)):
        for ti, (t0, tt) in enumerate(chunks(T, TT)):
            nc.sync.wait_ge(cv.h, cv.n)
            nc.sync.wait_ge(st.h, st.n)
            ld.inc(nc.sync.dma_start(out=ta[0:nk, 0:tt], in_=a[k0:k0 + nk, t0:t0 + tt]), 16)
            ld.inc(nc.sync.dma_start(out=tb[0:nk, 0:tt], in_=b[k0:k0 + nk, t0:t0 + tt]), 16)
            nc.vector.wait_ge(ld.h, ld.n)
            p.V.tensor_tensor(to[0:nk, 0:tt], ta[0:nk, 0:tt], tb[0:nk, 0:tt], ALU.mult)
            nc.sync.wait_ge(cv.h, cv.n)
            st.inc(nc.sync.dma_start(out=o[k0:k0 + nk, t0:t0 + tt], in_=to[0:nk, 0:tt]), 16)
    nc.sync.wait_ge(st.h, st.n)


def build_l0post(p, *, R=2048, T=TPC):
    nc = p.nc
    TT = 512
    ysd = p.din("ys", (R, T))
    ud = p.din("u", (R, T))
    szd = p.din("sz", (R, T))
    Dd = p.din("Dp", (R, 1))
    ob = p.dout("y_b", (R, T), BF16)
    ld, cv, cs, st = (p.sem(s) for s in ("ld", "cv", "cs", "st"))
    t1 = p.sbuf("t1", [128, TT], F32)
    t2 = p.sbuf("t2", [128, TT], F32)
    t3 = p.sbuf("t3", [128, TT], F32)
    dt_ = p.sbuf("dt", [128, 1], F32)
    tb = p.sbuf("tb", [128, TT], BF16)
    for ki, (k0, nk) in enumerate(chunks(R, 128)):
        nc.sync.wait_ge(cv.h, cv.n)
        ld.inc(nc.sync.dma_start(out=dt_[0:nk, :], in_=Dd[k0:k0 + nk, :]), 16)
        for ti, (t0, tt) in enumerate(chunks(T, TT)):
            nc.sync.wait_ge(cv.h, cv.n)
            nc.sync.wait_ge(st.h, st.n)
            ld.inc(nc.sync.dma_start(out=t1[0:nk, 0:tt], in_=ysd[k0:k0 + nk, t0:t0 + tt]), 16)
            ld.inc(nc.sync.dma_start(out=t2[0:nk, 0:tt], in_=ud[k0:k0 + nk, t0:t0 + tt]), 16)
            ld.inc(nc.sync.dma_start(out=t3[0:nk, 0:tt], in_=szd[k0:k0 + nk, t0:t0 + tt]), 16)
            nc.vector.wait_ge(ld.h, ld.n)
            p.V.scalar_tensor_tensor(t2[0:nk, 0:tt], t2[0:nk, 0:tt], dt_[0:nk, 0:1],
                                                  t1[0:nk, 0:tt], ALU.mult, ALU.add)
            p.V.tensor_tensor(t2[0:nk, 0:tt], t2[0:nk, 0:tt], t3[0:nk, 0:tt], ALU.mult)
            nc.scalar.wait_ge(cv.h, cv.n)
            p.S.copy(tb[0:nk, 0:tt], t2[0:nk, 0:tt])
            nc.sync.wait_ge(cs.h, cs.n)
            st.inc(nc.sync.dma_start(out=ob[k0:k0 + nk, t0:t0 + tt], in_=tb[0:nk, 0:tt]), 16)
    nc.sync.wait_ge(st.h, st.n)


def build_dequant(p, *, O, I):
    nc = p.nc
    NB = I // 32
    w = p.din("w", (O, I))
    ident = p.din("ident", (128, 128))
    wt_ = p.dout("w_dq_T", (I, O), BF16)
    ld, cv, cs, ct, st = (p.sem(s) for s in ("ld", "cv", "cs", "ct", "st"))
    wtile = p.sbuf("wtile", [128, I], F32)
    bm = p.sbuf("bm", [128, NB], F32)
    rs = p.sbuf("rs", [128, NB], F32)
    wq = p.sbuf("wq", [128, I], BF16)
    idb = p.sbuf("idb", [128, 128], BF16)
    idf = p.sbuf("idf", [128, 128], F32)
    trb = p.sbuf("trb", [128, 128], BF16)
    pst = p.psum("pst", [128, 128], BF16)
    ld.inc(nc.sync.dma_start(out=idf[:], in_=ident), 16)
    nc.vector.wait_ge(ld.h, ld.n)
    p.V.tensor_copy(idb[:], idf[:])
    for oi, (o0, no) in enumerate(chunks(O, 128)):
        nc.sync.wait_ge(cv.h, cv.n)
        ld.inc(nc.sync.dma_start(out=wtile[0:no, :], in_=w[o0:o0 + no, :]), 16)
        nc.vector.wait_ge(ct.h, ct.n)
        nc.vector.wait_ge(ld.h, ld.n)
        p.V.reduce_max(bm.ap().unsqueeze(2)[0:no], wtile.ap().rearrange("p (a b) -> p a b", b=32)[0:no],
                                    axis=AX, apply_absolute_value=True)
        nc.vector.wait_ge(cv.h, cv.n)
        p.V.tensor_scalar(bm[0:no, :], bm[0:no, :], 1.0 / 7.0, 1e-12, ALU.mult, ALU.max)
        nc.vector.wait_ge(cv.h, cv.n)
        p.V.reciprocal(rs[0:no, :], bm[0:no, :])
        nc.vector.wait_ge(cv.h, cv.n)
        wv = wtile.ap().rearrange("p (a b) -> p a b", b=32)
        p.V.tensor_tensor(wv[0:no], wv[0:no], rs.ap().unsqueeze(2).broadcast_to([128, NB, 32])[0:no], ALU.mult)
        nc.vector.wait_ge(cv.h, cv.n)
        p.V.tensor_scalar(wtile[0:no, :], wtile[0:no, :], RND, RND, ALU.add, ALU.subtract)
        nc.vector.wait_ge(cv.h, cv.n)
        p.V.tensor_scalar(wtile[0:no, :], wtile[0:no, :], 7.0, -8.0, ALU.min, ALU.max)
        nc.vector.wait_ge(cv.h, cv.n)
        p.V.tensor_tensor(wq.ap().rearrange("p (a b) -> p a b", b=32)[0:no], wv[0:no],
                                       bm.ap().unsqueeze(2).broadcast_to([128, NB, 32])[0:no], ALU.mult)
        q_mark = cv.n
        for ii, (i0, ni) in enumerate(chunks(I, 128)):
            nc.tensor.wait_ge(cv.h, cv.n)
            ct.inc(nc.tensor.transpose(pst[:, :], wq[:, i0:i0 + 128], idb[:]), 1)
            nc.vector.wait_ge(ct.h, ct.n)
            nc.vector.wait_ge(st.h, st.n)
            p.V.tensor_copy(trb[0:ni, 0:no], pst[0:ni, 0:no])
            nc.sync.wait_ge(cv.h, cv.n)
            st.inc(nc.sync.dma_start(out=wt_[i0:i0 + ni, o0:o0 + no], in_=trb[0:ni, 0:no]), 16)
    nc.sync.wait_ge(st.h, st.n)


def build_qdq_x(p, *, R, T, out_f32=False, out_name="xq_b"):
    nc = p.nc
    TT = 512
    QMAX = 32767.0
    x = p.din("x", (R, T))
    pmax8 = p.din("pmax8", (1, 8))
    ob = p.dout(out_name, (R, T), BF16)
    of = p.dout(out_name + "_f", (R, T)) if out_f32 else None
    sc_b = p.dint("sc_b", (1, 2))
    ld, cv, cs, st = (p.sem(s) for s in ("ld", "cv", "cs", "st"))
    pm = p.sbuf("pm", [1, 8], F32)
    sst = p.sbuf("sst", [1, 2], F32)
    qm = p.sbuf("qm", [1, 1], F32)
    sbc = p.sbuf("sbc", [128, 2], F32)
    xt = p.sbuf("xt", [128, TT], F32)
    xb = p.sbuf("xb", [128, TT], BF16)
    ld.inc(nc.sync.dma_start(out=pm[:], in_=pmax8), 16)
    nc.vector.wait_ge(ld.h, ld.n)
    p.V.reduce_max(qm[0:1, 0:1], pm[0:1, :], axis=AX)
    p.V.tensor_scalar(sst[0:1, 0:1], qm[0:1, 0:1], 1.0 / QMAX, 1e-12, ALU.mult, ALU.max)
    p.V.reciprocal(sst[0:1, 1:2], sst[0:1, 0:1])
    nc.sync.wait_ge(cv.h, cv.n)
    st.inc(nc.sync.dma_start(out=sc_b.ap(), in_=sst[:]), 16)
    nc.sync.wait_ge(st.h, st.n)
    ld.inc(nc.sync.dma_start(out=sbc[:], in_=sc_b.ap().partition_broadcast(128)), 16)
    sbc_d = ld.n
    for ki, (k0, nk) in enumerate(chunks(R, 128)):
        for ti, (t0, tt) in enumerate(chunks(T, TT)):
            nc.sync.wait_ge(cv.h, cv.n)
            nc.sync.wait_ge(cs.h, cs.n)
            nc.sync.wait_ge(st.h, st.n)
            ld.inc(nc.sync.dma_start(out=xt[0:nk, 0:tt], in_=x[k0:k0 + nk, t0:t0 + tt]), 16)
            nc.vector.wait_ge(ld.h, ld.n)
            p.V.tensor_scalar(xt[0:nk, 0:tt], xt[0:nk, 0:tt], sbc[0:nk, 1:2], RND, ALU.mult, ALU.add)
            p.V.tensor_scalar(xt[0:nk, 0:tt], xt[0:nk, 0:tt], RND, QMAX, ALU.subtract, ALU.min)
            p.V.tensor_scalar(xt[0:nk, 0:tt], xt[0:nk, 0:tt], -QMAX - 1.0, sbc[0:nk, 0:1], ALU.max, ALU.mult)
            nc.scalar.wait_ge(cv.h, cv.n)
            nc.scalar.wait_ge(st.h, st.n)
            p.S.copy(xb[0:nk, 0:tt], xt[0:nk, 0:tt])
            nc.sync.wait_ge(cs.h, cs.n)
            st.inc(nc.sync.dma_start(out=ob[k0:k0 + nk, t0:t0 + tt], in_=xb[0:nk, 0:tt]), 16)
            if out_f32:
                nc.sync.wait_ge(cv.h, cv.n)
                st.inc(nc.sync.dma_start(out=of[k0:k0 + nk, t0:t0 + tt], in_=xt[0:nk, 0:tt]), 16)
    nc.sync.wait_ge(st.h, st.n)


def build_ffn_mid(p, *, R=3072, T=TPC):
    nc = p.nc
    TT = 512
    QMAX = 32767.0
    up = p.din("up", (R, T))
    gate = p.din("gate", (R, T))
    pm_u = p.din("pm_up8", (1, 8))
    pm_g = p.din("pm_gate8", (1, 8))
    upq = p.dout("upq", (R, T))
    g2p = p.dout("g2p", (R, T))
    pmo = p.dout("pm_g2", (1, 8))
    sc_b = p.dint("sc_b", (1, 4))
    pmax_bounce = p.dint("pmax_bounce", (128, 1))
    ld, cv, cs, st = (p.sem(s) for s in ("ld", "cv", "cs", "st"))
    pm = p.sbuf("pm", [1, 8], F32)
    sst = p.sbuf("sst", [1, 4], F32)
    qm = p.sbuf("qm", [1, 1], F32)
    sbc = p.sbuf("sbc", [128, 4], F32)
    tu = p.sbuf("tu", [128, TT], F32)
    tg = p.sbuf("tg", [128, TT], F32)
    ts = p.sbuf("ts", [128, TT], F32)
    mx = p.sbuf("mx", [128, 1], F32)
    mx2 = p.sbuf("mx2", [128, 1], F32)
    mxr = p.sbuf("mxr", [1, 128], F32)
    mxr8 = p.sbuf("mxr8", [1, 8], F32)
    for j, pmd in ((0, pm_u), (2, pm_g)):
        ld.inc(nc.sync.dma_start(out=pm[:], in_=pmd), 16)
        nc.vector.wait_ge(ld.h, ld.n)
        p.V.reduce_max(qm[0:1, 0:1], pm[0:1, :], axis=AX)
        p.V.tensor_scalar(sst[0:1, j:j + 1], qm[0:1, 0:1], 1.0 / QMAX, 1e-12, ALU.mult, ALU.max)
        p.V.reciprocal(sst[0:1, j + 1:j + 2], sst[0:1, j:j + 1])
    nc.sync.wait_ge(cv.h, cv.n)
    st.inc(nc.sync.dma_start(out=sc_b.ap(), in_=sst[:]), 16)
    nc.sync.wait_ge(st.h, st.n)
    ld.inc(nc.sync.dma_start(out=sbc[:], in_=sc_b.ap().partition_broadcast(128)), 16)
    first = True
    for ki, (k0, nk) in enumerate(chunks(R, 128)):
        for ti, (t0, tt) in enumerate(chunks(T, TT)):
            nc.sync.wait_ge(cv.h, cv.n)
            nc.sync.wait_ge(cs.h, cs.n)
            nc.sync.wait_ge(st.h, st.n)
            ld.inc(nc.sync.dma_start(out=tu[0:nk, 0:tt], in_=up[k0:k0 + nk, t0:t0 + tt]), 16)
            ld.inc(nc.sync.dma_start(out=tg[0:nk, 0:tt], in_=gate[k0:k0 + nk, t0:t0 + tt]), 16)
            nc.vector.wait_ge(ld.h, ld.n)
            p.V.tensor_scalar(tu[0:nk, 0:tt], tu[0:nk, 0:tt], sbc[0:nk, 1:2], RND, ALU.mult, ALU.add)
            p.V.tensor_scalar(tu[0:nk, 0:tt], tu[0:nk, 0:tt], RND, QMAX, ALU.subtract, ALU.min)
            p.V.tensor_scalar(tu[0:nk, 0:tt], tu[0:nk, 0:tt], -QMAX - 1.0, sbc[0:nk, 0:1], ALU.max, ALU.mult)
            p.V.tensor_scalar(tg[0:nk, 0:tt], tg[0:nk, 0:tt], sbc[0:nk, 3:4], RND, ALU.mult, ALU.add)
            p.V.tensor_scalar(tg[0:nk, 0:tt], tg[0:nk, 0:tt], RND, QMAX, ALU.subtract, ALU.min)
            p.V.tensor_scalar(tg[0:nk, 0:tt], tg[0:nk, 0:tt], -QMAX - 1.0, sbc[0:nk, 2:3], ALU.max, ALU.mult)
            nc.scalar.wait_ge(cv.h, cv.n)
            p.S.activation(ts[0:nk, 0:tt], tg[0:nk, 0:tt], AF.Sigmoid)
            nc.vector.wait_ge(cs.h, cs.n)
            p.V.tensor_scalar(ts[0:nk, 0:tt], ts[0:nk, 0:tt], 65536.0, RND, ALU.mult, ALU.add)
            p.V.tensor_scalar(ts[0:nk, 0:tt], ts[0:nk, 0:tt], RND, 65535.0, ALU.subtract, ALU.min)
            p.V.tensor_scalar(ts[0:nk, 0:tt], ts[0:nk, 0:tt], 0.0, 1.0 / 65536.0, ALU.max, ALU.mult)
            p.V.tensor_tensor(tg[0:nk, 0:tt], tg[0:nk, 0:tt], ts[0:nk, 0:tt], ALU.mult)
            if first:
                p.V.reduce_max(mx[0:nk, 0:1], tg[0:nk, 0:tt], axis=AX, apply_absolute_value=True)
                first = False
            else:
                p.V.reduce_max(mx2[0:nk, 0:1], tg[0:nk, 0:tt], axis=AX, apply_absolute_value=True)
                p.V.tensor_tensor(mx[0:nk, 0:1], mx[0:nk, 0:1], mx2[0:nk, 0:1], ALU.max)
            nc.sync.wait_ge(cv.h, cv.n)
            st.inc(nc.sync.dma_start(out=upq[k0:k0 + nk, t0:t0 + tt], in_=tu[0:nk, 0:tt]), 16)
            st.inc(nc.sync.dma_start(out=g2p[k0:k0 + nk, t0:t0 + tt], in_=tg[0:nk, 0:tt]), 16)
    nc.sync.wait_ge(cv.h, cv.n)
    st.inc(nc.sync.dma_start(out=pmax_bounce.ap(), in_=mx[:]), 16)
    nc.sync.wait_ge(st.h, st.n)
    ld.inc(nc.sync.dma_start(out=mxr[0:1, :], in_=pmax_bounce.ap().rearrange("p x -> x p")), 16)
    nc.vector.wait_ge(ld.h, ld.n)
    p.V.memset(mxr8[0:1, :], 0.0)
    p.V.reduce_max(mxr8[0:1, 0:1], mxr[0:1, :], axis=AX)
    nc.sync.wait_ge(cv.h, cv.n)
    st.inc(nc.sync.dma_start(out=pmo, in_=mxr8[0:1, :]), 16)
    nc.sync.wait_ge(st.h, st.n)


def build_ffn_o(p, *, R=3072, T=TPC):
    nc = p.nc
    TT = 512
    QMAX = 32767.0
    g2pd = p.din("g2p", (R, T))
    upqd = p.din("upq", (R, T))
    pm8 = p.din("pm_g28", (1, 8))
    opre = p.dout("o_pre", (R, T))
    pmo = p.dout("pm_o", (1, 8))
    sc_b = p.dint("sc_b", (1, 2))
    pmax_bounce = p.dint("pmax_bounce", (128, 1))
    ld, cv, st = (p.sem(s) for s in ("ld", "cv", "st"))
    pm = p.sbuf("pm", [1, 8], F32)
    sst = p.sbuf("sst", [1, 2], F32)
    qm = p.sbuf("qm", [1, 1], F32)
    sbc = p.sbuf("sbc", [128, 2], F32)
    tg = p.sbuf("tg", [128, TT], F32)
    tu = p.sbuf("tu", [128, TT], F32)
    mx = p.sbuf("mx", [128, 1], F32)
    mx2 = p.sbuf("mx2", [128, 1], F32)
    mxr = p.sbuf("mxr", [1, 128], F32)
    mxr8 = p.sbuf("mxr8", [1, 8], F32)
    ld.inc(nc.sync.dma_start(out=pm[:], in_=pm8), 16)
    nc.vector.wait_ge(ld.h, ld.n)
    p.V.reduce_max(qm[0:1, 0:1], pm[0:1, :], axis=AX)
    p.V.tensor_scalar(sst[0:1, 0:1], qm[0:1, 0:1], 1.0 / QMAX, 1e-12, ALU.mult, ALU.max)
    p.V.reciprocal(sst[0:1, 1:2], sst[0:1, 0:1])
    nc.sync.wait_ge(cv.h, cv.n)
    st.inc(nc.sync.dma_start(out=sc_b.ap(), in_=sst[:]), 16)
    nc.sync.wait_ge(st.h, st.n)
    ld.inc(nc.sync.dma_start(out=sbc[:], in_=sc_b.ap().partition_broadcast(128)), 16)
    first = True
    for ki, (k0, nk) in enumerate(chunks(R, 128)):
        for ti, (t0, tt) in enumerate(chunks(T, TT)):
            nc.sync.wait_ge(cv.h, cv.n)
            nc.sync.wait_ge(st.h, st.n)
            ld.inc(nc.sync.dma_start(out=tg[0:nk, 0:tt], in_=g2pd[k0:k0 + nk, t0:t0 + tt]), 16)
            ld.inc(nc.sync.dma_start(out=tu[0:nk, 0:tt], in_=upqd[k0:k0 + nk, t0:t0 + tt]), 16)
            nc.vector.wait_ge(ld.h, ld.n)
            p.V.tensor_scalar(tg[0:nk, 0:tt], tg[0:nk, 0:tt], sbc[0:nk, 1:2], RND, ALU.mult, ALU.add)
            p.V.tensor_scalar(tg[0:nk, 0:tt], tg[0:nk, 0:tt], RND, QMAX, ALU.subtract, ALU.min)
            p.V.tensor_scalar(tg[0:nk, 0:tt], tg[0:nk, 0:tt], -QMAX - 1.0, sbc[0:nk, 0:1], ALU.max, ALU.mult)
            p.V.tensor_tensor(tg[0:nk, 0:tt], tg[0:nk, 0:tt], tu[0:nk, 0:tt], ALU.mult)
            if first:
                p.V.reduce_max(mx[0:nk, 0:1], tg[0:nk, 0:tt], axis=AX, apply_absolute_value=True)
                first = False
            else:
                p.V.reduce_max(mx2[0:nk, 0:1], tg[0:nk, 0:tt], axis=AX, apply_absolute_value=True)
                p.V.tensor_tensor(mx[0:nk, 0:1], mx[0:nk, 0:1], mx2[0:nk, 0:1], ALU.max)
            nc.sync.wait_ge(cv.h, cv.n)
            st.inc(nc.sync.dma_start(out=opre[k0:k0 + nk, t0:t0 + tt], in_=tg[0:nk, 0:tt]), 16)
    nc.sync.wait_ge(cv.h, cv.n)
    st.inc(nc.sync.dma_start(out=pmax_bounce.ap(), in_=mx[:]), 16)
    nc.sync.wait_ge(st.h, st.n)
    ld.inc(nc.sync.dma_start(out=mxr[0:1, :], in_=pmax_bounce.ap().rearrange("p x -> x p")), 16)
    nc.vector.wait_ge(ld.h, ld.n)
    p.V.memset(mxr8[0:1, :], 0.0)
    p.V.reduce_max(mxr8[0:1, 0:1], mxr[0:1, :], axis=AX)
    nc.sync.wait_ge(cv.h, cv.n)
    st.inc(nc.sync.dma_start(out=pmo, in_=mxr8[0:1, :]), 16)
    nc.sync.wait_ge(st.h, st.n)



def build_scan1(p, *, R=1024, T=L, NST=16, ag=False):
    nc = p.nc
    TT = 2048
    DC = R // 128
    ins = {}
    for br in ("f", "b"):
        if ag:
            ins[f"u_{br}"] = p.din(f"ag_u{br}", (2 * 2048, TPC))
            ins[f"xdbl_{br}"] = p.din(f"ag_xdbl_{br}", (192, TPC))
        else:
            ins[f"u_{br}"] = p.din(f"u_{br}", (R, T))
            ins[f"dtr_{br}"] = p.din(f"dtr_{br}", (64, T))
        ins[f"wdt_{br}"] = p.din(f"wdt_{br}", (64, R))
        ins[f"bdt_{br}"] = p.din(f"bdt_{br}", (128, R // 128))
        ins[f"Alog_{br}"] = p.din(f"Alog_{br}", (128, DC * NST))
        ins[f"Dp_{br}"] = p.din(f"Dp_{br}", (128, R // 128))
    if ag:
        zd = p.din("ag_z", (2 * 2048, TPC))
        idxd = p.din("sc1_idx", (128, 2 * DC), I32)
        ysum = p.dout("y_sum", (2 * R, T // 2))
        ysv = ysum.rearrange("(d h) t -> d (h t)", h=2)
    else:
        bcd = p.din("bc_all", (64, T))
        zd = p.din("z", (R, T))
        ysum = p.dout("y_sum", (R, T))
        ysv = ysum

    ld, lw, cv, cs, ct, st = (p.sem(s) for s in ("ld", "lw", "cv", "cs", "ct", "st"))
    wdtb = {br: p.sbuf(f"wdtb{br}", [64, R], BF16) for br in ("f", "b")}
    bdt = {br: p.sbuf(f"bdt{br}", [128, DC], F32) for br in ("f", "b")}
    Atl = {br: p.sbuf(f"Atl{br}", [128, DC * NST], F32) for br in ("f", "b")}
    Dpt = {br: p.sbuf(f"Dpt{br}", [128, DC], F32) for br in ("f", "b")}
    dtrb = {br: p.sbuf(f"dtrb{br}", [64, T], BF16) for br in ("f", "b")}

    with p.nc.sbuf_tensor(p.prefix + "tmp64", [64, T], F32) as tmp64:
        for br in ("f", "b"):
            lw.inc(nc.sync.dma_start(out=bdt[br][:, :], in_=ins[f"bdt_{br}"]), 16)
            lw.inc(nc.sync.dma_start(out=Atl[br][:], in_=ins[f"Alog_{br}"]), 16)
            lw.inc(nc.sync.dma_start(out=Dpt[br][:, :], in_=ins[f"Dp_{br}"]), 16)
            nc.sync.wait_ge(cv.h, cv.n)
            if ag:
                lw.inc(nc.sync.dma_start(out=tmp64[:, 0:TPC], in_=ins[f"xdbl_{br}"][0:64, :]), 16)
                lw.inc(nc.sync.dma_start(out=tmp64[:, TPC:T], in_=ins[f"xdbl_{br}"][96:160, :]), 16)
            else:
                lw.inc(nc.sync.dma_start(out=tmp64[:], in_=ins[f"dtr_{br}"]), 16)
            nc.vector.wait_ge(lw.h, lw.n)
            p.V.tensor_copy(dtrb[br][:], tmp64[:])
            nc.sync.wait_ge(cv.h, cv.n)
            lw.inc(nc.sync.dma_start(out=tmp64[0:64, 0:R], in_=ins[f"wdt_{br}"]), 16)
            nc.vector.wait_ge(lw.h, lw.n)
            p.V.tensor_copy(wdtb[br][:], tmp64[0:64, 0:R])
        nc.scalar.wait_ge(lw.h, lw.n)
        for br in ("f", "b"):
            p.S.activation(Atl[br][:], Atl[br][:], AF.Exp)
            nc.vector.wait_ge(cs.h, p.sem("cs").n)
            p.V.tensor_scalar(Atl[br][:], Atl[br][:], -1.0, None, ALU.mult)
    prep_v = cv.n

    scr = [p.sbuf(f"scr{i}", [128, 512], F32) for i in range(2)]
    du = p.sbuf("du", [128, T], F32)
    dl = p.sbuf("dl", [128, T], F32)
    ut = p.sbuf("ut", [128, T], F32)
    at = p.sbuf("at", [128, T], F32)
    bt = p.sbuf("bt", [128, T], F32)
    ht = p.sbuf("ht", [128, T], F32)
    Brep = [p.sbuf(f"Brep{i}", [128, T], F32) for i in range(2)]
    yac = p.sbuf("yac", [128, T], F32)
    ytot = p.sbuf("ytot", [128, T], F32)
    idxt = p.sbuf("idxs", [128, 2 * DC], I32) if ag else None
    ps = [p.psum(f"ps{i}", [128, 512]) for i in range(2)]

    if ag:
        ld.inc(nc.sync.dma_start(out=idxt[:], in_=idxd), 16)
        nc.gpsimd.wait_ge(ld.h, ld.n)

    def load_rows(dst, srcd, ci, cols=None):
        if ag:
            nc.gpsimd.wait_ge(cv.h, cv.n)
            nc.gpsimd.wait_ge(cs.h, p.sem("cs").n)
            nc.gpsimd.wait_ge(st.h, st.n)
            cg = p.sem("cg")
            for half in range(2):
                cg.inc(nc.gpsimd.indirect_dma_start(
                    out=dst[:, half * TPC:(half + 1) * TPC], out_offset=None, in_=srcd,
                    in_offset=bass.IndirectOffsetOnAxis(ap=idxt[:, half * DC + ci:half * DC + ci + 1], axis=0)), 16)
            return cg
        ld.inc(nc.sync.dma_start(out=dst[:], in_=srcd[ci * 128:(ci + 1) * 128, :]), 16)
        return ld

    def bc_load(dst_ap, br, kind, n, t0, tt):
        bi = 0 if br == "f" else 1
        if ag:
            xb = ins[f"xdbl_{br}"]
            r0 = (t0 // TPC) * 96
            ld.inc(nc.sync.dma_start(out=dst_ap,
                                     in_=xb[r0 + 64 + 16 * kind + n:r0 + 65 + 16 * kind + n, 0:tt].partition_broadcast(128)), 16)
        else:
            ld.inc(nc.sync.dma_start(out=dst_ap,
                                     in_=bcd[bi * 32 + 16 * kind + n:bi * 32 + 16 * kind + n + 1, t0:t0 + tt].partition_broadcast(128)), 16)

    for ci in range(DC):
        nc.sync.wait_ge(cv.h, cv.n)
        nc.sync.wait_ge(cs.h, p.sem("cs").n)
        nc.sync.wait_ge(st.h, st.n)
        first_br = True
        for br in ("f", "b"):
            rev = (br == "b")
            nc.sync.wait_ge(cv.h, cv.n)
            nc.sync.wait_ge(cs.h, p.sem("cs").n)
            usem = load_rows(ut, ins[f"u_{br}"], ci)
            u_sem, u_ld = usem, usem.n
            nc.vector.wait_ge(cs.h, p.sem("cs").n)
            for ti, (t0, tt) in enumerate(chunks(T, 512)):
                pb = ti % 2
                nc.tensor.wait_ge(cv.h, cv.n)
                ct.inc(nc.tensor.matmul(ps[pb][:, 0:tt], wdtb[br][:, ci * 128:(ci + 1) * 128],
                                        dtrb[br][:, t0:t0 + tt], start=True, stop=True), 1)
                nc.vector.wait_ge(ct.h, ct.n)
                p.V.tensor_scalar(dl[:, t0:t0 + tt], ps[pb][:, 0:tt], bdt[br][:, ci:ci + 1], None, ALU.add)
                nc.scalar.wait_ge(cv.h, cv.n)
                p.S.activation(scr[0][:, 0:tt], dl[:, t0:t0 + tt], AF.Abs)
                p.S.activation(scr[0][:, 0:tt], scr[0][:, 0:tt], AF.Exp, scale=-1.0)
                p.S.activation(scr[0][:, 0:tt], scr[0][:, 0:tt], AF.Ln, bias=1.0)
                nc.vector.wait_ge(cs.h, p.sem("cs").n)
                p.V.tensor_scalar(scr[1][:, 0:tt], dl[:, t0:t0 + tt], 0.0, None, ALU.max)
                p.V.tensor_add(dl[:, t0:t0 + tt], scr[0][:, 0:tt], scr[1][:, 0:tt])
            nc.vector.wait_ge(u_sem.h, u_ld)
            p.V.tensor_tensor(du[:], dl[:], ut[:], ALU.mult)
            bi = 0 if br == "f" else 1
            nc.sync.wait_ge(cv.h, cv.n)
            bc_load(Brep[0][:, 0:TPC], br, 0, 0, 0, TPC)
            bc_load(Brep[0][:, TPC:T], br, 0, 0, TPC, TPC)
            bmark = [ld.n, 0]
            cmark = [0, 0]
            exp_mark = 0
            for n in range(NST):
                j = n % 2
                if n + 1 < NST:
                    nc.sync.wait_ge(cv.h, cmark[1 - j])
                    bc_load(Brep[1 - j][:, 0:TPC], br, 0, n + 1, 0, TPC)
                    bc_load(Brep[1 - j][:, TPC:T], br, 0, n + 1, TPC, TPC)
                    bmark[1 - j] = ld.n
                nc.scalar.wait_ge(cv.h, cv.n)
                p.S.activation(at[:], dl[:], AF.Exp, scale=Atl[br][:, ci * NST + n:ci * NST + n + 1])
                exp_mark = p.sem("cs").n
                nc.vector.wait_ge(ld.h, bmark[j])
                nc.vector.wait_ge(cs.h, exp_mark)
                p.V.tensor_tensor(bt[:], du[:], Brep[j][:], ALU.mult)
                if rev:
                    p.V.tensor_tensor_scan(ht[:, ::-1], at[:, ::-1], bt[:, ::-1], 0.0, ALU.mult, ALU.add)
                else:
                    p.V.tensor_tensor_scan(ht[:], at[:], bt[:], 0.0, ALU.mult, ALU.add)
                nc.sync.wait_ge(cv.h, cv.n)
                bc_load(Brep[j][:, 0:TPC], br, 1, n, 0, TPC)
                bc_load(Brep[j][:, TPC:T], br, 1, n, TPC, TPC)
                nc.vector.wait_ge(ld.h, ld.n)
                p.V.tensor_tensor(ht[:], ht[:], Brep[j][:], ALU.mult)
                if n == 0:
                    p.V.tensor_copy(yac[:], ht[:])
                else:
                    p.V.tensor_add(yac[:], yac[:], ht[:])
                cmark[j] = cv.n
            nc.vector.wait_ge(u_sem.h, u_ld)
            p.V.scalar_tensor_tensor(yac[:], ut[:], Dpt[br][:, ci:ci + 1], yac[:], ALU.mult, ALU.add)
            nc.sync.wait_ge(cv.h, cv.n)
            zsem = load_rows(ht, zd, ci)
            nc.scalar.wait_ge(zsem.h, zsem.n)
            p.S.activation(ht[:], ht[:], AF.Silu)
            nc.vector.wait_ge(cs.h, p.sem("cs").n)
            p.V.tensor_tensor(yac[:], yac[:], ht[:], ALU.mult)
            if first_br:
                p.V.tensor_copy(ytot[:], yac[:])
                first_br = False
            else:
                p.V.tensor_add(ytot[:], ytot[:], yac[:])
        nc.sync.wait_ge(cv.h, cv.n)
        st.inc(nc.sync.dma_start(out=ysv[ci * 128:(ci + 1) * 128, :], in_=ytot[:]), 16)
    nc.sync.wait_ge(st.h, st.n)


def build_f1qkv(p, *, TH=2560, NTOK=2304, QOFF=257):
    nc = p.nc
    TT = 512
    xd = p.din("x", (DM, TH))
    wqk = p.din("wqk", (DM, 2048))
    wv = p.din("wv", (DM, DM))
    cosq = p.din("cosq", (128, 2048))
    sinq = p.din("sinq", (128, 2048))
    cosk = p.din("cosk", (128, NTOK))
    sink = p.din("sink", (128, NTOK))
    sclq = p.din("sclq", (2, 128))
    sclk = p.din("sclk", (2, 128))
    indd = p.din("ind2", (128, 2))
    rotm = p.din("rotm", (128, 128))
    q2b = p.dout("q2b", (DM, 2048), BF16)
    k2b = p.dout("k2b", (DM, NTOK), BF16)
    v_tok = p.dout("v_tok", (NTOK, DM), BF16)

    ld, lw, cv, cs, ct, st = (p.sem(s) for s in ("ld", "lw", "cv", "cs", "ct", "st"))
    xt = p.sbuf("xt", [128, 8 * TH], F32)
    sxb = p.sbuf("sxb", [128, 8 * TH], BF16)
    wtb = [p.sbuf(f"wtb{i}", [128, 8 * 128], BF16) for i in range(2)]
    wtf = p.sbuf("wtf", [128, 8 * 128], F32)
    vt_w = p.sbuf("vt_w", [128, 8 * DM], BF16)
    ind = p.sbuf("ind", [128, 2], F32)
    scl = p.sbuf("scl", [2, 2 * 128], F32)
    rot = p.sbuf("rot", [128, 128], BF16)
    rotf = p.sbuf("rotf", [128, 128], F32)
    tabc = p.sbuf("tabc", [128, TT], F32)
    tabs_ = p.sbuf("tabs", [128, TT], F32)
    qf = p.sbuf("qf", [128, TT], F32)
    qsq = p.sbuf("qsq", [128, TT], F32)
    q1b = p.sbuf("q1b", [128, TT], BF16)
    nrm = p.sbuf("nrm", [2, TT], F32)
    q2t = p.sbuf("q2t", [128, TT], F32)
    q2c = p.sbuf("q2c", [128, TT], BF16)
    vevac = p.sbuf("vevac", [128, TT], BF16)
    ps_a = p.psum("ps_a", [128, TT])
    ps_n = p.psum("ps_n", [2, TT])
    ps_r = p.psum("ps_r", [128, TT])
    ps_b = p.psum("ps_b", [128, TT])

    for ki in range(8):
        ld.inc(nc.sync.dma_start(out=xt[:, ki * TH:(ki + 1) * TH], in_=xd[ki * 128:(ki + 1) * 128, :]), 16)
    ld.inc(nc.sync.dma_start(out=rotf[:], in_=rotm), 16)
    ld.inc(nc.sync.dma_start(out=ind[:], in_=indd), 16)
    ld.inc(nc.sync.dma_start(out=scl[0:2, 0:128], in_=sclq), 16)
    ld.inc(nc.sync.dma_start(out=scl[0:2, 128:256], in_=sclk), 16)
    nc.vector.wait_ge(ld.h, ld.n)
    p.V.tensor_copy(rot[:], rotf[:])
    nc.scalar.wait_ge(ld.h, ld.n)
    for ki in range(8):
        src_off = ki * TH + (1 if ki < 4 else 0)
        p.S.copy(sxb[:, ki * TH + 1:ki * TH + 1 + NTOK], xt[:, src_off:src_off + NTOK])
    sx_mark = p.sem("cs").n

    for ki in range(8):
        nc.scalar.wait_ge(lw.h, lw.n)
        lw.inc(nc.scalar.dma_start(out=wtf[:, 0:8 * 128], in_=wv[ki * 128:(ki + 1) * 128, :].rearrange("p (a b) -> p (a b)", a=8) if False else wv[ki * 128:(ki + 1) * 128, :]), 16)
        nc.scalar.wait_ge(lw.h, lw.n)
        p.S.copy(vt_w[:, ki * DM:(ki + 1) * DM], wtf[:, 0:DM])
    vt_wmark = p.sem("cs").n

    nc.tensor.wait_ge(cs.h, vt_wmark)
    for tc, (tk0, ntk) in enumerate(chunks(NTOK, 128)):
        for nt in range(2):
            nc.tensor.wait_ge(cv.h, cv.n)
            ins_mm = None
            for ki in range(8):
                ins_mm = nc.tensor.matmul(ps_a[0:ntk, 0:512],
                                          sxb[:, ki * TH + 1 + tk0:ki * TH + 1 + tk0 + ntk],
                                          vt_w[:, ki * DM + nt * 512:ki * DM + nt * 512 + 512],
                                          start=(ki == 0), stop=(ki == 7))
            ct.inc(ins_mm, 1)
            nc.vector.wait_ge(ct.h, ct.n)
            nc.vector.wait_ge(st.h, st.n)
            p.V.tensor_copy(vevac[0:ntk, 0:512], ps_a[0:ntk, 0:512])
            nc.sync.wait_ge(cv.h, cv.n)
            st.inc(nc.sync.dma_start(out=v_tok[tk0:tk0 + ntk, nt * 512:nt * 512 + 512], in_=vevac[0:ntk, 0:512]), 16)

    for qk in range(2):
        ncols = 2048 if qk == 0 else NTOK
        coff = QOFF if qk == 0 else 1
        cosd, sind = (cosq, sinq) if qk == 0 else (cosk, sink)
        outd = q2b if qk == 0 else k2b
        for ti, (t0, tt) in enumerate(chunks(ncols, TT)):
            for ci in range(8):
                buf = ci % 2
                nc.vector.wait_ge(cs.h, p.sem("cs").n)
                nc.scalar.wait_ge(ct.h, ct.n)
                nc.scalar.wait_ge(cv.h, cv.n)
                for kj in range(8):
                    lw.inc(nc.scalar.dma_start(out=wtf[0:128, kj * 128:kj * 128 + 128],
                                               in_=wqk[kj * 128:(kj + 1) * 128,
                                                       qk * 1024 + ci * 128:qk * 1024 + ci * 128 + 128]), 16)
                nc.scalar.wait_ge(lw.h, lw.n)
                p.S.copy(wtb[buf][:, :], wtf[:, :])
                wmark = p.sem("cs").n
                nc.tensor.wait_ge(cs.h, wmark)
                nc.tensor.wait_ge(cv.h, cv.n)
                ins_mm = None
                for kj in range(8):
                    ins_mm = nc.tensor.matmul(ps_a[:, 0:tt], wtb[buf][:, kj * 128:kj * 128 + 128],
                                              sxb[:, kj * TH + coff + t0:kj * TH + coff + t0 + tt],
                                              start=(kj == 0), stop=(kj == 7))
                ct.inc(ins_mm, 1)
                nc.vector.wait_ge(ct.h, ct.n)
                p.V.tensor_copy(qf[:, 0:tt], ps_a[:, 0:tt])
                nc.scalar.wait_ge(cv.h, cv.n)
                p.S.activation(qsq[:, 0:tt], qf[:, 0:tt], AF.Square)
                nc.tensor.wait_ge(cs.h, p.sem("cs").n)
                ct.inc(nc.tensor.matmul(ps_n[0:2, 0:tt], ind[:, 0:2], qsq[:, 0:tt], start=True, stop=True), 1)
                nc.vector.wait_ge(ct.h, ct.n)
                p.V.tensor_scalar(nrm[0:2, 0:tt], ps_n[0:2, 0:tt], 1e-30, None, ALU.max)
                nc.scalar.wait_ge(cv.h, cv.n)
                p.S.activation(nrm[0:2, 0:tt], nrm[0:2, 0:tt], AF.Ln)
                p.S.activation(nrm[0:2, 0:tt], nrm[0:2, 0:tt], AF.Exp, scale=0.5)
                nc.vector.wait_ge(cs.h, p.sem("cs").n)
                p.V.tensor_scalar(nrm[0:2, 0:tt], nrm[0:2, 0:tt], 1e-12, None, ALU.max)
                p.V.reciprocal(nrm[0:2, 0:tt], nrm[0:2, 0:tt])
                nc.tensor.wait_ge(cv.h, cv.n)
                ct.inc(nc.tensor.matmul(ps_r[:, 0:tt], scl[0:2, qk * 128:qk * 128 + 128], nrm[0:2, 0:tt],
                                        start=True, stop=True), 1)
                nc.vector.wait_ge(ct.h, ct.n)
                p.V.tensor_tensor(qf[:, 0:tt], qf[:, 0:tt], ps_r[:, 0:tt], ALU.mult)
                nc.scalar.wait_ge(cv.h, cv.n)
                p.S.copy(q1b[:, 0:tt], qf[:, 0:tt])
                nc.tensor.wait_ge(cs.h, p.sem("cs").n)
                ct.inc(nc.tensor.matmul(ps_b[:, 0:tt], rot[:, :], q1b[:, 0:tt], start=True, stop=True), 1)
                rot_mark = ct.n
                nc.sync.wait_ge(cv.h, cv.n)
                ld.inc(nc.sync.dma_start(out=tabc[:, 0:tt], in_=cosd[:, t0:t0 + tt]), 16)
                ld.inc(nc.sync.dma_start(out=tabs_[:, 0:tt], in_=sind[:, t0:t0 + tt]), 16)
                nc.vector.wait_ge(ld.h, ld.n)
                nc.vector.wait_ge(ct.h, rot_mark)
                p.V.tensor_tensor(q2t[:, 0:tt], qf[:, 0:tt], tabc[:, 0:tt], ALU.mult)
                p.V.tensor_tensor(qsq[:, 0:tt], ps_b[:, 0:tt], tabs_[:, 0:tt], ALU.mult)
                p.V.tensor_add(q2t[:, 0:tt], q2t[:, 0:tt], qsq[:, 0:tt])
                nc.scalar.wait_ge(cv.h, cv.n)
                nc.scalar.wait_ge(st.h, st.n)
                p.S.copy(q2c[:, 0:tt], q2t[:, 0:tt])
                nc.sync.wait_ge(cs.h, p.sem("cs").n)
                st.inc(nc.sync.dma_start(out=outd[ci * 128:(ci + 1) * 128, t0:t0 + tt], in_=q2c[:, 0:tt]), 16)
    nc.sync.wait_ge(st.h, st.n)


def build_f1attn(p, *, NTOK=2304):
    nc = p.nc
    q2 = p.din("q2b", (DM, 2048), BF16)
    k2 = p.din("k2b", (DM, NTOK), BF16)
    vt = p.din("v_tok", (NTOK, DM), BF16)
    maskd = p.din("masks", (8 * 2 * 128, 512))
    wod = p.din("wo", (DM, DM))
    xres = p.din("xres", (DM, 2048))
    xo = p.dout("xo", (DM, 2048))

    ld, lw, cv, cs, ct, st = (p.sem(s) for s in ("ld", "lw", "cv", "cs", "ct", "st"))
    ts_ = p.sem("ts")
    q2t = p.sbuf("q2t", [128, 8 * 2048], BF16)
    k2t = p.sbuf("k2t", [128, 8 * NTOK], BF16)
    vtt = p.sbuf("vtt", [128, 18 * DM], BF16)
    mt = [p.sbuf(f"mt{i}", [128, 512], F32) for i in range(2)]
    sm = [p.sbuf(f"sm{i}", [128, 512], F32) for i in range(2)]
    mx = [p.sbuf(f"mx{i}", [128, 1], F32) for i in range(2)]
    den = [p.sbuf(f"den{i}", [128, 1], F32) for i in range(2)]
    pb_ = [p.sbuf(f"pb{i}", [128, 512], BF16) for i in range(2)]
    pT = [p.sbuf(f"pT{i}", [128, 4 * 128], BF16) for i in range(2)]
    aot = p.sbuf("aot", [128, 8 * 2048], BF16)
    wtf = p.sbuf("wtf", [128, 8 * 128], F32)
    wtb = [p.sbuf(f"wtb{i}", [128, 8 * 128], BF16) for i in range(2)]
    rtile = p.sbuf("rtile", [128, 512], F32)
    otile = [p.sbuf(f"otile{i}", [128, 512], F32) for i in range(2)]
    ps_s = [p.psum(f"ps_s{i}", [128, 512]) for i in range(2)]
    ps_o = [p.psum(f"ps_o{i}", [64, 128]) for i in range(2)]
    ps_w = [p.psum(f"ps_w{i}", [128, 512]) for i in range(2)]

    for ki in range(8):
        ld.inc(nc.sync.dma_start(out=q2t[:, ki * 2048:(ki + 1) * 2048], in_=q2[ki * 128:(ki + 1) * 128, :]), 16)
        ld.inc(nc.sync.dma_start(out=k2t[:, ki * NTOK:(ki + 1) * NTOK], in_=k2[ki * 128:(ki + 1) * 128, :]), 16)
    for tc in range(18):
        ld.inc(nc.sync.dma_start(out=vtt[:, tc * DM:(tc + 1) * DM], in_=vt[tc * 128:(tc + 1) * 128, :]), 16)
    in_ld = ld.n
    nc.tensor.wait_ge(ld.h, in_ld)

    NB = 2
    sim_mark = [0] * NB
    stt_mark = [0] * NB
    exp_mark = [0] * NB
    scl_mark = [0] * NB
    ts_mark = [0] * NB
    av_mark = [0] * NB
    evac_mark = [0] * NB
    it = 0
    m_lds = {}
    for w in range(8):
        for ih in range(2):
            mb = mt[(w * 2 + ih) % 2]
            nc.sync.wait_ge(cv.h, max(stt_mark))
            ld.inc(nc.sync.dma_start(out=mb[:], in_=maskd[(w * 2 + ih) * 128:(w * 2 + ih) * 128 + 128, :]), 16)
            m_ld = ld.n
            for h in range(16):
                j = it % NB
                ci, hh = h // 2, h % 2
                po = 64 * hh
                kvb = w * 256
                qcol = w * 256 + ih * 128
                nc.tensor.wait_ge(ld.h, in_ld)
                nc.tensor.wait_ge(cv.h, stt_mark[j])
                ct.inc(nc.tensor.matmul(ps_s[j][0:128, 0:512],
                                        q2t[po:po + 64, ci * 2048 + qcol:ci * 2048 + qcol + 128],
                                        k2t[po:po + 64, ci * NTOK + kvb:ci * NTOK + kvb + 512],
                                        start=True, stop=True), 1)
                sim_mark[j] = ct.n
                nc.vector.wait_ge(ld.h, m_ld)
                nc.vector.wait_ge(ct.h, sim_mark[j])
                nc.vector.wait_ge(cs.h, exp_mark[j])
                p.V.scalar_tensor_tensor(sm[j][:], ps_s[j][0:128, 0:512], 8.0, mb[:], ALU.mult, ALU.add)
                p.V.reduce_max(mx[j][:], sm[j][:], axis=AX)
                p.V.tensor_scalar(mx[j][:], mx[j][:], -1.0, None, ALU.mult)
                stt_mark[j] = cv.n
                nc.scalar.wait_ge(cv.h, stt_mark[j])
                nc.scalar.wait_ge(ts_.h, ts_mark[j])
                p.S.activation(pb_[j][:], sm[j][:], AF.Exp, bias=mx[j][:, 0:1], accum_out=den[j][:])
                exp_mark[j] = p.sem("cs").n
                nc.vector.wait_ge(cs.h, exp_mark[j])
                p.V.reciprocal(den[j][:], den[j][:])
                p.V.tensor_scalar(pb_[j][:], pb_[j][:], den[j][:, 0:1], None, ALU.mult)
                scl_mark[j] = cv.n
                nc.scalar.wait_ge(cv.h, scl_mark[j])
                nc.scalar.wait_ge(ct.h, av_mark[j])
                for jc in range(4):
                    ts_.inc(nc.scalar.dma_start_transpose(pT[j][:, jc * 128:jc * 128 + 128],
                                                          pb_[j][:, jc * 128:jc * 128 + 128]), 16)
                ts_mark[j] = ts_.n
                nc.tensor.wait_ge(ts_.h, ts_mark[j])
                nc.tensor.wait_ge(cv.h, evac_mark[j])
                ins_mm = None
                for jc in range(4):
                    tc = w * 2 + jc
                    ins_mm = nc.tensor.matmul(ps_o[j][0:64, 0:128],
                                              vtt[:, tc * DM + h * 64:tc * DM + h * 64 + 64],
                                              pT[j][:, jc * 128:jc * 128 + 128],
                                              start=(jc == 0), stop=(jc == 3))
                ct.inc(ins_mm, 1)
                av_mark[j] = ct.n
                nc.vector.wait_ge(ct.h, av_mark[j])
                p.V.tensor_copy(aot[po:po + 64, ci * 2048 + qcol:ci * 2048 + qcol + 128], ps_o[j][0:64, 0:128])
                evac_mark[j] = cv.n
                it += 1
    ao_mark = cv.n

    TT = 512
    q = 0
    ps_free = [0, 0]
    wt_last = [0, 0]
    for mi in range(8):
        buf = mi % 2
        nc.scalar.wait_ge(ct.h, wt_last[buf])
        for kj in range(8):
            lw.inc(nc.scalar.dma_start(out=wtf[:, kj * 128:kj * 128 + 128],
                                       in_=wod[kj * 128:(kj + 1) * 128, mi * 128:mi * 128 + 128]), 16)
        nc.scalar.wait_ge(lw.h, lw.n)
        p.S.copy(wtb[buf][:, :], wtf[:, :])
        wmark = p.sem("cs").n
        for ti in range(4):
            t0 = ti * TT
            pbk = q % 2
            nc.sync.wait_ge(cv.h, cv.n)
            ld.inc(nc.sync.dma_start(out=rtile[:, :], in_=xres[mi * 128:mi * 128 + 128, t0:t0 + TT]), 16)
            r_ld = ld.n
            nc.tensor.wait_ge(cs.h, wmark)
            nc.tensor.wait_ge(cv.h, max(ao_mark, ps_free[pbk]))
            ins_mm = None
            for kj in range(8):
                ins_mm = nc.tensor.matmul(ps_w[pbk][:, 0:TT], wtb[buf][:, kj * 128:kj * 128 + 128],
                                          aot[:, kj * 2048 + t0:kj * 2048 + t0 + TT],
                                          start=(kj == 0), stop=(kj == 7))
            ct.inc(ins_mm, 1)
            wt_last[buf] = ct.n
            nc.vector.wait_ge(ct.h, ct.n)
            nc.vector.wait_ge(ld.h, r_ld)
            nc.vector.wait_ge(st.h, st.n)
            p.V.tensor_tensor(otile[pbk][:, :], ps_w[pbk][:, 0:TT], rtile[:, :], ALU.add)
            ps_free[pbk] = cv.n
            nc.sync.wait_ge(cv.h, cv.n)
            st.inc(nc.sync.dma_start(out=xo[mi * 128:mi * 128 + 128, t0:t0 + TT], in_=otile[pbk][:, :]), 16)
            q += 1
    nc.sync.wait_ge(st.h, st.n)


def build_f1h(p, *, T=TPC):
    nc = p.nc
    TT = 512
    xd = p.din("x", (DM, T + 1))
    w1a = p.din("w1a", (DM, 2730))
    w1g = p.din("w1g", (DM, 2730))
    b1a = p.din("b1a", (2730, 1))
    b1g = p.din("b1g", (2730, 1))
    lngd = p.din("lng", (128, 8))
    lnbd = p.din("lnb", (128, 8))
    onesd = p.din("ones128", (128, 128))
    hgate = p.dout("hgate", (2730, T), BF16)

    ld, lw, cv, cs, ct, st = (p.sem(s) for s in ("ld", "lw", "cv", "cs", "ct", "st"))
    xt = p.sbuf("xt", [128, 8 * (T + 1)], F32)
    xnb = p.sbuf("xnb", [128, 8 * T], BF16)
    sq = [p.sbuf(f"sq{i}", [128, TT], F32) for i in range(2)]
    ones = p.sbuf("ones", [128, 128], F32)
    lng = p.sbuf("lngt", [128, 8], F32)
    lnb = p.sbuf("lnbt", [128, 8], F32)
    statm = p.sbuf("statm", [1, TT], F32)
    statr = p.sbuf("statr", [1, TT], F32)
    tvar = p.sbuf("tvar", [1, TT], F32)
    tmsq = p.sbuf("tmsq", [1, TT], F32)
    xo = p.sbuf("xo", [128, TT], F32)
    wtfa = p.sbuf("wtfa", [128, 8 * 128], F32)
    wtba = [p.sbuf(f"wtba{i}", [128, 8 * 128], BF16) for i in range(2)]
    wtbg = [p.sbuf(f"wtbg{i}", [128, 8 * 128], BF16) for i in range(2)]
    p.S.memzero(wtfa[:, :])
    bta = p.sbuf("bta", [128, 1], F32)
    btg = p.sbuf("btg", [128, 1], F32)
    gel = p.sbuf("gel", [128, TT], F32)
    hb = p.sbuf("hb", [128, TT], BF16)
    ps_s = p.psum("ps_s", [1, TT])
    ps_q = p.psum("ps_q", [1, TT])
    ps_b = p.psum("ps_b", [128, 2 * TT])
    ps_a = [p.psum(f"ps_a{i}", [128, TT]) for i in range(2)]
    ps_g = [p.psum(f"ps_g{i}", [128, TT]) for i in range(2)]

    ld.inc(nc.sync.dma_start(out=ones[:], in_=onesd), 16)
    ld.inc(nc.sync.dma_start(out=lng[:, :], in_=lngd), 16)
    ld.inc(nc.sync.dma_start(out=lnb[:, :], in_=lnbd), 16)
    for ki in range(8):
        ld.inc(nc.sync.dma_start(out=xt[:, ki * (T + 1):(ki + 1) * (T + 1)], in_=xd[ki * 128:(ki + 1) * 128, :]), 16)
    x_ld = ld.n

    for ti, (t0, tt) in enumerate(chunks(T, TT)):
        def sx(ki):
            off = ki * (T + 1) + (1 if ki < 4 else 0)
            return xt[:, off + t0:off + t0 + tt]
        nc.tensor.wait_ge(ld.h, x_ld)
        nc.tensor.wait_ge(cv.h, cv.n)
        ins_mm = None
        for ki in range(8):
            ins_mm = nc.tensor.matmul(ps_s[0:1, 0:tt], ones[:, 0:1], sx(ki), start=(ki == 0), stop=(ki == 7))
        ct.inc(ins_mm, 1)
        mm_sq = []
        for ki in range(8):
            sb = sq[ki % 2]
            nc.scalar.wait_ge(ld.h, x_ld)
            if ki >= 2:
                nc.scalar.wait_ge(ct.h, mm_sq[ki - 2])
            p.S.activation(sb[:, 0:tt], sx(ki), AF.Square)
            nc.tensor.wait_ge(cs.h, p.sem("cs").n)
            ct.inc(nc.tensor.matmul(ps_q[0:1, 0:tt], ones[:, 0:1], sb[:, 0:tt],
                                    start=(ki == 0), stop=(ki == 7)), 1)
            mm_sq.append(ct.n)
        nc.vector.wait_ge(ct.h, ct.n)
        p.V.tensor_scalar(statm[0:1, 0:tt], ps_s[0:1, 0:tt], 1.0 / DM, None, ALU.mult)
        p.V.tensor_scalar(tmsq[0:1, 0:tt], ps_q[0:1, 0:tt], 1.0 / DM, None, ALU.mult)
        p.V.tensor_tensor(tvar[0:1, 0:tt], statm[0:1, 0:tt], statm[0:1, 0:tt], ALU.mult)
        p.V.tensor_tensor(tvar[0:1, 0:tt], tmsq[0:1, 0:tt], tvar[0:1, 0:tt], ALU.subtract)
        p.V.tensor_scalar(tvar[0:1, 0:tt], tvar[0:1, 0:tt], 1e-5, None, ALU.add)
        nc.scalar.wait_ge(cv.h, cv.n)
        p.S.activation(tvar[0:1, 0:tt], tvar[0:1, 0:tt], AF.Ln)
        p.S.activation(statr[0:1, 0:tt], tvar[0:1, 0:tt], AF.Exp, scale=-0.5)
        nc.tensor.wait_ge(cs.h, p.sem("cs").n)
        nc.tensor.wait_ge(cv.h, cv.n)
        nc.tensor.matmul(ps_b[:, 0:tt], ones[0:1, :], statm[0:1, 0:tt], start=True, stop=True)
        ct.inc(nc.tensor.matmul(ps_b[:, TT:TT + tt], ones[0:1, :], statr[0:1, 0:tt], start=True, stop=True), 1)
        bmark = ct.n
        for ki in range(8):
            nc.vector.wait_ge(ct.h, bmark)
            nc.vector.wait_ge(ld.h, x_ld)
            nc.vector.wait_ge(cs.h, p.sem("cs").n)
            p.V.tensor_tensor(xo[:, 0:tt], sx(ki), ps_b[:, 0:tt], ALU.subtract)
            p.V.tensor_tensor(xo[:, 0:tt], xo[:, 0:tt], ps_b[:, TT:TT + tt], ALU.mult)
            p.V.tensor_scalar(xo[:, 0:tt], xo[:, 0:tt], lng[:, ki:ki + 1], lnb[:, ki:ki + 1], ALU.mult, ALU.add)
            nc.scalar.wait_ge(cv.h, cv.n)
            p.S.copy(xnb[:, ki * T + t0:ki * T + t0 + tt], xo[:, 0:tt])
    xn_mark = p.sem("cs").n

    mcs = chunks(2730, 128)
    wt_last = [0, 0]
    q = 0
    for mi, (m0, nm) in enumerate(mcs):
        buf = mi % 2
        nc.scalar.wait_ge(ct.h, wt_last[buf])
        for kj in range(8):
            lw.inc(nc.scalar.dma_start(out=wtfa[0:128, kj * 128:kj * 128 + nm],
                                       in_=w1a[kj * 128:(kj + 1) * 128, m0:m0 + nm]), 16)
        nc.scalar.wait_ge(lw.h, lw.n)
        p.S.copy(wtba[buf][:, :], wtfa[:, :])
        for kj in range(8):
            lw.inc(nc.scalar.dma_start(out=wtfa[0:128, kj * 128:kj * 128 + nm],
                                       in_=w1g[kj * 128:(kj + 1) * 128, m0:m0 + nm]), 16)
        nc.scalar.wait_ge(lw.h, lw.n)
        p.S.copy(wtbg[buf][:, :], wtfa[:, :])
        wmark = p.sem("cs").n
        nc.sync.wait_ge(cv.h, cv.n)
        nc.sync.wait_ge(cs.h, p.sem("cs").n)
        lw.inc(nc.sync.dma_start(out=bta[0:nm, :], in_=b1a[m0:m0 + nm, :]), 16)
        lw.inc(nc.sync.dma_start(out=btg[0:nm, :], in_=b1g[m0:m0 + nm, :]), 16)
        b_ld = lw.n
        for ti, (t0, tt) in enumerate(chunks(T, TT)):
            pbk = q % 2
            nc.tensor.wait_ge(cs.h, p.sem("cs").n)
            nc.tensor.wait_ge(cv.h, cv.n)
            for kj in range(8):
                nc.tensor.matmul(ps_a[pbk][0:nm, 0:tt], wtba[buf][:, kj * 128:kj * 128 + nm],
                                 xnb[:, kj * T + t0:kj * T + t0 + tt], start=(kj == 0), stop=(kj == 7))
            ins_mm = None
            for kj in range(8):
                ins_mm = nc.tensor.matmul(ps_g[pbk][0:nm, 0:tt], wtbg[buf][:, kj * 128:kj * 128 + nm],
                                          xnb[:, kj * T + t0:kj * T + t0 + tt], start=(kj == 0), stop=(kj == 7))
            ct.inc(ins_mm, 1)
            wt_last[buf] = ct.n
            mmk = ct.n
            nc.scalar.wait_ge(ct.h, mmk)
            nc.scalar.wait_ge(lw.h, b_ld)
            nc.scalar.wait_ge(cv.h, cv.n)
            p.S.activation(gel[0:nm, 0:tt], ps_g[pbk][0:nm, 0:tt], AF.Gelu, bias=btg[0:nm, 0:1])
            nc.vector.wait_ge(cs.h, p.sem("cs").n)
            nc.vector.wait_ge(st.h, st.n)
            p.V.tensor_scalar(sq[0][0:nm, 0:tt], ps_a[pbk][0:nm, 0:tt], bta[0:nm, 0:1], None, ALU.add)
            p.V.tensor_tensor(sq[0][0:nm, 0:tt], sq[0][0:nm, 0:tt], gel[0:nm, 0:tt], ALU.mult)
            nc.scalar.wait_ge(cv.h, cv.n)
            nc.scalar.wait_ge(st.h, st.n)
            p.S.copy(hb[0:nm, 0:tt], sq[0][0:nm, 0:tt])
            nc.sync.wait_ge(cs.h, p.sem("cs").n)
            st.inc(nc.sync.dma_start(out=hgate[m0:m0 + nm, t0:t0 + tt], in_=hb[0:nm, 0:tt]), 16)
            q += 1
    nc.sync.wait_ge(st.h, st.n)


def _cc(a):
    return np.ascontiguousarray(a)


def _pad_t(a, Tpad):
    out = np.zeros((a.shape[0], Tpad), dtype=a.dtype)
    out[:, :a.shape[1]] = a
    return out


def run_l0(inputs, store):
    x = np.asarray(inputs["L0_attn_x"])
    cores = [(i // 2, i % 2) for i in range(NC)]
    ones128 = np.ones((128, 128), np.float32)
    ident = np.eye(128, dtype=np.float32)

    TH = 2560
    xs = []
    for b, h in cores:
        t0 = h * TPC
        xa = np.zeros((DM, TH), np.float32)
        if h == 1:
            xa[:, 0:3] = x[b, t0 - 3:t0, :].T
        xa[:, 3:3 + TPC] = x[b, t0:t0 + TPC, :].T
        xs.append(_cc(xa))
    store["x_fm"] = [_cc(a[:, 3:3 + TPC]) for a in xs]
    r = run_launch("ln0", build_ln, [{"x": xs[i], "ones128": ones128} for i in range(NC)],
                   T=TH, want_pmax=True)
    store["xn_f"] = [r[i]["xn_f"] for i in range(NC)]
    store["xn_b"] = [r[i]["xn_b"] for i in range(NC)]
    pmax8 = np.concatenate([r[i]["pmax"][0:1, 0] for i in range(NC)])[None, :]

    W_in = np.asarray(inputs["L0_attn_W_in"], np.float32)
    r = run_launch("mm_xz", build_mm,
                   [{"lhsT": _cc(W_in[:, :2048]), "rhs": store["xn_b"][i]} for i in range(NC)],
                   K=1024, M=2048, T=TH, rhs_bf16=True, out_f32=True, out_bf16=False)
    xc = [r[i]["out_f"] for i in range(NC)]
    r = run_launch("mm_xz_z", build_mm,
                   [{"lhsT": _cc(W_in[:, 2048:]), "rhs": store["xn_b"][i]} for i in range(NC)],
                   K=1024, M=2048, T=TH, rhs_bf16=True, out_f32=True, epilogue=epi_silu)
    sz = [_cc(r[i]["out_f"][:, 3:3 + TPC]) for i in range(NC)]
    store["sz0"] = sz

    cw = np.asarray(inputs["L0_attn_conv_w"], np.float32)[:, 0, :]
    r = run_launch("conv0", build_conv0,
                   [{"xc": xc[i], "cw": cw} for i in range(NC)], T=TPC, TH=TH)
    u_f = [r[i]["u_f"] for i in range(NC)]
    u_b = [r[i]["u_b"] for i in range(NC)]
    store["u0"] = u_f

    W_x = np.asarray(inputs["L0_attn_W_x"], np.float32)
    r = run_launch("mm_xp", build_mm,
                   [{"lhsT": W_x, "rhs": u_b[i]} for i in range(NC)],
                   K=2048, M=32, T=TPC, rhs_bf16=True, out_f32=True)
    xp = [r[i]["out_f"] for i in range(NC)]

    W_dt = np.asarray(inputs["L0_attn_W_dt"], np.float32)
    b_dt = np.asarray(inputs["L0_attn_b_dt"], np.float32).reshape(-1, 1)
    r = run_launch("mm_dtA", build_mm,
                   [{"lhsT": W_dt, "rhs": _cc(xp[i][:16]), "bias": b_dt} for i in range(NC)],
                   K=16, M=2048, T=TPC, out_f32=True, bias=True, epilogue="dt_a")
    a0 = [r[i]["out_f"] for i in range(NC)]

    W_Bp = np.asarray(inputs["L0_attn_W_Bp"], np.float32)
    r = run_launch("mm_Bp", build_mm,
                   [{"lhsT": W_Bp, "rhs": _cc(xp[i][16:])} for i in range(NC)],
                   K=16, M=2048, T=TPC, out_f32=True)
    Bp = [r[i]["out_f"] for i in range(NC)]
    r = run_launch("mul0", build_mul,
                   [{"a": u_f[i], "b": Bp[i]} for i in range(NC)], R=2048, T=TPC)
    ub = [r[i]["o"] for i in range(NC)]

    sc_in = []
    for b, h in cores:
        lo, hi = h * 1024, (h + 1) * 1024
        a_full = np.concatenate([a0[2 * b][lo:hi], a0[2 * b + 1][lo:hi]], axis=1)
        ub_full = np.concatenate([ub[2 * b][lo:hi], ub[2 * b + 1][lo:hi]], axis=1)
        sc_in.append({"a": _cc(a_full), "ub": _cc(ub_full)})
    r = run_launch("scan0", build_scan0, sc_in, R=1024, T=L)
    ys_sc = [r[i]["ys"] for i in range(NC)]

    ys_tok = []
    for b, h in cores:
        t0, t1 = h * TPC, (h + 1) * TPC
        ys_tok.append(_cc(np.concatenate([ys_sc[2 * b][:, t0:t1], ys_sc[2 * b + 1][:, t0:t1]], axis=0)))

    Dp = np.asarray(inputs["L0_attn_D_param"], np.float32).reshape(-1, 1)
    r = run_launch("l0post", build_l0post,
                   [{"ys": ys_tok[i], "u": u_f[i], "sz": sz[i], "Dp": Dp} for i in range(NC)],
                   R=2048, T=TPC)
    y_b = [r[i]["y_b"] for i in range(NC)]
    W_out = np.asarray(inputs["L0_attn_W_out"], np.float32)
    r = run_launch("mm_attn0", build_mm,
                   [{"lhsT": W_out, "rhs": y_b[i]} for i in range(NC)],
                   K=2048, M=1024, T=TPC, rhs_bf16=True, out_f32=True)
    attn0 = [r[i]["out_f"] for i in range(NC)]
    store["attn0"] = attn0

    wg = np.asarray(inputs["L0_ffn_w_gate"], np.float32)
    wu = np.asarray(inputs["L0_ffn_w_up"], np.float32)
    wd = np.asarray(inputs["L0_ffn_w_down"], np.float32)
    r = run_launch("dq_3072_1024", build_dequant,
                   [{"w": wu, "ident": ident} for _ in range(NC)], O=3072, I=1024)
    wuT = r[0]["w_dq_T"]
    r = run_launch("dq_3072_1024", build_dequant,
                   [{"w": wg, "ident": ident} for _ in range(NC)], O=3072, I=1024)
    wgT = r[0]["w_dq_T"]
    r = run_launch("dq_1024_3072", build_dequant,
                   [{"w": wd, "ident": ident} for _ in range(NC)], O=1024, I=3072)
    wdT = r[0]["w_dq_T"]

    xn_tok = [_cc(store["xn_f"][i][:, 3:3 + TPC]) for i in range(NC)]
    store["xn0_tok"] = xn_tok
    r = run_launch("qdq_x", build_qdq_x,
                   [{"x": xn_tok[i], "pmax8": pmax8} for i in range(NC)], R=1024, T=TPC)
    xq_b = [r[i]["xq_b"] for i in range(NC)]
    store["xq_b"] = xq_b

    r = run_launch("mm_upgate", build_mm,
                   [{"lhsT": wuT, "rhs": xq_b[i]} for i in range(NC)],
                   K=1024, M=3072, T=TPC, rhs_bf16=True, lhsT_bf16=True, out_f32=True)
    up = [r[i]["out_f"] for i in range(NC)]
    r = run_launch("mm_upgate", build_mm,
                   [{"lhsT": wgT, "rhs": xq_b[i]} for i in range(NC)],
                   K=1024, M=3072, T=TPC, rhs_bf16=True, lhsT_bf16=True, out_f32=True)
    gate = [r[i]["out_f"] for i in range(NC)]
    store["up"] = up
    store["gate"] = gate

    r = run_launch("pm_R3072", build_pmax, [{"x": up[i]} for i in range(NC)], R=3072, T=TPC)
    pm_up8 = np.concatenate([r[i]["pmax"][0:1, 0] for i in range(NC)])[None, :]
    r = run_launch("pm_R3072", build_pmax, [{"x": gate[i]} for i in range(NC)], R=3072, T=TPC)
    pm_gate8 = np.concatenate([r[i]["pmax"][0:1, 0] for i in range(NC)])[None, :]

    r = run_launch("ffn_mid", build_ffn_mid,
                   [{"up": up[i], "gate": gate[i], "pm_up8": pm_up8, "pm_gate8": pm_gate8}
                    for i in range(NC)], R=3072, T=TPC)
    upq = [r[i]["upq"] for i in range(NC)]
    g2p = [r[i]["g2p"] for i in range(NC)]
    store["upq"] = upq
    store["g2p"] = g2p
    pm_g28 = np.concatenate([r[i]["pm_g2"][0:1, 0] for i in range(NC)])[None, :]

    r = run_launch("ffn_o", build_ffn_o,
                   [{"g2p": g2p[i], "upq": upq[i], "pm_g28": pm_g28} for i in range(NC)],
                   R=3072, T=TPC)
    o_pre = [r[i]["o_pre"] for i in range(NC)]
    pm_o8 = np.concatenate([r[i]["pm_o"][0:1, 0] for i in range(NC)])[None, :]

    r = run_launch("qdq_o", build_qdq_x,
                   [{"x": o_pre[i], "pmax8": pm_o8} for i in range(NC)], R=3072, T=TPC)
    oq_b = [r[i]["xq_b"] for i in range(NC)]

    r = run_launch("mm_down", build_mm,
                   [{"lhsT": wdT, "rhs": oq_b[i], "res0": attn0[i], "res1": store["x_fm"][i]}
                    for i in range(NC)],
                   K=3072, M=1024, T=TPC, rhs_bf16=True, lhsT_bf16=True, out_f32=True, nres=2)
    store["x1"] = [r[i]["out_f"] for i in range(NC)]
    return store


def _rotary_tables():
    n, DH, WIN = 4096, 64, 256
    half = np.arange(0, DH, 2, dtype=np.float64)
    inv_freq = 1.0 / 10000.0 ** (half / DH)
    t = np.arange(n, dtype=np.float64)
    freqs = np.concatenate([t[:, None] * inv_freq[None, :]] * 2, axis=1)
    base = (half + 0.4 * DH) / (1.4 * DH)
    power = (t - n // 2) / (WIN // 2)
    scale = np.concatenate([base[None, :] ** power[:, None]] * 2, axis=1)
    return freqs, scale


def _table_tile(vals, gpos):
    v = vals[np.clip(gpos, 0, 4095), :]
    out = np.concatenate([v.T, v.T], axis=0)
    return np.ascontiguousarray(out.astype(np.float32))


def run_l1(inputs, store):
    cores = [(i // 2, i % 2) for i in range(NC)]
    ones128 = np.ones((128, 128), np.float32)
    x1 = store["x1"]

    TH = 2560
    xs = []
    for i, (b, h) in enumerate(cores):
        xa = np.zeros((DM, TH), np.float32)
        if h == 1:
            xa[:, 0:3] = x1[2 * b][:, -3:]
        xa[:, 3:3 + TPC] = x1[i]
        if h == 0:
            xa[:, 3 + TPC:6 + TPC] = x1[2 * b + 1][:, 0:3]
        xs.append(_cc(xa))
    r = run_launch("ln1", build_ln, [{"x": xs[i], "ones128": ones128} for i in range(NC)],
                   T=TH, want_pmax=False)
    xn1_f = [r[i]["xn_f"] for i in range(NC)]
    xn1_b = [r[i]["xn_b"] for i in range(NC)]
    store["xn1_tok"] = [_cc(xn1_f[i][:, 3:3 + TPC]) for i in range(NC)]

    W_inp = _cc(np.asarray(inputs["L1_attn_in_proj_w"], np.float32).T)
    r = run_launch("mm_inproj", build_mm,
                   [{"lhsT": W_inp, "rhs": xn1_b[i]} for i in range(NC)],
                   K=1024, M=4096, T=TH, rhs_bf16=True, out_f32=True)
    xz1 = [r[i]["out_f"] for i in range(NC)]
    xc1 = [_cc(xz1[i][:2048]) for i in range(NC)]
    z1 = [_cc(xz1[i][2048:, 3:3 + TPC]) for i in range(NC)]

    cw_f = _cc(np.asarray(inputs["L1_attn_conv_w"], np.float32)[:, 0, :])
    cw_b = _cc(np.asarray(inputs["L1_attn_conv_w_b"], np.float32)[:, 0, ::-1])
    cb_f = np.asarray(inputs["L1_attn_conv_b"], np.float32)
    cb_b = np.asarray(inputs["L1_attn_conv_b_b"], np.float32)
    r = run_launch("conv1f", build_conv1,
                   [{"xc": xc1[i], "cw": cw_f, "cb": _cc(cb_f.reshape(-1, 1))} for i in range(NC)],
                   T=TPC, TH=TH, off=0)
    u1f_f = [r[i]["u_f"] for i in range(NC)]
    u1f_b = [r[i]["u_b"] for i in range(NC)]
    r = run_launch("conv1b", build_conv1,
                   [{"xc": xc1[i], "cw": cw_b, "cb": _cc(cb_b.reshape(-1, 1))} for i in range(NC)],
                   T=TPC, TH=TH, off=3)
    u1b_f = [r[i]["u_f"] for i in range(NC)]
    u1b_b = [r[i]["u_b"] for i in range(NC)]
    store["u1f"] = u1f_f
    store["u1b"] = u1b_f

    xpw_f = _cc(np.asarray(inputs["L1_attn_x_proj_w"], np.float32).T)
    xpw_b = _cc(np.asarray(inputs["L1_attn_x_proj_w_b"], np.float32).T)
    r = run_launch("mm_xdbl", build_mm,
                   [{"lhsT": xpw_f, "rhs": u1f_b[i]} for i in range(NC)],
                   K=2048, M=96, T=TPC, rhs_bf16=True, out_f32=True)
    xdbl_f = [r[i]["out_f"] for i in range(NC)]
    r = run_launch("mm_xdbl", build_mm,
                   [{"lhsT": xpw_b, "rhs": u1b_b[i]} for i in range(NC)],
                   K=2048, M=96, T=TPC, rhs_bf16=True, out_f32=True)
    xdbl_b = [r[i]["out_f"] for i in range(NC)]
    store["xdbl_f"] = xdbl_f
    store["xdbl_b"] = xdbl_b

    wdt_f = np.asarray(inputs["L1_attn_dt_proj_w"], np.float32).T
    wdt_b = np.asarray(inputs["L1_attn_dt_proj_w_b"], np.float32).T
    bdt_f = np.asarray(inputs["L1_attn_dt_proj_b"], np.float32)
    bdt_b = np.asarray(inputs["L1_attn_dt_proj_b_b"], np.float32)
    Alog_f = np.asarray(inputs["L1_attn_A_log"], np.float32)
    Alog_b = np.asarray(inputs["L1_attn_A_b_log"], np.float32)
    Dp_f = np.asarray(inputs["L1_attn_D"], np.float32)
    Dp_b = np.asarray(inputs["L1_attn_D_b"], np.float32)

    def stage_vec(v, rows):
        return _cc(v[rows].reshape(8, 128).T)

    def stage_alog(A, rows):
        a = A[rows].reshape(8, 128, 16)
        return _cc(np.transpose(a, (1, 0, 2)).reshape(128, 128))

    sc_in = []
    for b, dh in cores:
        rows = slice(dh * 1024, (dh + 1) * 1024)
        tc0, tc1 = 2 * b, 2 * b + 1
        d = {}
        d["u_f"] = _cc(np.concatenate([u1f_f[tc0][rows], u1f_f[tc1][rows]], axis=1))
        d["u_b"] = _cc(np.concatenate([u1b_f[tc0][rows], u1b_f[tc1][rows]], axis=1))
        d["dtr_f"] = _cc(np.concatenate([xdbl_f[tc0][:64], xdbl_f[tc1][:64]], axis=1))
        d["dtr_b"] = _cc(np.concatenate([xdbl_b[tc0][:64], xdbl_b[tc1][:64]], axis=1))
        bc = np.zeros((64, L), np.float32)
        bc[0:16] = np.concatenate([xdbl_f[tc0][64:80], xdbl_f[tc1][64:80]], axis=1)
        bc[16:32] = np.concatenate([xdbl_f[tc0][80:96], xdbl_f[tc1][80:96]], axis=1)
        bc[32:48] = np.concatenate([xdbl_b[tc0][64:80], xdbl_b[tc1][64:80]], axis=1)
        bc[48:64] = np.concatenate([xdbl_b[tc0][80:96], xdbl_b[tc1][80:96]], axis=1)
        d["bc_all"] = _cc(bc)
        d["wdt_f"] = _cc(wdt_f[:, rows])
        d["wdt_b"] = _cc(wdt_b[:, rows])
        d["bdt_f"] = stage_vec(bdt_f, rows)
        d["bdt_b"] = stage_vec(bdt_b, rows)
        d["Alog_f"] = stage_alog(Alog_f, rows)
        d["Alog_b"] = stage_alog(Alog_b, rows)
        d["Dp_f"] = stage_vec(Dp_f, rows)
        d["Dp_b"] = stage_vec(Dp_b, rows)
        d["z"] = _cc(np.concatenate([z1[tc0][rows], z1[tc1][rows]], axis=1))
        sc_in.append(d)
    r = run_launch("scan1", build_scan1, sc_in, R=1024, T=L)
    ysum = [r[i]["y_sum"] for i in range(NC)]
    store["ysum"] = ysum

    y_tok = []
    for b, h in cores:
        t0, t1 = h * TPC, (h + 1) * TPC
        y_tok.append(_cc(np.concatenate([ysum[2 * b][:, t0:t1], ysum[2 * b + 1][:, t0:t1]], axis=0)))
    Wop = _cc(np.asarray(inputs["L1_attn_out_proj_w"], np.float32).T)
    r = run_launch("mm_outproj", build_mm,
                   [{"lhsT": Wop, "rhs": y_tok[i]} for i in range(NC)],
                   K=2048, M=1024, T=TPC, out_f32=True)
    attn1 = [r[i]["out_f"] for i in range(NC)]
    store["attn1"] = attn1

    freqs, xsc = _rotary_tables()
    cosv, sinv = np.cos(freqs), np.sin(freqs)
    Wqkv = np.asarray(inputs["L1_ffn_Wqkv"], np.float32)
    q_scale = np.asarray(inputs["L1_ffn_q_scale"], np.float32)
    k_scale = np.asarray(inputs["L1_ffn_k_scale"], np.float32)
    Wo = np.asarray(inputs["L1_ffn_Wo"], np.float32)
    ln_g = np.asarray(inputs["L1_ffn_ln_g"], np.float32)
    ln_b = np.asarray(inputs["L1_ffn_ln_b"], np.float32)
    W1 = np.asarray(inputs["L1_ffn_W1"], np.float32)
    b1 = np.asarray(inputs["L1_ffn_b1"], np.float32)
    W2 = np.asarray(inputs["L1_ffn_W2"], np.float32)
    b2 = np.asarray(inputs["L1_ffn_b2"], np.float32)

    ind2 = np.zeros((128, 2), np.float32)
    ind2[0:64, 0] = 1.0
    ind2[64:128, 1] = 1.0
    rotm = np.zeros((128, 128), np.float32)
    for m in range(128):
        if m % 64 < 32:
            rotm[m + 32, m] = -1.0
        else:
            rotm[m - 32, m] = 1.0

    percore = []
    for b, h in cores:
        g_my = h * TPC + np.arange(TPC)
        g_kv = h * TPC - 256 + np.arange(2304)
        pc = {
            "cosq": _table_tile(cosv * xsc, g_my), "sinq": _table_tile(sinv * xsc, g_my),
            "cosk": _table_tile(cosv / xsc, g_kv), "sink": _table_tile(sinv / xsc, g_kv),
        }
        masks = np.zeros((8 * 2 * 128, 512), np.float32)
        for w in range(8):
            for ih in range(2):
                i_loc = ih * 128 + np.arange(128)[:, None] + (0 if True else 0)
                i_in_win = ih * 128 + np.arange(128)[:, None]
                j = np.arange(512)[None, :]
                ok = (j >= i_in_win) & (j <= i_in_win + 256)
                if h == 0 and w == 0:
                    ok &= (j >= 256)
                m = np.where(ok, 0.0, NEG_HUGE).astype(np.float32)
                masks[(w * 2 + ih) * 128:(w * 2 + ih) * 128 + 128] = m
        pc["masks"] = _cc(masks)
        percore.append(pc)

    x_cur = store["xn1_tok"]
    for li in range(6):
        wqk = _cc(Wqkv[li][:, :2048])
        wv = _cc(Wqkv[li][:, 2048:])
        sclq = np.zeros((2, 128), np.float32)
        sclk = np.zeros((2, 128), np.float32)
        for j in range(2):
            sclq[j, j * 64:(j + 1) * 64] = q_scale[li]
            sclk[j, j * 64:(j + 1) * 64] = k_scale[li]
        TH1 = 2560
        qkv_in = []
        for i, (b, h) in enumerate(cores):
            xa = np.zeros((DM, TH1), np.float32)
            if h == 1:
                xa[:, 0:257] = x_cur[2 * b][:, -257:]
            xa[:, 257:257 + TPC] = x_cur[i]
            qkv_in.append({"x": _cc(xa), "wqk": wqk, "wv": wv,
                           "cosq": percore[i]["cosq"], "sinq": percore[i]["sinq"],
                           "cosk": percore[i]["cosk"], "sink": percore[i]["sink"],
                           "sclq": _cc(sclq), "sclk": _cc(sclk), "ind2": ind2, "rotm": rotm})
        r = run_launch("f1qkv", build_f1qkv, qkv_in)
        q2b = [r[i]["q2b"] for i in range(NC)]
        k2b = [r[i]["k2b"] for i in range(NC)]
        v_tok = [r[i]["v_tok"] for i in range(NC)]

        r = run_launch("f1attn", build_f1attn,
                       [{"q2b": q2b[i], "k2b": k2b[i], "v_tok": v_tok[i],
                         "masks": percore[i]["masks"], "wo": _cc(Wo[li]), "xres": x_cur[i]}
                        for i in range(NC)])
        x_mid = [r[i]["xo"] for i in range(NC)]

        f1h_in = []
        for i, (b, h) in enumerate(cores):
            xa = np.zeros((DM, TPC + 1), np.float32)
            if h == 1:
                xa[:, 0:1] = x_mid[2 * b][:, -1:]
            xa[:, 1:] = x_mid[i]
            f1h_in.append({"x": _cc(xa),
                           "w1a": _cc(W1[li][:, :2730]), "w1g": _cc(W1[li][:, 2730:]),
                           "b1a": _cc(b1[li][:2730].reshape(-1, 1)), "b1g": _cc(b1[li][2730:].reshape(-1, 1)),
                           "lng": _cc(ln_g[li].reshape(8, 128).T), "lnb": _cc(ln_b[li].reshape(8, 128).T),
                           "ones128": ones128})
        r = run_launch("f1h", build_f1h, f1h_in)
        hgate = [r[i]["hgate"] for i in range(NC)]

        if li < 5:
            r = run_launch("mm_w2", build_mm,
                           [{"lhsT": _cc(W2[li]), "rhs": hgate[i], "bias": _cc(b2[li].reshape(-1, 1)),
                             "res0": x_mid[i]} for i in range(NC)],
                           K=2730, M=1024, T=TPC, rhs_bf16=True, bias=True, nres=1)
        else:
            r = run_launch("mm_w2_last", build_mm,
                           [{"lhsT": _cc(W2[li]), "rhs": hgate[i], "bias": _cc(b2[li].reshape(-1, 1)),
                             "res0": x_mid[i], "res1": x1[i], "res2": attn1[i]} for i in range(NC)],
                           K=2730, M=1024, T=TPC, rhs_bf16=True, bias=True, nres=3)
        x_cur = [r[i]["out_f"] for i in range(NC)]
        store[f"stk{li}"] = x_cur
        store[f"xmid{li}"] = x_mid
    store["x2"] = x_cur
    return store


def build_pmax(p, *, R, T):
    nc = p.nc
    TT = 512
    x = p.din("x", (R, T))
    pmax = p.dout("pmax", (1, 8))
    pb = p.dint("pb", (128, 1))
    ld, cv, st = (p.sem(s) for s in ("ld", "cv", "st"))
    xt = p.sbuf("xt", [128, TT], F32)
    mx = p.sbuf("mx", [128, 1], F32)
    mx2 = p.sbuf("mx2", [128, 1], F32)
    mxr = p.sbuf("mxr", [1, 128], F32)
    out8 = p.sbuf("out8", [1, 8], F32)
    first = True
    for ki, (k0, nk) in enumerate(chunks(R, 128)):
        for ti, (t0, tt) in enumerate(chunks(T, TT)):
            nc.sync.wait_ge(cv.h, cv.n)
            ld.inc(nc.sync.dma_start(out=xt[0:nk, 0:tt], in_=x[k0:k0 + nk, t0:t0 + tt]), 16)
            nc.vector.wait_ge(ld.h, ld.n)
            if first:
                p.V.reduce_max(mx[0:nk, 0:1], xt[0:nk, 0:tt], axis=AX, apply_absolute_value=True)
                first = False
            else:
                p.V.reduce_max(mx2[0:nk, 0:1], xt[0:nk, 0:tt], axis=AX, apply_absolute_value=True)
                p.V.tensor_tensor(mx[0:nk, 0:1], mx[0:nk, 0:1], mx2[0:nk, 0:1], ALU.max)
    nc.sync.wait_ge(cv.h, cv.n)
    st.inc(nc.sync.dma_start(out=pb.ap(), in_=mx[:]), 16)
    nc.sync.wait_ge(st.h, st.n)
    ld.inc(nc.sync.dma_start(out=mxr[0:1, :], in_=pb.ap().rearrange("p x -> x p")), 16)
    nc.vector.wait_ge(ld.h, ld.n)
    p.V.memset(out8[0:1, :], 0.0)
    p.V.reduce_max(out8[0:1, 0:1], mxr[0:1, :], axis=AX)
    nc.sync.wait_ge(cv.h, cv.n)
    st.inc(nc.sync.dma_start(out=pmax, in_=out8[0:1, :]), 16)
    nc.sync.wait_ge(st.h, st.n)


MEGA = False


def kernel_mega(inputs):
    p = Prog("mega", mega=True)
    stage = {}
    cores = [(i // 2, i % 2) for i in range(NC)]

    def run_phase(prefix, builder, feeds, vals, **kw):
        mark = len(p.ext_list)
        p.begin_phase(prefix, {k: (v if isinstance(v, bass.AP) else v.ap())
                               for k, v in (feeds or {}).items()})
        builder(p, **kw)
        p.end_phase()
        for full, local in p.ext_list[mark:]:
            assert local in vals, (prefix, local, list(vals))
            stage[full] = vals[local]

    def mk(name, shape, dt=F32):
        return p.nc.dram_tensor(name, list(shape), dt)

    def ag_pair(tag, src, dst):
        p.begin_phase(tag)
        emit_collective(p, "AllGather", ALU.bypass, PAIRS, src, dst)
        p.end_phase()

    def ar_max(tag, src, dst):
        p.begin_phase(tag)
        emit_collective(p, "AllReduce", ALU.max, ALL8, src, dst)
        p.end_phase()

    def copy_phase(tag, dst_ap, src_ap):
        p.begin_phase(tag)
        emit_dram_copy(p, dst_ap, src_ap)
        p.end_phase()

    x = np.asarray(inputs["L0_attn_x"])
    ones128 = np.ones((128, 128), np.float32)
    ident = np.eye(128, dtype=np.float32)
    x_pads = []
    for b, h in cores:
        t0 = h * TPC
        xa = np.zeros((DM, 2560), np.float32)
        if h == 1:
            xa[:, 0:3] = x[b, t0 - 3:t0, :].T
        xa[:, 3:3 + TPC] = x[b, t0:t0 + TPC, :].T
        x_pads.append(_cc(xa))
    mhl = [np.full((128, 1), float(h), np.float32) for b, h in cores]
    mhr = [np.full((128, 1), float(1 - h), np.float32) for b, h in cores]
    sc_idx = []
    ys_idx = []
    for b, h in cores:
        idx = np.zeros((128, 16), np.int32)
        for ki in range(8):
            idx[:, ki] = h * 1024 + ki * 128 + np.arange(128)
            idx[:, 8 + ki] = 2048 + h * 1024 + ki * 128 + np.arange(128)
        sc_idx.append(_cc(idx))
        yi = np.zeros((128, 16), np.int32)
        for ki in range(16):
            d = ki * 128 + np.arange(128)
            yi[:, ki] = (d // 1024) * 2048 + (d % 1024) * 2 + h
        ys_idx.append(_cc(yi))

    W_in = np.asarray(inputs["L0_attn_W_in"], np.float32)
    cw0 = _cc(np.asarray(inputs["L0_attn_conv_w"], np.float32)[:, 0, :])
    W_x = np.asarray(inputs["L0_attn_W_x"], np.float32)
    W_dt = np.asarray(inputs["L0_attn_W_dt"], np.float32)
    b_dt = _cc(np.asarray(inputs["L0_attn_b_dt"], np.float32).reshape(-1, 1))
    W_Bp = np.asarray(inputs["L0_attn_W_Bp"], np.float32)
    Dp0 = _cc(np.asarray(inputs["L0_attn_D_param"], np.float32).reshape(-1, 1))
    W_out = np.asarray(inputs["L0_attn_W_out"], np.float32)
    wu = np.asarray(inputs["L0_ffn_w_up"], np.float32)
    wg = np.asarray(inputs["L0_ffn_w_gate"], np.float32)
    wd = np.asarray(inputs["L0_ffn_w_down"], np.float32)

    with p.stack:
        with p.nc.Block():
            x_in = mk("x_in", (DM, 2560))
            p.ext_list.append(("x_in", "x_in"))
    p = Prog("mega", mega=True)
    stage = {}

    with p.stack:
        with p.nc.Block():
            nc = p.nc
            x_in = nc.dram_tensor("x_in", [DM, 2560], F32, kind="ExternalInput")
            stage["x_in"] = x_pads
            ones_e = nc.dram_tensor("ones_e", [128, 128], F32, kind="ExternalInput")
            stage["ones_e"] = ones128
            out_e = nc.dram_tensor("out_e", [DM, TPC], F32, kind="ExternalOutput")

            def run_phase2(prefix, builder, feeds, vals, **kw):
                mark = len(p.ext_list)
                p.begin_phase(prefix, {k: (v if isinstance(v, bass.AP) else v.ap())
                                       for k, v in (feeds or {}).items()})
                builder(p, **kw)
                p.end_phase()
                for full, local in p.ext_list[mark:]:
                    assert local in vals, (prefix, local, sorted(vals))
                    stage[full] = vals[local]

            def mk2(name, shape, dt=F32):
                return p.nc.dram_tensor(name, list(shape), dt)

            xn0f = mk2("xn0f", (DM, 2560)); xn0b = mk2("xn0b", (DM, 2560), BF16)
            pm0 = mk2("pm0", (1, 8)); pm0g = mk2("pm0g", (1, 8))
            run_phase2("ln0", build_ln,
                       {"x": x_in, "ones128": ones_e, "xn_f": xn0f, "xn_b": xn0b, "pmax": pm0},
                       {}, T=2560, want_pmax=True)
            ar_max("ar0", pm0, pm0g)

            xc0 = mk2("xc0", (2048, 2560))
            run_phase2("mxz", build_mm, {"rhs": xn0b, "out_f": xc0}, {"lhsT": _cc(W_in[:, :2048])},
                       K=1024, M=2048, T=2560, rhs_bf16=True)
            sz0 = mk2("sz0", (2048, 2560))
            run_phase2("mxzz", build_mm, {"rhs": xn0b, "out_f": sz0}, {"lhsT": _cc(W_in[:, 2048:])},
                       K=1024, M=2048, T=2560, rhs_bf16=True, epilogue=epi_silu)
            u0f = mk2("u0f", (2048, TPC)); u0b = mk2("u0b", (2048, TPC), BF16)
            run_phase2("cv0", build_conv0, {"xc": xc0, "u_f": u0f, "u_b": u0b}, {"cw": cw0}, T=TPC, TH=2560)
            xp0 = mk2("xp0", (32, TPC))
            run_phase2("mxp", build_mm, {"rhs": u0b, "out_f": xp0}, {"lhsT": W_x},
                       K=2048, M=32, T=TPC, rhs_bf16=True)
            a0 = mk2("a0", (2048, TPC))
            run_phase2("mdt", build_mm, {"rhs": xp0.ap()[0:16, :], "out_f": a0},
                       {"lhsT": W_dt, "bias": b_dt},
                       K=16, M=2048, T=TPC, bias=True, epilogue="dt_a")
            Bp0 = mk2("Bp0", (2048, TPC))
            run_phase2("mBp", build_mm, {"rhs": xp0.ap()[16:32, :], "out_f": Bp0}, {"lhsT": W_Bp},
                       K=16, M=2048, T=TPC)
            ub0 = mk2("ub0", (2048, TPC))
            run_phase2("mu0", build_mul, {"a": u0f, "b": Bp0, "o": ub0}, {}, R=2048, T=TPC)

            aga0 = mk2("aga0", (4096, TPC)); agub0 = mk2("agub0", (4096, TPC))
            ag_pair("aga", a0, aga0)
            ag_pair("agb", ub0, agub0)
            ys0w = mk2("ys0w", (2048, TPC))
            run_phase2("sc0", build_scan0, {"ag_a": aga0, "ag_ub": agub0, "ys": ys0w},
                       {"sc0_idx": sc_idx}, R=1024, T=L, ag=True)
            agys0 = mk2("agys0", (4096, TPC))
            ag_pair("agy0", ys0w, agys0)
            ysr0 = mk2("ysr0", (2048, TPC))
            run_phase2("gy0", build_gather_rows, {"dst": None} if False else {},
                       {"ys_idx": ys_idx}, src=agys0.ap(), idx_name="ys_idx", dst=ysr0.ap(),
                       R=2048, C=TPC)
            y0b = mk2("y0b", (2048, TPC), BF16)
            run_phase2("l0p", build_l0post,
                       {"ys": ysr0, "u": u0f, "sz": sz0.ap()[:, 3:3 + TPC], "y_b": y0b},
                       {"Dp": Dp0}, R=2048, T=TPC)
            attn0 = mk2("attn0", (DM, TPC))
            run_phase2("ma0", build_mm, {"rhs": y0b, "out_f": attn0}, {"lhsT": W_out},
                       K=2048, M=1024, T=TPC, rhs_bf16=True)

            wuT = mk2("wuT", (1024, 3072), BF16)
            run_phase2("dqu", build_dequant, {"w_dq_T": wuT}, {"w": wu, "ident": ident}, O=3072, I=1024)
            wgT = mk2("wgT", (1024, 3072), BF16)
            run_phase2("dqg", build_dequant, {"w_dq_T": wgT}, {"w": wg, "ident": ident}, O=3072, I=1024)
            wdT = mk2("wdT", (3072, 1024), BF16)
            run_phase2("dqd", build_dequant, {"w_dq_T": wdT}, {"w": wd, "ident": ident}, O=1024, I=3072)

            xq0b = mk2("xq0b", (DM, TPC), BF16)
            run_phase2("qdx", build_qdq_x,
                       {"x": xn0f.ap()[:, 3:3 + TPC], "pmax8": pm0g, "xq_b": xq0b}, {},
                       R=1024, T=TPC)
            up0 = mk2("up0", (3072, TPC)); gate0 = mk2("gate0", (3072, TPC))
            run_phase2("mup", build_mm, {"rhs": xq0b, "lhsT": wuT, "out_f": up0}, {},
                       K=1024, M=3072, T=TPC, rhs_bf16=True, lhsT_bf16=True)
            run_phase2("mgt", build_mm, {"rhs": xq0b, "lhsT": wgT, "out_f": gate0}, {},
                       K=1024, M=3072, T=TPC, rhs_bf16=True, lhsT_bf16=True)
            pmu = mk2("pmu", (1, 8)); pmug = mk2("pmug", (1, 8))
            run_phase2("pmu", build_pmax, {"x": up0, "pmax": pmu}, {}, R=3072, T=TPC)
            ar_max("aru", pmu, pmug)
            pmg = mk2("pmg", (1, 8)); pmgg = mk2("pmgg", (1, 8))
            run_phase2("pmg", build_pmax, {"x": gate0, "pmax": pmg}, {}, R=3072, T=TPC)
            ar_max("arg", pmg, pmgg)
            upq = mk2("upq", (3072, TPC)); g2p = mk2("g2p", (3072, TPC))
            pmg2 = mk2("pmg2", (1, 8)); pmg2g = mk2("pmg2g", (1, 8))
            run_phase2("fmd", build_ffn_mid,
                       {"up": up0, "gate": gate0, "pm_up8": pmug, "pm_gate8": pmgg,
                        "upq": upq, "g2p": g2p, "pm_g2": pmg2}, {}, R=3072, T=TPC)
            ar_max("ar2", pmg2, pmg2g)
            o_pre = mk2("o_pre", (3072, TPC)); pmo = mk2("pmo", (1, 8)); pmog = mk2("pmog", (1, 8))
            run_phase2("fo", build_ffn_o,
                       {"g2p": g2p, "upq": upq, "pm_g28": pmg2g, "o_pre": o_pre, "pm_o": pmo},
                       {}, R=3072, T=TPC)
            ar_max("aro", pmo, pmog)
            oq0b = mk2("oq0b", (3072, TPC), BF16)
            run_phase2("qdo", build_qdq_x, {"x": o_pre, "pmax8": pmog, "xq_b": oq0b}, {},
                       R=3072, T=TPC)
            x1 = mk2("x1", (DM, TPC))
            run_phase2("mdn", build_mm,
                       {"rhs": oq0b, "lhsT": wdT, "res0": attn0, "res1": x_in.ap()[:, 3:3 + TPC],
                        "out_f": x1}, {},
                       K=3072, M=1024, T=TPC, rhs_bf16=True, lhsT_bf16=True, nres=2)

            x1b = mk2("x1b", (DM, 6)); agx1b = mk2("agx1b", (2 * DM, 6))
            copy_phase("cx1a", x1b.ap()[:, 0:3], x1.ap()[:, 0:3])
            copy_phase("cx1b", x1b.ap()[:, 3:6], x1.ap()[:, TPC - 3:TPC])
            ag_pair("agxb", x1b, agx1b)
            x1p = mk2("x1p", (DM, 2560))
            run_phase2("zf1", build_zero_fill, {}, {}, dst_ap=x1p.ap()[:, 2054:2560], rows=DM, cols=506)
            copy_phase("cc1", x1p.ap()[:, 3:3 + TPC], x1.ap())
            run_phase2("mh1", build_masked_halo,
                       {"dst": None} if False else {}, {"mhl": mhl},
                       agsrc=agx1b.ap(), dst=x1p.ap()[:, 0:3], rows=DM, cols=3,
                       mask_name="mhl", src_r0=0, src_c0=3)
            run_phase2("mh2", build_masked_halo, {}, {"mhr": mhr},
                       agsrc=agx1b.ap(), dst=x1p.ap()[:, 2051:2054], rows=DM, cols=3,
                       mask_name="mhr", src_r0=DM, src_c0=0)
            xn1f = mk2("xn1f", (DM, 2560)); xn1b = mk2("xn1b", (DM, 2560), BF16)
            run_phase2("ln1", build_ln, {"x": x1p, "ones128": ones_e, "xn_f": xn1f, "xn_b": xn1b},
                       {}, T=2560, want_pmax=False)
            xz1 = mk2("xz1", (4096, 2560))
            run_phase2("mip", build_mm, {"rhs": xn1b, "out_f": xz1},
                       {"lhsT": _cc(np.asarray(inputs["L1_attn_in_proj_w"], np.float32).T)},
                       K=1024, M=4096, T=2560, rhs_bf16=True)
            u1f = mk2("u1f", (2048, TPC)); u1fb = mk2("u1fb", (2048, TPC), BF16)
            u1b = mk2("u1b", (2048, TPC)); u1bb = mk2("u1bb", (2048, TPC), BF16)
            cw1f = _cc(np.asarray(inputs["L1_attn_conv_w"], np.float32)[:, 0, :])
            cw1b = _cc(np.asarray(inputs["L1_attn_conv_w_b"], np.float32)[:, 0, ::-1])
            run_phase2("c1f", build_conv1,
                       {"xc": xz1.ap()[0:2048, :], "u_f": u1f, "u_b": u1fb},
                       {"cw": cw1f, "cb": _cc(np.asarray(inputs["L1_attn_conv_b"], np.float32).reshape(-1, 1))},
                       T=TPC, TH=2560, off=0)
            run_phase2("c1b", build_conv1,
                       {"xc": xz1.ap()[0:2048, :], "u_f": u1b, "u_b": u1bb},
                       {"cw": cw1b, "cb": _cc(np.asarray(inputs["L1_attn_conv_b_b"], np.float32).reshape(-1, 1))},
                       T=TPC, TH=2560, off=3)
            xdf = mk2("xdf", (96, TPC)); xdb = mk2("xdb", (96, TPC))
            run_phase2("xdf", build_mm, {"rhs": u1fb, "out_f": xdf},
                       {"lhsT": _cc(np.asarray(inputs["L1_attn_x_proj_w"], np.float32).T)},
                       K=2048, M=96, T=TPC, rhs_bf16=True)
            run_phase2("xdb", build_mm, {"rhs": u1bb, "out_f": xdb},
                       {"lhsT": _cc(np.asarray(inputs["L1_attn_x_proj_w_b"], np.float32).T)},
                       K=2048, M=96, T=TPC, rhs_bf16=True)
            agu1f = mk2("agu1f", (4096, TPC)); agu1b = mk2("agu1b", (4096, TPC))
            ag_pair("aguf", u1f, agu1f)
            ag_pair("agub2", u1b, agu1b)
            zcon = mk2("zcon", (2048, TPC))
            copy_phase("czc", zcon.ap(), xz1.ap()[2048:4096, 3:3 + TPC])
            agz1 = mk2("agz1", (4096, TPC))
            ag_pair("agz", zcon, agz1)
            agxdf = mk2("agxdf", (192, TPC)); agxdb = mk2("agxdb", (192, TPC))
            ag_pair("agdf", xdf, agxdf)
            ag_pair("agdb", xdb, agxdb)

            wdt_f = np.asarray(inputs["L1_attn_dt_proj_w"], np.float32).T
            wdt_b = np.asarray(inputs["L1_attn_dt_proj_w_b"], np.float32).T
            bdt_f = np.asarray(inputs["L1_attn_dt_proj_b"], np.float32)
            bdt_b = np.asarray(inputs["L1_attn_dt_proj_b_b"], np.float32)
            Alog_f = np.asarray(inputs["L1_attn_A_log"], np.float32)
            Alog_b = np.asarray(inputs["L1_attn_A_b_log"], np.float32)
            Dp_f = np.asarray(inputs["L1_attn_D"], np.float32)
            Dp_b = np.asarray(inputs["L1_attn_D_b"], np.float32)

            def stage_vec(v):
                return [_cc(v[slice(h * 1024, (h + 1) * 1024)].reshape(8, 128).T) for b, h in cores]

            def stage_alog(A):
                out = []
                for b, h in cores:
                    a = A[h * 1024:(h + 1) * 1024].reshape(8, 128, 16)
                    out.append(_cc(np.transpose(a, (1, 0, 2)).reshape(128, 128)))
                return out

            ys1w = mk2("ys1w", (2048, TPC))
            run_phase2("sc1", build_scan1,
                       {"ag_uf": agu1f, "ag_ub": agu1b, "ag_xdbl_f": agxdf, "ag_xdbl_b": agxdb,
                        "ag_z": agz1, "y_sum": ys1w},
                       {"sc1_idx": sc_idx,
                        "wdt_f": [_cc(wdt_f[:, h * 1024:(h + 1) * 1024]) for b, h in cores],
                        "wdt_b": [_cc(wdt_b[:, h * 1024:(h + 1) * 1024]) for b, h in cores],
                        "bdt_f": stage_vec(bdt_f), "bdt_b": stage_vec(bdt_b),
                        "Alog_f": stage_alog(Alog_f), "Alog_b": stage_alog(Alog_b),
                        "Dp_f": stage_vec(Dp_f), "Dp_b": stage_vec(Dp_b)},
                       R=1024, T=L, ag=True)
            agys1 = mk2("agys1", (4096, TPC))
            ag_pair("agy1", ys1w, agys1)
            y1r = mk2("y1r", (2048, TPC))
            run_phase2("gy1", build_gather_rows, {}, {"ys1_idx": ys_idx},
                       src=agys1.ap(), idx_name="ys1_idx", dst=y1r.ap(), R=2048, C=TPC)
            attn1 = mk2("attn1", (DM, TPC))
            run_phase2("mop", build_mm, {"rhs": y1r, "out_f": attn1},
                       {"lhsT": _cc(np.asarray(inputs["L1_attn_out_proj_w"], np.float32).T)},
                       K=2048, M=1024, T=TPC)

            freqs, xsc = _rotary_tables()
            cosv, sinv = np.cos(freqs), np.sin(freqs)
            Wqkv = np.asarray(inputs["L1_ffn_Wqkv"], np.float32)
            q_scale = np.asarray(inputs["L1_ffn_q_scale"], np.float32)
            k_scale = np.asarray(inputs["L1_ffn_k_scale"], np.float32)
            Wo = np.asarray(inputs["L1_ffn_Wo"], np.float32)
            ln_g = np.asarray(inputs["L1_ffn_ln_g"], np.float32)
            ln_b = np.asarray(inputs["L1_ffn_ln_b"], np.float32)
            W1 = np.asarray(inputs["L1_ffn_W1"], np.float32)
            b1 = np.asarray(inputs["L1_ffn_b1"], np.float32)
            W2 = np.asarray(inputs["L1_ffn_W2"], np.float32)
            b2 = np.asarray(inputs["L1_ffn_b2"], np.float32)
            ind2 = np.zeros((128, 2), np.float32)
            ind2[0:64, 0] = 1.0
            ind2[64:128, 1] = 1.0
            rotm = np.zeros((128, 128), np.float32)
            for m in range(128):
                if m % 64 < 32:
                    rotm[m + 32, m] = -1.0
                else:
                    rotm[m - 32, m] = 1.0
            tabs = []
            masksl = []
            for b, h in cores:
                g_my = h * TPC + np.arange(TPC)
                g_kv = h * TPC - 256 + np.arange(2304)
                tabs.append({"cosq": _table_tile(cosv * xsc, g_my), "sinq": _table_tile(sinv * xsc, g_my),
                             "cosk": _table_tile(cosv / xsc, g_kv), "sink": _table_tile(sinv / xsc, g_kv)})
                masks = np.zeros((8 * 2 * 128, 512), np.float32)
                for w in range(8):
                    for ih in range(2):
                        iw = ih * 128 + np.arange(128)[:, None]
                        j = np.arange(512)[None, :]
                        okm = (j >= iw) & (j <= iw + 256)
                        if h == 0 and w == 0:
                            okm &= (j >= 256)
                        masks[(w * 2 + ih) * 128:(w * 2 + ih) * 128 + 128] = np.where(okm, 0.0, NEG_HUGE)
                masksl.append(_cc(masks))

            x_cur_ap = xn1f.ap()[:, 3:3 + TPC]
            for li in range(6):
                sclq = np.zeros((2, 128), np.float32)
                sclk = np.zeros((2, 128), np.float32)
                for j in range(2):
                    sclq[j, j * 64:(j + 1) * 64] = q_scale[li]
                    sclk[j, j * 64:(j + 1) * 64] = k_scale[li]
                xcb = mk2(f"xcb{li}", (DM, 257)); agxcb = mk2(f"agxcb{li}", (2 * DM, 257))
                copy_phase(f"cb{li}", xcb.ap(), x_cur_ap[:, TPC - 257:TPC])
                ag_pair(f"agc{li}", xcb, agxcb)
                xqin = mk2(f"xqin{li}", (DM, 2560))
                run_phase2(f"zq{li}", build_zero_fill, {}, {}, dst_ap=xqin.ap()[:, 2305:2560], rows=DM, cols=255)
                copy_phase(f"cq{li}", xqin.ap()[:, 257:257 + TPC], x_cur_ap)
                run_phase2(f"mq{li}", build_masked_halo, {}, {"mhl": mhl},
                           agsrc=agxcb.ap(), dst=xqin.ap()[:, 0:257], rows=DM, cols=257,
                           mask_name="mhl", src_r0=0, src_c0=0)
                q2b = mk2(f"q2b{li}", (DM, 2048), BF16)
                k2b = mk2(f"k2b{li}", (DM, 2304), BF16)
                vtk = mk2(f"vtk{li}", (2304, DM), BF16)
                run_phase2(f"qk{li}", build_f1qkv,
                           {"x": xqin, "q2b": q2b, "k2b": k2b, "v_tok": vtk},
                           {"wqk": _cc(Wqkv[li][:, :2048]), "wv": _cc(Wqkv[li][:, 2048:]),
                            "cosq": [t["cosq"] for t in tabs], "sinq": [t["sinq"] for t in tabs],
                            "cosk": [t["cosk"] for t in tabs], "sink": [t["sink"] for t in tabs],
                            "sclq": _cc(sclq), "sclk": _cc(sclk), "ind2": ind2, "rotm": rotm})
                xmid = mk2(f"xmid{li}", (DM, TPC))
                run_phase2(f"at{li}", build_f1attn,
                           {"q2b": q2b, "k2b": k2b, "v_tok": vtk, "xres": x_cur_ap, "xo": xmid},
                           {"masks": masksl, "wo": _cc(Wo[li])})
                xmb = mk2(f"xmb{li}", (DM, 1)); agxmb = mk2(f"agxmb{li}", (2 * DM, 1))
                copy_phase(f"cm{li}", xmb.ap(), xmid.ap()[:, TPC - 1:TPC])
                ag_pair(f"agm{li}", xmb, agxmb)
                xhin = mk2(f"xhin{li}", (DM, TPC + 1))
                copy_phase(f"ch{li}", xhin.ap()[:, 1:TPC + 1], xmid.ap())
                run_phase2(f"mx{li}", build_masked_halo, {}, {"mhl": mhl},
                           agsrc=agxmb.ap(), dst=xhin.ap()[:, 0:1], rows=DM, cols=1,
                           mask_name="mhl", src_r0=0, src_c0=0)
                hg = mk2(f"hg{li}", (2730, TPC), BF16)
                run_phase2(f"fh{li}", build_f1h, {"x": xhin, "ones128": ones_e, "hgate": hg},
                           {"w1a": _cc(W1[li][:, :2730]), "w1g": _cc(W1[li][:, 2730:]),
                            "b1a": _cc(b1[li][:2730].reshape(-1, 1)), "b1g": _cc(b1[li][2730:].reshape(-1, 1)),
                            "lng": _cc(ln_g[li].reshape(8, 128).T), "lnb": _cc(ln_b[li].reshape(8, 128).T)})
                if li < 5:
                    xnew = mk2(f"xnew{li}", (DM, TPC))
                    run_phase2(f"w2{li}", build_mm,
                               {"rhs": hg, "res0": xmid, "out_f": xnew},
                               {"lhsT": _cc(W2[li]), "bias": _cc(b2[li].reshape(-1, 1))},
                               K=2730, M=1024, T=TPC, rhs_bf16=True, bias=True, nres=1)
                    x_cur_ap = xnew.ap()
                else:
                    run_phase2(f"w2{li}", build_mm,
                               {"rhs": hg, "res0": xmid, "res1": x1, "res2": attn1,
                                "out_f": out_e},
                               {"lhsT": _cc(W2[li]), "bias": _cc(b2[li].reshape(-1, 1))},
                               K=2730, M=1024, T=TPC, rhs_bf16=True, bias=True, nres=3)

        in_maps = []
        for i in range(NC):
            m = {}
            for full, val in stage.items():
                m[full] = val[i] if isinstance(val, list) else val
            in_maps.append(m)
        res = run_bass_kernel_spmd(p.nc, in_maps, core_ids=list(range(NC)))
    out = np.zeros((B, L, DM), np.float32)
    for i in range(NC):
        b, h = i // 2, i % 2
        out[b, h * TPC:(h + 1) * TPC, :] = np.asarray(res.results[i]["out_e"], np.float32).T
    return out


def kernel(**inputs):
    if MEGA:
        try:
            return kernel_mega(inputs)
        except Exception:
            pass
    store = {}
    run_l0(inputs, store)
    run_l1(inputs, store)
    out = np.zeros((B, L, DM), np.float32)
    for i in range(NC):
        b, h = i // 2, i % 2
        out[b, h * TPC:(h + 1) * TPC, :] = np.asarray(store["x2"][i], np.float32).T
    return out
